# version 05
# baseline (speedup 1.0000x reference)
import os
import sys
import numpy as np

sys.path.insert(0, "/opt/trn_rl_repo")
import concourse.bass as bass
import concourse.mybir as mybir
from concourse.bass_utils import run_bass_kernel_spmd

F32 = mybir.dt.float32
BF16 = mybir.dt.bfloat16
I32 = mybir.dt.int32
AF = mybir.ActivationFunctionType
ALU = mybir.AluOpType

DM = 1024
DI = 2048
NS = 16
RK = 64
L = 8192
OWN = 2048
CH = 512
NCH = 20
NPRE = 12
SLAB = 16
EPS = 1e-5
DEBUG = bool(int(os.environ.get("BASSMK_DEBUG", "0")))

ENGINES = ("sync", "tensor", "scalar", "vector", "gpsimd")


class Buf:
    def __init__(self):
        self.w = None
        self.readers = {}

    def wr(self):
        return [(e, v) for e, v in self.readers.items()] + \
            ([self.w] if self.w is not None else [])

    def setw(self, tok):
        self.w = tok
        self.readers = {}

    def rd(self):
        return [self.w] if self.w is not None else []

    def setr(self, tok):
        if tok is not None:
            e, v = tok
            if self.readers.get(e, 0) < v:
                self.readers[e] = v


class Bld:
    def __init__(self):
        self.ops = {e: [] for e in ENGINES}
        self.cnt = {e: 0 for e in ENGINES}
        self.bufs = {}

    def B(self, key):
        if key not in self.bufs:
            self.bufs[key] = Buf()
        return self.bufs[key]

    def emit(self, eng, fn, waits=(), sig=True):
        ws = []
        for t in waits:
            if t is None:
                continue
            pe, v = t
            if pe == eng and pe != "sync":
                continue
            ws.append((pe, v))
        if sig:
            self.cnt[eng] += 16 if eng == "sync" else 1
            tok = (eng, self.cnt[eng])
        else:
            tok = None
        self.ops[eng].append((fn, ws, sig))
        return tok

    def op(self, eng, fn, reads=(), writes=()):
        waits = []
        for k in reads:
            waits += self.B(k).rd()
        for k in writes:
            waits += self.B(k).wr()
        tok = self.emit(eng, fn, waits=waits)
        for k in reads:
            self.B(k).setr(tok)
        for k in writes:
            self.B(k).setw(tok)
        return tok

    def dma(self, out, in_, reads=(), writes=()):
        def fn(e):
            return e.dma_start(out=out, in_=in_)
        return self.op("sync", fn, reads=reads, writes=writes)

    def replay(self, block, sems):
        for eng_name in ENGINES:
            ops = self.ops[eng_name]
            if not ops:
                continue

            def make_body(ename, olist):
                def body(eng):
                    seen = {}
                    cum = 0
                    for fn, ws, sig in olist:
                        for pe, v in ws:
                            if seen.get(pe, 0) >= v:
                                continue
                            seen[pe] = v
                            eng.wait_ge(sems[pe], v)
                        inst = fn(eng)
                        if sig:
                            cum += 16 if ename == "sync" else 1
                            inst.then_inc(sems[ename], 16 if ename == "sync" else 1)
                            if ename == "sync":
                                eng.wait_ge(sems[ename], cum)
                                seen[ename] = cum
                return body
            getattr(block, eng_name)(make_body(eng_name, ops))


def build_program():
    nc = bass.Bass("TRN2", target_bir_lowering=False)

    def din(name, shape, dt=F32):
        return nc.dram_tensor(name, shape, dt, kind="ExternalInput").ap()

    xstream = din("xstream", [NCH * CH, DM])
    win_xT = din("win_xT", [DM, DI])
    win_zT = din("win_zT", [DM, DI])
    wout_T = din("wout_T", [DI, DM])
    conv_st = din("conv_st", [NCH, DI, 4])
    convb_st = din("convb_st", [NCH, 128, SLAB])
    xproj_st = din("xproj_st", [NCH, DI, 96])
    dtw_st = din("dtw_st", [NCH, RK, DI])
    dtb_st = din("dtb_st", [NCH, 128, SLAB])
    alog_st = din("alog_st", [NCH, DI, NS])
    dvec_st = din("dvec_st", [NCH, 128, SLAB])
    pka = din("pka", [128, NCH])
    hkil = din("hkil", [128, NCH])
    hsav = din("hsav", [128, NCH])
    hsavn = din("hsavn", [128, NCH])
    selm = din("selm", [128, NPRE])
    selb = din("selb", [32, SLAB * 128])
    id128b = din("id128b", [128, 128], BF16)
    id128f = din("id128f", [128, 128])
    fc1_q = din("fc1_q", [DM, 4 * DM], I32)
    fc1_sc = din("fc1_sc", [8, 4 * DM])
    fc1_b = din("fc1_b", [128, 32])
    fc2_q = din("fc2_q", [4 * DM, DM], I32)
    fc2_sc = din("fc2_sc", [32, DM])
    fc2_b = din("fc2_b", [128, 8])
    ln_g = din("ln_g", [DM])
    ln_b = din("ln_b", [DM])

    out = nc.dram_tensor("out", [OWN, DM], F32, kind="ExternalOutput").ap()

    def dscr(name, shape, dt=F32):
        return nc.dram_tensor(name, shape, dt, kind="Internal").ap()

    sc_winx = dscr("sc_winx", [128, 8 * DI], BF16)
    sc_winz = dscr("sc_winz", [128, 8 * DI], BF16)
    sc_wout = dscr("sc_wout", [128, SLAB * DM], BF16)
    sc_w1 = dscr("sc_w1", [128, 8, 4 * DM], BF16)
    sc_w2 = dscr("sc_w2", [128, 32, DM], BF16)
    sc_bc = dscr("sc_bc", [32, CH])
    sc_xp = dscr("sc_xp", [NCH, DI, 96], BF16)
    sc_dtw = dscr("sc_dtw", [NCH, RK, DI], BF16)
    sc_op = dscr("sc_op", [OWN, DM])
    sc_x2 = dscr("sc_x2", [OWN, DM])

    dbg = {}
    if DEBUG:
        for nm, shape in [("d_xln", [128, 4 * DM]), ("d_xz", [128, CH]),
                          ("d_xlnT", [128, 8 * CH]), ("d_xr", [128, 520]),
                          ("d_cv", [128, CH]),
                          ("d_u", [128, SLAB * CH]), ("d_delta", [128, SLAB * CH]),
                          ("d_G", [128, NPRE * 256]), ("d_W", [128, NPRE * 256]),
                          ("d_P", [128, NPRE * 256]), ("d_hl", [128, 256]),
                          ("d_a", [128, 512]), ("d_bh", [128, 512]),
                          ("d_u12", [128, 512]), ("d_dl12", [128, 512]),
                          ("d_bct", [32, 512]), ("d_xl2", [128, 4096]),
                          ("d_y1", [128, 512]), ("d_z2", [128, 4096]),
                          ("d_gpre", [128, CH]), ("d_x2", [128, 4 * DM]),
                          ("d_xdbl", [96, CH]), ("d_h", [128, CH])]:
            dbg[nm] = nc.dram_tensor(nm, shape, F32, kind="ExternalOutput").ap()

    ACOLS = 53180
    _off = [0]

    def A(n):
        o = _off[0]
        _off[0] += n
        assert _off[0] <= ACOLS, f"arena overflow at {_off[0]}"
        return o

    o_winx = A(4096)
    o_u = A(8192)
    o_delta = A(8192)
    o_du2 = A(2048)
    o_x = A(2048)
    o_xlnT = A(2048)
    o_xln = A(1024)
    o_sp = A(1536)
    o_dum = o_sp
    o_xr = A(1040)
    O_RHO = o_xr
    o_cv = A(1024)
    O_OPR = o_xr
    O_XRES = o_cv
    o_E = A(1024)
    o_bh = A(512)
    o_h = A(1024)
    o_q = A(512)
    o_sz = A(512)
    o_gp = A(512)
    o_xt = A(1536)
    o_u1 = A(8192)
    o_selb = A(2048)
    o_stA = A(256)
    o_stB = A(256)
    o_i16 = A(256)
    o_i12 = A(256)
    o_parg = A(256)
    o_halo = A(48)
    o_halo2 = A(48)
    o_hsv = A(48)
    o_msk = A(128)
    o_ones = A(512)
    o_acur = A(256)
    o_wc = A(64)
    o_wcb = A(16)
    o_dtb = A(16)
    o_dvec = A(16)
    o_dtwb = A(1024)
    o_xpb = A(768)
    o_bct = A(512)
    o_ct = A(512)
    o_mu = A(64)
    o_idb = A(64)
    o_idf = A(128)
    o_ub = A(512)
    o_dtrb = o_ub
    o_rtot = A(16)
    assert _off[0] <= ACOLS

    b = Bld()

    import contextlib
    es = contextlib.ExitStack()
    AR = es.enter_context(nc.sbuf_tensor("AR", [128, ACOLS], F32))
    PSF = es.enter_context(nc.psum_tensor("PSF", [128, 3584], F32))
    PSB = es.enter_context(nc.psum_tensor("PSB", [128, 1024], BF16))
    sems = {e: es.enter_context(nc.semaphore("sem_" + e)) for e in ENGINES}

    def fr(off, n):
        return AR[:, off:off + n]

    def fb(off, n_bf16):
        return AR[:, off:off + (n_bf16 + 1) // 2].bitcast(BF16)[:, 0:n_bf16]

    def ps(bank, n=512):
        return PSF[:, bank * 512:bank * 512 + n]

    pk = lambda bank: f"ps{bank}"

    pka_c = lambda c: AR[:, o_msk + c:o_msk + c + 1]
    hk_c = lambda c: AR[:, o_msk + 20 + c:o_msk + 21 + c]
    sv_c = lambda c: AR[:, o_msk + 40 + c:o_msk + 41 + c]
    svn_c = lambda c: AR[:, o_msk + 60 + c:o_msk + 61 + c]
    selm_ap = AR[:, o_msk + 80:o_msk + 92]
    A_col = lambda m, n: AR[:, o_acur + m * 16 + n:o_acur + m * 16 + n + 1]

    b.op("vector", lambda e: e.memset(fr(o_ones, 512), 1.0), writes=("ones",))
    b.op("vector", lambda e: e.memset(AR[:, o_mu + 8:o_mu + 9], EPS), writes=("eps",))
    b.op("vector", lambda e: e.memset(fr(o_halo, 48), 0.0), writes=("halo",))
    b.op("vector", lambda e: e.memset(fr(o_hsv, 48), 0.0), writes=("hsv",))
    

    b.dma(fr(o_msk, 20), pka, writes=("mk",))
    b.dma(AR[:, o_msk + 20:o_msk + 40], hkil, writes=("mk",))
    b.dma(AR[:, o_msk + 40:o_msk + 60], hsav, writes=("mk",))
    b.dma(AR[:, o_msk + 60:o_msk + 80], hsavn, writes=("mk",))
    b.dma(AR[:, o_msk + 80:o_msk + 92], selm, writes=("mk",))
    b.dma(AR[0:32, o_selb:o_selb + 2048], selb, writes=("selb",))
    b.dma(fb(o_idb, 128), id128b, writes=("idb",))
    b.dma(fr(o_idf, 128), id128f, writes=("idf",))

    def stage(src3d, ncols, dst, key):
        half = ncols // 2
        K_ = src3d.shape[1]
        for hh in range(2):
            b.dma(fr(o_u, half), src3d[:, hh * (K_ // 2):(hh + 1) * (K_ // 2), :],
                  reads=(), writes=("u_all",))
            def cast(e, hh=hh):
                return e.activation(out=fb(o_delta, half), in_=fr(o_u, half),
                                    func=AF.Copy)
            b.op("scalar", cast, reads=("u_all",), writes=("d_all",))
            b.dma(dst[:, hh * half:(hh + 1) * half], fb(o_delta, half),
                  reads=("d_all",), writes=(key,))

    stage(win_xT.rearrange("(k p) m -> p k m", p=128), 8 * DI, sc_winx, "winx_sc")
    stage(win_zT.rearrange("(k p) m -> p k m", p=128), 8 * DI, sc_winz, "winz_sc")
    stage(wout_T.rearrange("(k p) m -> p k m", p=128), SLAB * DM, sc_wout, "wout_sc")

    def dequant(qd, scd, n_k, W, dst, key):
        for k in range(n_k):
            b.dma(AR[:, o_u:o_u + W].bitcast(I32), qd[k * 128:(k + 1) * 128, :],
                  writes=("u_all",))
            b.dma(fr(o_u + W, W), scd[k, :].partition_broadcast(128),
                  writes=("u_all",))
            def cvt(e, W=W):
                return e.activation(out=fr(o_u + 2 * W, W),
                                    in_=AR[:, o_u:o_u + W].bitcast(I32),
                                    func=AF.Copy, bias=-128.0)
            b.op("scalar", cvt, reads=("u_all",), writes=("u_all",))
            def mul(e, W=W):
                return e.tensor_tensor(out=fb(o_delta, W), in0=fr(o_u + 2 * W, W),
                                       in1=fr(o_u + W, W), op=ALU.mult)
            b.op("vector", mul, reads=("u_all",), writes=("d_all",))
            b.dma(dst[:, k, :], fb(o_delta, W), reads=("d_all",), writes=(key,))

    for cc_ in range(NCH):
        b.dma(fr(o_u, 1536), xproj_st[cc_].rearrange("(s p) r -> p s r", p=128),
              writes=("u_all",))
        def cxp(e):
            return e.activation(out=fb(o_delta, 1536), in_=fr(o_u, 1536), func=AF.Copy)
        b.op("scalar", cxp, reads=("u_all",), writes=("d_all",))
        b.dma(sc_xp[cc_].rearrange("(s p) r -> p s r", p=128), fb(o_delta, 1536),
              reads=("d_all",), writes=("xp_sc",))
        b.dma(fr(o_u, 2048)[0:64, :], dtw_st[cc_], writes=("u_all",))
        def cdt(e):
            return e.activation(out=fb(o_delta, 2048)[0:64, :],
                                in_=fr(o_u, 2048)[0:64, :], func=AF.Copy)
        b.op("scalar", cdt, reads=("u_all",), writes=("d_all",))
        b.dma(sc_dtw[cc_], fb(o_delta, 2048)[0:64, :], reads=("d_all",),
              writes=("dtw_sc",))

    dequant(fc1_q, fc1_sc, 8, 4 * DM, sc_w1, "w1_sc")
    dequant(fc2_q, fc2_sc, 32, DM, sc_w2, "w2_sc")


    st_cur, st_nxt = o_stA, o_stB
    kst_cur, kst_nxt = "stA", "stB"

    for c in range(NCH):
        own = c >= NPRE
        ownfwd = c >= 16

        b.dma(fb(o_dtwb, 2048)[0:64, :], sc_dtw[c], reads=("dtw_sc",), writes=("dtwb",))
        b.dma(fb(o_xpb, 1536), sc_xp[c].rearrange("(s p) r -> p s r", p=128),
              reads=("xp_sc",), writes=("xpb",))
        b.dma(fr(o_wc, 64), conv_st[c].rearrange("(s p) k -> p s k", p=128),
              writes=("wc",))
        b.dma(fr(o_wcb, 16), convb_st[c], writes=("wcb",))
        b.dma(fr(o_dtb, 16), dtb_st[c], writes=("dtb",))
        b.dma(fr(o_dvec, 16), dvec_st[c], writes=("dvec",))
        b.dma(fr(o_parg, 256), alog_st[c].rearrange("(s p) n -> p s n", p=128),
              writes=("parg",))
        def mkA(e):
            return e.activation(out=fr(o_acur, 256), in_=fr(o_parg, 256), func=AF.Exp)
        b.op("scalar", mkA, reads=("parg",), writes=("acur",))
        def mknegA(e):
            return e.tensor_scalar(out=fr(o_acur, 256), in0=fr(o_acur, 256),
                                   scalar1=-1.0, scalar2=None, op0=ALU.mult)
        b.op("vector", mknegA, reads=(), writes=("acur",))

        for tt in range(4):
            kx = f"x{tt % 2}"
            b.dma(fr(o_x + (tt % 2) * DM, DM),
                  xstream[c * CH + tt * 128:c * CH + (tt + 1) * 128, :], writes=(kx,))
            xa = fr(o_x + (tt % 2) * DM, DM)
            def s1(e, xa=xa):
                return e.activation(out=fr(o_dum, DM), in_=xa, func=AF.Copy,
                                    accum_out=AR[:, o_mu:o_mu + 1])
            b.op("scalar", s1, reads=(kx,), writes=("sp", "mu"))
            def s2(e, xa=xa):
                return e.activation(out=fr(o_dum, DM), in_=xa, func=AF.Square,
                                    accum_out=AR[:, o_mu + 1:o_mu + 2])
            b.op("scalar", s2, reads=(kx,), writes=("sp", "mu"))
            def mstat(e):
                return e.tensor_scalar(out=AR[:, o_mu + 2:o_mu + 4],
                                       in0=AR[:, o_mu:o_mu + 2], scalar1=1.0 / DM,
                                       scalar2=None, op0=ALU.mult)
            b.op("vector", mstat, reads=(), writes=("mu",))
            def musq(e):
                return e.activation(out=AR[:, o_mu + 9:o_mu + 10],
                                    in_=AR[:, o_mu + 2:o_mu + 3], func=AF.Square)
            b.op("scalar", musq, reads=("mu",), writes=("mu",))
            def nvar(e):
                return e.tensor_tensor(out=AR[:, o_mu + 4:o_mu + 5],
                                       in0=AR[:, o_mu + 9:o_mu + 10],
                                       in1=AR[:, o_mu + 3:o_mu + 4], op=ALU.subtract)
            b.op("vector", nvar, reads=(), writes=("mu",))
            def lnv(e):
                return e.activation(out=AR[:, o_mu + 5:o_mu + 6],
                                    in_=AR[:, o_mu + 4:o_mu + 5], func=AF.Ln,
                                    scale=-1.0, bias=AR[:, o_mu + 8:o_mu + 9])
            b.op("scalar", lnv, reads=("mu", "eps"), writes=("mu",))
            def rstd(e):
                return e.activation(out=AR[:, o_mu + 6:o_mu + 7],
                                    in_=AR[:, o_mu + 5:o_mu + 6], func=AF.Exp,
                                    scale=-0.5)
            b.op("scalar", rstd, reads=(), writes=("mu",))
            def nb(e):
                return e.tensor_scalar(out=AR[:, o_mu + 7:o_mu + 8],
                                       in0=AR[:, o_mu + 2:o_mu + 3],
                                       scalar1=AR[:, o_mu + 6:o_mu + 7],
                                       scalar2=-1.0, op0=ALU.mult, op1=ALU.mult)
            b.op("vector", nb, reads=(), writes=("mu",))
            kxl = "xl"
            def norm(e, xa=xa, tt=tt):
                return e.tensor_scalar(out=fr(o_xln, DM), in0=xa,
                                       scalar1=AR[:, o_mu + 6:o_mu + 7],
                                       scalar2=AR[:, o_mu + 7:o_mu + 8],
                                       op0=ALU.mult, op1=ALU.add)
            b.op("vector", norm, reads=("mu", kx), writes=(kxl,))
            if DEBUG and c == 0:
                b.dma(dbg["d_xln"][:, tt * DM:(tt + 1) * DM],
                      fr(o_xln, DM), reads=(kxl,))
            for k in range(8):
                bnk = 2 + (k % 2)
                def tr(e, tt=tt, k=k, bnk=bnk):
                    return e.transpose(ps(bnk, 128),
                                       fr(o_xln, DM)[:, k * 128:(k + 1) * 128],
                                       fr(o_idf, 128))
                b.op("tensor", tr, reads=(kxl, "idf"), writes=(pk(bnk),))
                def evac(e, tt=tt, k=k, bnk=bnk):
                    return e.activation(
                        out=fb(o_xlnT, 8 * CH)[:, k * CH + tt * 128:k * CH + (tt + 1) * 128],
                        in_=ps(bnk, 128), func=AF.Copy)
                b.op("scalar", evac, reads=(pk(bnk),), writes=("xlnT",))
        xlnT_k = lambda k: fb(o_xlnT, 8 * CH)[:, k * CH:(k + 1) * CH]
        if DEBUG and c == 0:
            def dxt(e):
                return e.activation(out=fr(o_dum, DM), in_=fb(o_xlnT, 8 * CH)[:, 0:1024],
                                    func=AF.Copy)
            b.op("scalar", dxt, reads=("xlnT",), writes=("sp",))
            b.dma(dbg["d_xlnT"][:, 0:1024], fr(o_dum, DM), reads=("sp",))
            def dxt2(e):
                return e.activation(out=fr(o_dum, DM), in_=fb(o_xlnT, 8 * CH)[:, 2048:3072],
                                    func=AF.Copy)
            b.op("scalar", dxt2, reads=("xlnT",), writes=("sp",))
            b.dma(dbg["d_xlnT"][:, 2048:3072], fr(o_dum, DM), reads=("sp",))

        def sav1(e, c=c):
            return e.tensor_scalar(out=fr(o_hsv, 48), in0=fr(o_hsv, 48),
                                   scalar1=svn_c(c), scalar2=None, op0=ALU.mult)
        b.op("vector", sav1, reads=("mk",), writes=("hsv",))
        def sav2(e, c=c):
            return e.scalar_tensor_tensor(out=fr(o_hsv, 48), in0=fr(o_halo, 48),
                                          scalar=sv_c(c), in1=fr(o_hsv, 48),
                                          op0=ALU.mult, op1=ALU.add)
        b.op("vector", sav2, reads=("halo", "mk"), writes=("hsv",))
        def hk2(e, c=c):
            return e.tensor_scalar(out=fr(o_halo2, 48), in0=fr(o_halo, 48),
                                   scalar1=hk_c(c), scalar2=None, op0=ALU.mult)
        b.op("vector", hk2, reads=("halo", "mk"), writes=("halo2",))
        if DEBUG and c == 0:
            b.dma(dbg["d_hl"][:, 0:48], fr(o_halo, 48), reads=("halo",))
            b.dma(dbg["d_hl"][:, 48:96], fr(o_halo2, 48), reads=("halo2",))
            b.dma(dbg["d_hl"][:, 96:144], fr(o_hsv, 48), reads=("hsv",))
            b.dma(dbg["d_hl"][:, 144:192], fr(o_msk, 48), reads=("mk",))

        for hf in range(2):
            b.dma(fb(o_winx, 8192).rearrange("p (k m) -> p k m", k=8),
                  sc_winx.rearrange("p (k m) -> p k m", k=8)[:, :, hf * DM:(hf + 1) * DM],
                  reads=("winx_sc",), writes=("winx",))
            for mm in range(8):
                m = hf * 8 + mm
                pb = 2 + (m % 2)
                for k in range(8):
                    def mm_(e, k=k, mm=mm, pb=pb):
                        wt = fb(o_winx, 8192)[:, k * DM + mm * 128:k * DM + (mm + 1) * 128]
                        return e.matmul(ps(pb), wt, xlnT_k(k),
                                        start=(k == 0), stop=(k == 7))
                    b.op("tensor", mm_, reads=("winx", "xlnT"), writes=(pk(pb),))
                oxr = o_xr + (m % 2) * 520
                kxr = f"xr{m % 2}"
                def xev(e, pb=pb, oxr=oxr):
                    return e.copy(out=AR[:, oxr + 3:oxr + 515], in_=ps(pb))
                b.op("scalar", xev, reads=(pk(pb),), writes=(kxr,))
                if DEBUG and c == 0 and m == 0:
                    b.dma(dbg["d_xz"], AR[:, oxr + 3:oxr + 515], reads=(kxr,))
                    b.dma(dbg["d_xr"], AR[:, oxr:oxr + 520], reads=(kxr,))
                def tail(e, m=m, oxr=oxr):
                    return e.tensor_copy(out=fr(o_halo, 48)[:, m * 3:m * 3 + 3],
                                         in_=AR[:, oxr + 512:oxr + 515])
                b.op("vector", tail, reads=(kxr,), writes=("halo",))
                ocv0 = o_cv
                ocv1 = o_cv + 512
                o_hsrc = o_hsv if c == 16 else o_halo2
                k_hsrc = "hsv" if c == 16 else "halo2"
                wcol = lambda j, m=m: AR[:, o_wc + m * 4 + j:o_wc + m * 4 + j + 1]
                def c0h(e, m=m, wcol=wcol, o_hsrc=o_hsrc):
                    return e.tensor_scalar(out=fr(ocv0, 3),
                                           in0=AR[:, o_hsrc + m * 3:o_hsrc + m * 3 + 3],
                                           scalar1=wcol(0),
                                           scalar2=AR[:, o_wcb + m:o_wcb + m + 1],
                                           op0=ALU.mult, op1=ALU.add)
                b.op("vector", c0h, reads=(k_hsrc, "wc", "wcb"), writes=("cv0",))
                def c0(e, oxr=oxr, m=m, wcol=wcol):
                    return e.tensor_scalar(out=AR[:, ocv0 + 3:ocv0 + 512],
                                           in0=AR[:, oxr + 3:oxr + 512],
                                           scalar1=wcol(0),
                                           scalar2=AR[:, o_wcb + m:o_wcb + m + 1],
                                           op0=ALU.mult, op1=ALU.add)
                b.op("vector", c0, reads=(kxr, "wc", "wcb"), writes=("cv0",))
                def c1h(e, m=m, wcol=wcol, o_hsrc=o_hsrc):
                    return e.scalar_tensor_tensor(
                        out=fr(ocv1, 2),
                        in0=AR[:, o_hsrc + m * 3 + 1:o_hsrc + m * 3 + 3],
                        scalar=wcol(1), in1=fr(ocv0, 2),
                        op0=ALU.mult, op1=ALU.add)
                b.op("vector", c1h, reads=(k_hsrc, "wc", "cv0"), writes=("cv1",))
                def c1(e, oxr=oxr, m=m, wcol=wcol):
                    return e.scalar_tensor_tensor(out=AR[:, ocv1 + 2:ocv1 + 512],
                                                  in0=AR[:, oxr + 3:oxr + 513],
                                                  scalar=wcol(1),
                                                  in1=AR[:, ocv0 + 2:ocv0 + 512],
                                                  op0=ALU.mult, op1=ALU.add)
                b.op("vector", c1, reads=(kxr, "wc", "cv0"), writes=("cv1",))
                def c2h(e, m=m, wcol=wcol, o_hsrc=o_hsrc):
                    return e.scalar_tensor_tensor(
                        out=fr(ocv0, 1),
                        in0=AR[:, o_hsrc + m * 3 + 2:o_hsrc + m * 3 + 3],
                        scalar=wcol(2), in1=fr(ocv1, 1),
                        op0=ALU.mult, op1=ALU.add)
                b.op("vector", c2h, reads=(k_hsrc, "wc", "cv1"), writes=("cv0",))
                def c2(e, oxr=oxr, m=m, wcol=wcol):
                    return e.scalar_tensor_tensor(out=AR[:, ocv0 + 1:ocv0 + 512],
                                                  in0=AR[:, oxr + 3:oxr + 514],
                                                  scalar=wcol(2),
                                                  in1=AR[:, ocv1 + 1:ocv1 + 512],
                                                  op0=ALU.mult, op1=ALU.add)
                b.op("vector", c2, reads=(kxr, "wc", "cv1"), writes=("cv0",))
                def c3(e, oxr=oxr, m=m, wcol=wcol):
                    return e.scalar_tensor_tensor(out=fr(ocv1, CH),
                                                  in0=AR[:, oxr + 3:oxr + 515],
                                                  scalar=wcol(3), in1=fr(ocv0, CH),
                                                  op0=ALU.mult, op1=ALU.add)
                b.op("vector", c3, reads=(kxr, "wc", "cv0"), writes=("cv1",))
                if DEBUG and c == 0 and m == 0:
                    b.dma(dbg["d_cv"], fr(ocv1, CH), reads=("cv1",))
                    b.dma(dbg["d_hl"][:, 196:200], AR[:, oxr:oxr + 4], reads=(kxr,))
                def silu(e, m=m):
                    return e.activation(out=fr(o_u + m * CH, CH), in_=fr(ocv1, CH),
                                        func=AF.Silu)
                b.op("scalar", silu, reads=("cv1",), writes=(f"u{m}",) + (("u_all",) if c == 0 else ()))
                def silub(e, m=m):
                    return e.activation(out=fb(o_ub + (m % 2) * 256, CH),
                                        in_=fr(ocv1, CH), func=AF.Silu)
                b.op("scalar", silub, reads=("cv1",), writes=(f"ub{m % 2}",))
                def xp(e, m=m):
                    return e.matmul(ps(6)[0:96, :],
                                    fb(o_xpb, 1536)[:, m * 96:(m + 1) * 96],
                                    fb(o_ub + (m % 2) * 256, CH),
                                    start=(m == 0), stop=(m == 15))
                b.op("tensor", xp, reads=("xpb", f"ub{m % 2}"), writes=(pk(6),))

        def dtr(e):
            return e.activation(out=fb(o_dtrb, CH)[0:64, :], in_=ps(6)[0:64, :],
                                func=AF.Copy)
        b.op("scalar", dtr, reads=(pk(6),), writes=("dtrb",))
        def bct(e):
            return e.copy(out=fr(o_bct, CH)[0:32, :], in_=ps(6)[64:96, :])
        b.op("scalar", bct, reads=(pk(6),), writes=("bct",))
        if DEBUG and c == 0:
            def dxd1(e):
                return e.copy(out=fr(o_sp, CH)[0:64, :], in_=ps(6)[0:64, :])
            b.op("scalar", dxd1, reads=(pk(6),), writes=("sp",))
            def dxd2(e):
                return e.copy(out=fr(o_sp + 512, CH)[0:32, :], in_=ps(6)[64:96, :])
            b.op("scalar", dxd2, reads=(pk(6),), writes=("sp",))
            b.dma(dbg["d_xdbl"][0:64, :], fr(o_sp, CH)[0:64, :], reads=("sp",))
            b.dma(dbg["d_xdbl"][64:96, :], fr(o_sp + 512, CH)[0:32, :], reads=("sp",))
        if own:
            b.dma(sc_bc, fr(o_bct, CH)[0:32, :], reads=("bct",), writes=("scbc",))
            b.dma(fr(o_ct, 512)[0:16, :], sc_bc[16:32, :], reads=("scbc",),
                  writes=("ct",))

        for m in range(SLAB):
            pb = 4 + (m % 2)
            def dt_(e, m=m, pb=pb):
                return e.matmul(ps(pb), fb(o_dtwb, 2048)[0:64, m * 128:(m + 1) * 128],
                                fb(o_dtrb, CH)[0:64, :], start=True, stop=True)
            b.op("tensor", dt_, reads=("dtwb", "dtrb"), writes=(pk(pb),))
            def vb(e, m=m, pb=pb):
                return e.tensor_scalar(out=fr(o_sp, CH), in0=ps(pb),
                                       scalar1=AR[:, o_dtb + m:o_dtb + m + 1],
                                       scalar2=None, op0=ALU.add)
            b.op("vector", vb, reads=("dtb", pk(pb)), writes=("sp",))
            def ab(e):
                return e.scalar_tensor_tensor(out=fr(o_sp + 512, CH), in0=fr(o_sp, CH),
                                              scalar=-1.0, in1=fr(o_sp, CH),
                                              op0=ALU.mult, op1=ALU.max)
            b.op("vector", ab, reads=(), writes=("sp",))
            def e1(e):
                return e.activation(out=fr(o_sp + 1024, CH), in_=fr(o_sp + 512, CH),
                                    func=AF.Exp, scale=-1.0)
            b.op("scalar", e1, reads=("sp",), writes=("sp",))
            def l1(e):
                return e.activation(out=fr(o_sp + 512, CH), in_=fr(o_sp + 1024, CH),
                                    func=AF.Ln, bias=1.0)
            b.op("scalar", l1, reads=(), writes=("sp",))
            def rl(e):
                return e.tensor_scalar(out=fr(o_sp + 1024, CH), in0=fr(o_sp, CH),
                                       scalar1=0.0, scalar2=None, op0=ALU.max)
            b.op("vector", rl, reads=(), writes=("sp",))
            def dl(e, m=m):
                return e.tensor_tensor(out=fr(o_delta + m * CH, CH),
                                       in0=fr(o_sp + 512, CH), in1=fr(o_sp + 1024, CH),
                                       op=ALU.add)
            b.op("vector", dl, reads=("sp",), writes=(f"dl{m}",) + (("d_all",) if c == 0 else ()))

        if DEBUG and c == 0:
            b.dma(dbg["d_u"], fr(o_u, SLAB * CH),
                  reads=tuple(f"u{m}" for m in range(SLAB)))
            b.dma(dbg["d_delta"], fr(o_delta, SLAB * CH),
                  reads=tuple(f"dl{m}" for m in range(SLAB)))

        G_ap = lambda cc, lane: AR[:, o_u1 + cc * 256 + lane:o_u1 + cc * 256 + lane + 1]
        P_off = o_u1 + 3072
        W_off = o_u1

        if not own:
            b.op("vector", lambda e: e.memset(AR[:, O_RHO + 512:O_RHO + 513], 0.0),
                 writes=("xr0",))
            for m in range(SLAB):
                def rho_(e, m=m):
                    return e.tensor_tensor_scan(
                        out=AR[:, O_RHO:O_RHO + 512][:, ::-1],
                        data0=fr(o_ones, CH),
                        data1=fr(o_delta + m * CH, CH)[:, ::-1],
                        initial=0.0, op0=ALU.mult, op1=ALU.add)
                b.op("vector", rho_, reads=(f"dl{m}", "ones"), writes=("xr0",))
                def du_(e, m=m):
                    return e.tensor_tensor(out=fr(o_u + m * CH, CH),
                                           in0=fr(o_delta + m * CH, CH),
                                           in1=fr(o_u + m * CH, CH), op=ALU.mult)
                b.op("vector", du_, reads=(f"dl{m}",), writes=(f"u{m}",))
                def rcp(e, m=m):
                    return e.tensor_copy(out=fr(o_delta + m * CH, CH),
                                         in_=AR[:, O_RHO + 1:O_RHO + 513])
                b.op("vector", rcp, reads=("xr0",), writes=(f"dl{m}",))
                def rtc(e, m=m):
                    return e.tensor_copy(out=AR[:, o_rtot + m:o_rtot + m + 1],
                                         in_=AR[:, O_RHO:O_RHO + 1])
                b.op("vector", rtc, reads=("xr0",), writes=("rtot",))
            for m in range(SLAB):
                def parg_(e, m=m):
                    return e.tensor_scalar(out=fr(o_parg, 256)[:, m * 16:(m + 1) * 16],
                                           in0=fr(o_acur, 256)[:, m * 16:(m + 1) * 16],
                                           scalar1=AR[:, o_rtot + m:o_rtot + m + 1],
                                           scalar2=None, op0=ALU.mult)
                b.op("vector", parg_, reads=("acur", "rtot"), writes=("parg",))
            for n in range(NS):
                bb = n % 2
                def brep(e, n=n, bb=bb):
                    return e.matmul(ps(bb), AR[0:32, o_selb + n * 128:o_selb + (n + 1) * 128],
                                    fr(o_bct, CH)[0:32, :], start=True, stop=True)
                b.op("tensor", brep, reads=("selb", "bct"), writes=(pk(bb),))
                for m in range(SLAB):
                    oE = o_E + (m % 2) * 512
                    kE = f"E{m % 2}"
                    def E_(e, m=m, n=n, oE=oE):
                        return e.activation(out=fr(oE, CH),
                                            in_=fr(o_delta + m * CH, CH),
                                            func=AF.Exp, scale=A_col(m, n))
                    b.op("scalar", E_, reads=(f"dl{m}", "acur"), writes=(kE,))
                    def bh(e, m=m, bb=bb):
                        return e.tensor_tensor(out=fr(o_bh, CH),
                                               in0=fr(o_u + m * CH, CH),
                                               in1=ps(bb), op=ALU.mult)
                    b.op("vector", bh, reads=(f"u{m}", pk(bb)), writes=("bh",))
                    lane = m * 16 + n
                    def G_(e, c=c, lane=lane, oE=oE):
                        return e.scalar_tensor_tensor(
                            out=fr(o_sp, CH), in0=fr(oE, CH), scalar=1.0,
                            in1=fr(o_bh, CH), op0=ALU.mult, op1=ALU.mult,
                            accum_out=G_ap(c, lane))
                    b.op("vector", G_, reads=(kE, "bh"), writes=("Gb", "sp"))
            def pkl(e, c=c):
                return e.tensor_scalar(out=fr(o_parg, 256), in0=fr(o_parg, 256),
                                       scalar1=pka_c(c), scalar2=None, op0=ALU.add)
            b.op("vector", pkl, reads=("mk",), writes=("parg",))
            def P_(e, c=c):
                return e.activation(out=AR[:, P_off + c * 256:P_off + (c + 1) * 256],
                                    in_=fr(o_parg, 256), func=AF.Exp)
            b.op("scalar", P_, reads=("parg",), writes=("Pb",))

        if c == NPRE - 1:
            if DEBUG:
                b.dma(dbg["d_G"], fr(o_u1, 3072), reads=("Gb",))
                b.dma(dbg["d_P"], AR[:, P_off:P_off + 3072], reads=("Pb",))
            for lane in range(256):
                def ws(e, lane=lane):
                    return e.tensor_tensor_scan(
                        out=AR[:, W_off + lane:W_off + NPRE * 256:256],
                        data0=AR[:, P_off + lane:P_off + NPRE * 256:256],
                        data1=AR[:, o_u1 + lane:o_u1 + NPRE * 256:256],
                        initial=0.0, op0=ALU.mult, op1=ALU.add)
                b.op("vector", ws, reads=("Pb",), writes=("Wb", "Gb"))
            def i12(e):
                return e.tensor_copy(out=fr(o_i12, 256),
                                     in_=AR[:, W_off + 11 * 256:W_off + 12 * 256])
            b.op("vector", i12, reads=("Wb",), writes=("i12",))
            for lane in range(256):
                def i16(e, lane=lane):
                    return e.scalar_tensor_tensor(
                        out=fr(o_sp, NPRE),
                        in0=AR[:, W_off + lane:W_off + NPRE * 256:256],
                        scalar=1.0, in1=selm_ap, op0=ALU.mult, op1=ALU.mult,
                        accum_out=AR[:, o_i16 + lane:o_i16 + lane + 1])
                b.op("vector", i16, reads=("Wb", "mk"), writes=("i16", "sp"))
            if DEBUG:
                b.dma(dbg["d_W"], AR[:, W_off:W_off + 3072], reads=("Wb",))

        if own:
            if c == NPRE:
                b.dma(fb(o_u1, 8 * DI), sc_winz, reads=("winz_sc", "Gb", "Pb"),
                      writes=("winzS",))
            winz_t = lambda k, m: fb(o_u1, 8 * DI)[:, k * DI + m * 128:k * DI + (m + 1) * 128]
            wout_s = lambda m, half: fb(o_u, SLAB * DM)[:, m * DM + half * 512:m * DM + (half + 1) * 512]

            for g in range(4):
                for mm in range(4):
                    m = g * 4 + mm
                    def dug(e, m=m, mm=mm):
                        return e.tensor_tensor(out=fr(o_du2 + mm * CH, CH),
                                               in0=fr(o_delta + m * CH, CH),
                                               in1=fr(o_u + m * CH, CH), op=ALU.mult)
                    b.op("vector", dug, reads=(f"dl{m}", f"u{m}"), writes=(f"dug{mm}",))
                for n in range(NS):
                    bb = n % 2
                    cb = 6
                    def brep(e, n=n, bb=bb):
                        return e.matmul(ps(bb), AR[0:32, o_selb + n * 128:o_selb + (n + 1) * 128],
                                        fr(o_bct, CH)[0:32, :], start=True, stop=True)
                    b.op("tensor", brep, reads=("selb", "bct"), writes=(pk(bb),))
                    def crep(e, n=n, cb=cb):
                        return e.matmul(ps(cb),
                                        AR[0:32, o_selb + n * 128:o_selb + (n + 1) * 128][0:16, :],
                                        fr(o_ct, 512)[0:16, :], start=True, stop=True)
                    b.op("tensor", crep, reads=("selb", "ct"), writes=(pk(cb),))
                    if DEBUG and c == NPRE and g == 0 and n == 0:
                        def dbrep(e, bb=bb):
                            return e.tensor_copy(out=fr(o_sz, CH), in_=ps(bb))
                        b.op("vector", dbrep, reads=(pk(bb),), writes=("sz",))
                        b.dma(dbg["d_hl"][:, 0:128], fr(o_sz, CH)[:, 0:128],
                              reads=("sz",))
                        b.dma(dbg["d_bct"], fr(o_bct, CH)[0:32, :], reads=("bct",))
                    for mm in range(4):
                        m = g * 4 + mm
                        lane = m * 16 + n
                        oE = o_E + (mm % 2) * 512
                        kE = f"E{mm % 2}"
                        def a_(e, m=m, n=n, oE=oE):
                            return e.activation(out=fr(oE, CH),
                                                in_=fr(o_delta + m * CH, CH),
                                                func=AF.Exp, scale=A_col(m, n))
                        b.op("scalar", a_, reads=(f"dl{m}", "acur"), writes=(kE,))
                        def bh(e, mm=mm, bb=bb):
                            return e.tensor_tensor(out=fr(o_bh, CH),
                                                   in0=fr(o_du2 + mm * CH, CH),
                                                   in1=ps(bb), op=ALU.mult)
                        b.op("vector", bh, reads=(f"dug{mm}", pk(bb)), writes=("bh",))
                        if DEBUG and c == NPRE and n == 0 and m == 0:
                            b.dma(dbg["d_a"], fr(oE, CH), reads=(kE,))
                            b.dma(dbg["d_bh"], fr(o_bh, CH), reads=("bh",))
                            b.dma(dbg["d_u12"], fr(o_u, CH), reads=("u0",))
                            b.dma(dbg["d_dl12"], fr(o_delta, CH), reads=("dl0",))
                        if c == NPRE:
                            init_ap = AR[:, o_i12 + lane:o_i12 + lane + 1]
                            ikey = "i12"
                        elif c == 16:
                            init_ap = AR[:, o_i16 + lane:o_i16 + lane + 1]
                            ikey = "i16"
                        else:
                            init_ap = AR[:, st_cur + lane:st_cur + lane + 1]
                            ikey = kst_cur
                        oh = o_h + (mm % 2) * 512
                        kh = f"h{mm % 2}"
                        def scan(e, oE=oE, oh=oh, init_ap=init_ap):
                            return e.tensor_tensor_scan(
                                out=fr(oh, CH), data0=fr(oE, CH), data1=fr(o_bh, CH),
                                initial=init_ap, op0=ALU.mult, op1=ALU.add)
                        b.op("vector", scan, reads=(kE, "bh", ikey), writes=(kh,))
                        def stx(e, oh=oh, lane=lane, st_nxt=st_nxt):
                            return e.tensor_copy(out=AR[:, st_nxt + lane:st_nxt + lane + 1],
                                                 in_=AR[:, oh + 511:oh + 512])
                        b.op("vector", stx, reads=(kh,), writes=(kst_nxt,))
                        def q_(e, oh=oh, cb=cb, mm=mm):
                            return e.tensor_tensor(out=fb(o_q + (mm % 2) * 256, CH),
                                                   in0=fr(oh, CH), in1=ps(cb),
                                                   op=ALU.mult)
                        b.op("vector", q_, reads=(kh, pk(cb)), writes=(f"q{mm % 2}",))
                        if DEBUG and c == NPRE and n == 0 and m == 0:
                            b.dma(dbg["d_h"], fr(oh, CH), reads=(kh,))
                        yb = 2 + mm
                        def nsum(e, mm=mm, yb=yb, n=n):
                            return e.matmul(ps(yb), fb(o_idb, 128),
                                            fb(o_q + (mm % 2) * 256, CH),
                                            start=(n == 0), stop=(n == NS - 1))
                        b.op("tensor", nsum, reads=(f"q{mm % 2}", "idb"), writes=(pk(yb),))
                for mm in range(4):
                    m = g * 4 + mm
                    yb = 2 + mm
                    def gpre(e, m=m, yb=yb):
                        return e.scalar_tensor_tensor(
                            out=fr(o_gp, CH), in0=fr(o_u + m * CH, CH),
                            scalar=AR[:, o_dvec + m:o_dvec + m + 1],
                            in1=ps(yb), op0=ALU.mult, op1=ALU.add)
                    b.op("vector", gpre, reads=(f"u{m}", "dvec", pk(yb)), writes=("gp",))
                    if DEBUG and c == NPRE and m == 0:
                        b.dma(dbg["d_gpre"], fr(o_gp, CH), reads=("gp",))
                    for k in range(8):
                        def z_(e, k=k, m=m, yb=yb):
                            return e.matmul(ps(yb), winz_t(k, m), xlnT_k(k),
                                            start=(k == 0), stop=(k == 7))
                        b.op("tensor", z_, reads=("winzS", "xlnT", "gp"), writes=(pk(yb),))
                    def sz(e, yb=yb):
                        return e.activation(out=fr(o_sz, CH), in_=ps(yb), func=AF.Silu)
                    b.op("scalar", sz, reads=(pk(yb),), writes=("sz",))
                    def gate(e, m=m, ownfwd=ownfwd):
                        dst = fb(o_delta + m * CH, CH)
                        if not ownfwd:
                            dst = dst[:, ::-1]
                        return e.tensor_tensor(out=dst, in0=fr(o_gp, CH),
                                               in1=fr(o_sz, CH), op=ALU.mult)
                    b.op("vector", gate, reads=("gp", "sz"), writes=(f"dl{m}",))
                b.dma(fb(o_u, SLAB * DM)[:, g * 4096:(g + 1) * 4096],
                      sc_wout[:, g * 4096:(g + 1) * 4096],
                      reads=("wout_sc",),
                      writes=tuple(f"u{g * 4 + j}" for j in range(4)))
            st_cur, st_nxt = st_nxt, st_cur
            kst_cur, kst_nxt = kst_nxt, kst_cur

            for t_ in range(4):
                pb0 = 2 + 2 * (t_ % 2)
                for half in range(2):
                    pb = pb0 + half
                    for m in range(SLAB):
                        def op_(e, m=m, t_=t_, half=half, pb=pb):
                            gt = fb(o_delta + m * CH, CH)[:, t_ * 128:(t_ + 1) * 128]
                            return e.matmul(ps(pb), gt, wout_s(m, half),
                                            start=(m == 0), stop=(m == SLAB - 1))
                        b.op("tensor", op_, reads=(f"dl{m}", f"u{m}"), writes=(pk(pb),))
                oxt = o_xt
                kxt = "xt"
                def oev(e, pb0=pb0, oxt=oxt):
                    return e.copy(out=fr(oxt, DM), in_=ps(pb0, 1024))
                b.op("scalar", oev, reads=(pk(pb0), pk(pb0 + 1)), writes=("xt",))
                if not ownfwd:
                    row0 = 7680 - 512 * c + 128 * t_
                    b.dma(sc_op[row0:row0 + 128, :], fr(oxt, DM),
                          reads=(kxt,), writes=("scop",))
                else:
                    tokrow = (c - 16) * CH + t_ * 128
                    b.dma(fr(O_OPR, DM), sc_op[tokrow:tokrow + 128, :],
                          reads=("scop",), writes=("xr0", "xr1"))
                    b.dma(fr(O_XRES, DM), xstream[L + tokrow:L + tokrow + 128, :],
                          writes=("cv0", "cv1"))
                    def add1(e, oxt=oxt):
                        return e.tensor_tensor(out=fr(O_OPR, DM),
                                               in0=fr(oxt, DM), in1=fr(O_OPR, DM),
                                               op=ALU.add)
                    b.op("vector", add1, reads=(kxt,), writes=("xr0", "xr1"))
                    def add2(e, oxt=oxt):
                        return e.tensor_tensor(out=fr(oxt, DM), in0=fr(O_OPR, DM),
                                               in1=fr(O_XRES, DM), op=ALU.add)
                    b.op("vector", add2, reads=("xr0", "xr1", "cv0", "cv1"), writes=("xt",))
                    b.dma(sc_x2[tokrow:tokrow + 128, :], fr(oxt, DM),
                          reads=(kxt,), writes=("x2w",))

    of_x2 = o_u
    of_xl2 = o_u + 4096
    of_h2T = o_delta
    of_w1b = o_delta + 2048
    of_w2b = o_delta + 3072
    of_y1 = o_u1
    of_z2 = o_winx
    of_gb = o_x
    of_fb = o_xlnT

    fence_keys = (tuple(f"u{m}" for m in range(SLAB))
                  + tuple(f"dl{m}" for m in range(SLAB))
                  + ("winzS", "xlnT", "x0", "x1", "fence"))
    b.op("vector", lambda e: e.memset(fr(o_mu, 4), 0.0), writes=fence_keys)
    b.dma(fr(of_gb, DM), ln_g.partition_broadcast(128), reads=("fence",), writes=("x0", "x1"))
    b.dma(fr(of_gb + DM, DM), ln_b.partition_broadcast(128), writes=("x0", "x1"))
    b.dma(fr(of_fb, 32), fc1_b, writes=("xlnT",))
    b.dma(fr(of_fb + 32, 8), fc2_b, writes=("xlnT",))
    def hb2(e):
        return e.tensor_scalar(out=fr(of_fb + 40, 8), in0=fr(of_fb + 32, 8),
                               scalar1=0.5, scalar2=None, op0=ALU.mult)
    b.op("vector", hb2, reads=("xlnT",), writes=("xlnT",))

    for fch in range(4):
        b.dma(fr(of_x2, 4 * DM),
              sc_x2[fch * CH:(fch + 1) * CH, :].rearrange("(t p) d -> p t d", p=128),
              reads=("fence", "x2w"), writes=("fx2",))
        for tt in range(4):
            def stats(src_key, src_ap):
                def s1(e):
                    return e.activation(out=fr(o_dum, DM), in_=src_ap, func=AF.Copy,
                                        accum_out=AR[:, o_mu:o_mu + 1])
                b.op("scalar", s1, reads=(src_key,), writes=("sp", "mu"))
                def s2(e):
                    return e.activation(out=fr(o_dum, DM), in_=src_ap, func=AF.Square,
                                        accum_out=AR[:, o_mu + 1:o_mu + 2])
                b.op("scalar", s2, reads=(src_key,), writes=("sp", "mu"))
                def mstat(e):
                    return e.tensor_scalar(out=AR[:, o_mu + 2:o_mu + 4],
                                           in0=AR[:, o_mu:o_mu + 2],
                                           scalar1=1.0 / DM, scalar2=None, op0=ALU.mult)
                b.op("vector", mstat, reads=(), writes=("mu",))
                def musq(e):
                    return e.activation(out=AR[:, o_mu + 9:o_mu + 10],
                                        in_=AR[:, o_mu + 2:o_mu + 3], func=AF.Square)
                b.op("scalar", musq, reads=("mu",), writes=("mu",))
                def nvar(e):
                    return e.tensor_tensor(out=AR[:, o_mu + 4:o_mu + 5],
                                           in0=AR[:, o_mu + 9:o_mu + 10],
                                           in1=AR[:, o_mu + 3:o_mu + 4], op=ALU.subtract)
                b.op("vector", nvar, reads=(), writes=("mu",))
                def lnv(e):
                    return e.activation(out=AR[:, o_mu + 5:o_mu + 6],
                                        in_=AR[:, o_mu + 4:o_mu + 5], func=AF.Ln,
                                        scale=-1.0, bias=AR[:, o_mu + 8:o_mu + 9])
                b.op("scalar", lnv, reads=("mu", "eps"), writes=("mu",))
                def rstd(e):
                    return e.activation(out=AR[:, o_mu + 6:o_mu + 7],
                                        in_=AR[:, o_mu + 5:o_mu + 6], func=AF.Exp,
                                        scale=-0.5)
                b.op("scalar", rstd, reads=(), writes=("mu",))
                def nb(e):
                    return e.tensor_scalar(out=AR[:, o_mu + 7:o_mu + 8],
                                           in0=AR[:, o_mu + 2:o_mu + 3],
                                           scalar1=AR[:, o_mu + 6:o_mu + 7],
                                           scalar2=-1.0, op0=ALU.mult, op1=ALU.mult)
                b.op("vector", nb, reads=(), writes=("mu",))
            stats("fx2", fr(of_x2 + tt * DM, DM))
            def n1(e, tt=tt):
                return e.tensor_scalar(out=fr(of_xl2 + tt * DM, DM),
                                       in0=fr(of_x2 + tt * DM, DM),
                                       scalar1=AR[:, o_mu + 6:o_mu + 7],
                                       scalar2=AR[:, o_mu + 7:o_mu + 8],
                                       op0=ALU.mult, op1=ALU.add)
            b.op("vector", n1, reads=("fx2", "mu"), writes=("fxl",))
            stats("fxl", fr(of_xl2 + tt * DM, DM))
            def n2(e, tt=tt):
                return e.tensor_scalar(out=fr(o_dum, DM), in0=fr(of_xl2 + tt * DM, DM),
                                       scalar1=AR[:, o_mu + 6:o_mu + 7],
                                       scalar2=AR[:, o_mu + 7:o_mu + 8],
                                       op0=ALU.mult, op1=ALU.add)
            b.op("vector", n2, reads=("fxl", "mu"), writes=("sp",))
            def gmul(e):
                return e.tensor_tensor(out=fr(o_xt, DM), in0=fr(o_dum, DM),
                                       in1=fr(of_gb, DM), op=ALU.mult)
            b.op("vector", gmul, reads=("sp", "x0"), writes=("xt",))
            def badd(e):
                return e.tensor_tensor(out=fr(o_dum, DM), in0=fr(o_xt, DM),
                                       in1=fr(of_gb + DM, DM), op=ALU.add)
            b.op("vector", badd, reads=("xt", "x1"), writes=("sp",))
            for k in range(8):
                bnk = 2 + (k % 2)
                def tr(e, k=k, bnk=bnk):
                    return e.transpose(ps(bnk, 128),
                                       fr(o_dum, DM)[:, k * 128:(k + 1) * 128],
                                       fr(o_idf, 128))
                b.op("tensor", tr, reads=("sp", "idf"), writes=(pk(bnk),))
                def ev(e, tt=tt, k=k, bnk=bnk):
                    return e.activation(
                        out=fb(of_h2T, 8 * CH)[:, k * CH + tt * 128:k * CH + (tt + 1) * 128],
                        in_=ps(bnk, 128), func=AF.Copy)
                b.op("scalar", ev, reads=(pk(bnk),), writes=("h2T",))
        h2T_k = lambda k: fb(of_h2T, 8 * CH)[:, k * CH:(k + 1) * CH]

        for Ft in range(32):
            kw = f"w1b{Ft % 2}"
            b.dma(fb(of_w1b + (Ft % 2) * 512, 8 * 128).rearrange("p (k m) -> p k m", k=8),
                  sc_w1[:, :, Ft * 128:(Ft + 1) * 128],
                  reads=("w1_sc", "fence"), writes=(kw,))
            pb = 2 + (Ft % 2)
            for k in range(8):
                def mm_(e, Ft=Ft, k=k, pb=pb):
                    wt = fb(of_w1b + (Ft % 2) * 512, 8 * 128)[:, k * 128:(k + 1) * 128]
                    return e.matmul(ps(pb), wt, h2T_k(k), start=(k == 0), stop=(k == 7))
                b.op("tensor", mm_, reads=(kw, "h2T"), writes=(pk(pb),))
            def sl(e, Ft=Ft, pb=pb):
                return e.activation(out=fb(of_y1, 32 * CH)[:, Ft * CH:(Ft + 1) * CH],
                                    in_=ps(pb), func=AF.Silu,
                                    bias=AR[:, of_fb + Ft:of_fb + Ft + 1])
            b.op("scalar", sl, reads=("xlnT", "fence", pk(pb)), writes=("y1",))
            if DEBUG and fch == 3 and Ft == 0:
                def dy1(e):
                    return e.activation(out=fr(o_xt, CH),
                                        in_=fb(of_y1, 32 * CH)[:, 0:CH], func=AF.Copy)
                b.op("scalar", dy1, reads=("y1",), writes=("xt",))
                b.dma(dbg["d_y1"], fr(o_xt, CH), reads=("xt",))
                b.dma(dbg["d_xl2"], fr(of_xl2, 4 * DM), reads=("fxl",))
        for dm in range(8):
            kw = f"w2b{dm % 2}"
            b.dma(fb(of_w2b + (dm % 2) * 2048, 32 * 128).rearrange("p (k m) -> p k m", k=32),
                  sc_w2[:, :, dm * 128:(dm + 1) * 128],
                  reads=("w2_sc", "fence"), writes=(kw,))
            pb = 4 + (dm % 2)
            for k in range(32):
                def mm_(e, dm=dm, k=k, pb=pb):
                    wt = fb(of_w2b + (dm % 2) * 2048, 32 * 128)[:, k * 128:(k + 1) * 128]
                    y1k = fb(of_y1, 32 * CH)[:, k * CH:(k + 1) * CH]
                    return e.matmul(ps(pb), wt, y1k, start=(k == 0), stop=(k == 31))
                b.op("tensor", mm_, reads=(kw, "y1"), writes=(pk(pb),))
            def hv(e, dm=dm, pb=pb):
                return e.tensor_scalar(out=fr(of_z2 + dm * CH, CH), in0=ps(pb),
                                       scalar1=0.5,
                                       scalar2=AR[:, of_fb + 40 + dm:of_fb + 41 + dm],
                                       op0=ALU.mult, op1=ALU.add)
            b.op("vector", hv, reads=("xlnT", "fence", pk(pb)), writes=("z2",))
        if DEBUG and fch == 3:
            b.dma(dbg["d_z2"], fr(of_z2, 4096), reads=("z2",))
        for tt in range(4):
            for dm in range(8):
                bnk = 2 + (dm % 2)
                def tr(e, dm=dm, tt=tt, bnk=bnk):
                    return e.transpose(ps(bnk, 128),
                                       fr(of_z2 + dm * CH, CH)[:, tt * 128:(tt + 1) * 128],
                                       fr(o_idf, 128))
                b.op("tensor", tr, reads=("z2", "idf"), writes=(pk(bnk),))
                def zev(e, dm=dm, bnk=bnk):
                    return e.activation(out=fr(o_dum + dm * 128, 128),
                                        in_=ps(bnk, 128), func=AF.Copy)
                b.op("scalar", zev, reads=(pk(bnk),), writes=("sp",))
            def f1(e, tt=tt):
                return e.tensor_tensor(out=fr(o_xt, DM), in0=fr(of_x2 + tt * DM, DM),
                                       in1=fr(of_xl2 + tt * DM, DM), op=ALU.add)
            b.op("vector", f1, reads=("fx2", "fxl"), writes=("xt",))
            def f2(e, tt=tt):
                return e.tensor_tensor(out=fr(o_xt, DM), in0=fr(o_xt, DM),
                                       in1=fr(o_dum, DM), op=ALU.add)
            b.op("vector", f2, reads=("sp",), writes=("xt",))
            trow = fch * CH + tt * 128
            b.dma(out[trow:trow + 128, :], fr(o_xt, DM), reads=("xt",), writes=("outw",))
        if DEBUG:
            b.dma(dbg["d_x2"], fr(of_x2, 4 * DM), reads=("fx2",))

    b.emit("sync", lambda e: e.nop(), waits=[("sync", b.cnt["sync"])], sig=False)

    with nc.Block() as block:
        b.replay(block, sems)
    es.close()
    return nc


def _streams(inputs, bq, q):
    x = np.ascontiguousarray(inputs["L0_attn_hidden_states"][bq])
    s = OWN * q
    stream = np.concatenate([x[:s], x[s:][::-1], x[s:s + OWN]], 0)
    isf = [(c < 4 * q) or (c >= 16) for c in range(NCH)]

    def sel(f, r):
        f = np.ascontiguousarray(f); r = np.ascontiguousarray(r)
        return np.stack([f if isf[c] else r for c in range(NCH)], 0)

    d = {
        "xstream": stream,
        "conv_st": sel(inputs["L0_attn_conv_w"][:, 0, :], inputs["L0_attn_conv_w_b"][:, 0, :]),
        "convb_st": sel(inputs["L0_attn_conv_b"].reshape(SLAB, 128).T,
                        inputs["L0_attn_conv_b_b"].reshape(SLAB, 128).T),
        "xproj_st": sel(inputs["L0_attn_x_proj_w"].T, inputs["L0_attn_x_proj_w_b"].T),
        "dtw_st": sel(inputs["L0_attn_dt_proj_w"].T, inputs["L0_attn_dt_proj_w_b"].T),
        "dtb_st": sel(inputs["L0_attn_dt_proj_b"].reshape(SLAB, 128).T,
                      inputs["L0_attn_dt_proj_b_b"].reshape(SLAB, 128).T),
        "alog_st": sel(inputs["L0_attn_A_log"], inputs["L0_attn_A_b_log"]),
        "dvec_st": sel(inputs["L0_attn_D"].reshape(SLAB, 128).T,
                       inputs["L0_attn_D_b"].reshape(SLAB, 128).T),
    }
    pk_ = np.zeros((128, NCH), np.float32)
    if q > 0:
        pk_[:, 4 * q] = -1e38
    hk = np.ones((128, NCH), np.float32); hk[:, 4 * q] = 0.0
    sv = np.zeros((128, NCH), np.float32); sv[:, 4 * q] = 1.0
    sm = np.zeros((128, NPRE), np.float32)
    if q > 0:
        sm[:, 4 * q - 1] = 1.0
    d.update({"pka": pk_, "hkil": hk, "hsav": sv, "hsavn": 1.0 - sv, "selm": sm})
    return d


def kernel(**inputs):
    import ml_dtypes
    inputs = {k: np.asarray(v) for k, v in inputs.items()}
    nc = build_program()

    selb = np.zeros((32, SLAB * 128), np.float32)
    for n in range(SLAB):
        selb[n, n * 128:(n + 1) * 128] = 1.0

    shared = {
        "win_xT": np.ascontiguousarray(inputs["L0_attn_in_proj_w"][:DI].T),
        "win_zT": np.ascontiguousarray(inputs["L0_attn_in_proj_w"][DI:].T),
        "wout_T": np.ascontiguousarray(inputs["L0_attn_out_proj_w"].T),
        "selb": selb,
        "id128b": np.eye(128, dtype=ml_dtypes.bfloat16),
        "id128f": np.eye(128, dtype=np.float32),
        "fc1_q": np.ascontiguousarray(inputs["L0_ffn_fc1_q"]),
        "fc1_sc": np.ascontiguousarray(inputs["L0_ffn_fc1_scales"]),
        "fc1_b": np.ascontiguousarray(inputs["L0_ffn_fc1_bias"].reshape(32, 128).T),
        "fc2_q": np.ascontiguousarray(inputs["L0_ffn_fc2_q"]),
        "fc2_sc": np.ascontiguousarray(inputs["L0_ffn_fc2_scales"]),
        "fc2_b": np.ascontiguousarray(inputs["L0_ffn_fc2_bias"].reshape(8, 128).T),
        "ln_g": np.ascontiguousarray(inputs["L0_ffn_ln_gamma"]),
        "ln_b": np.ascontiguousarray(inputs["L0_ffn_ln_beta"]),
    }
    in_maps = []
    for cc in range(8):
        bq, q = divmod(cc, 4)
        m = dict(shared)
        m.update(_streams(inputs, bq, q))
        in_maps.append(m)

    res = run_bass_kernel_spmd(nc, in_maps, core_ids=list(range(8)))
    outp = np.empty((2, L, DM), np.float32)
    for cc in range(8):
        bq, q = divmod(cc, 4)
        outp[bq, OWN * q:OWN * (q + 1)] = np.asarray(res.results[cc]["out"])
    if DEBUG:
        kernel.last_results = res
    return outp


# version 09
# speedup vs baseline: 1.3849x; 1.3849x over previous
import os
import sys
import numpy as np

sys.path.insert(0, "/opt/trn_rl_repo")
import concourse.bass as bass
import concourse.mybir as mybir
from concourse.bass_utils import run_bass_kernel_spmd

F32 = mybir.dt.float32
BF16 = mybir.dt.bfloat16
I32 = mybir.dt.int32
AF = mybir.ActivationFunctionType
ALU = mybir.AluOpType

DM = 1024
DI = 2048
CHQ = 512
S = 4
NS = 16
RK = 64
L = 8192
NT = 1024
NSC = 8
OWN = 2048
EPS = 1e-5
DEBUG = bool(int(os.environ.get("BASSMK_DEBUG", "0")))

ENGINES = ("sync", "tensor", "scalar", "vector", "gpsimd")
GROUPS = [[0, 1, 2, 3], [4, 5, 6, 7]]


class Buf:
    def __init__(self):
        self.w = None
        self.readers = {}

    def wr(self):
        return [(e, v) for e, v in self.readers.items()] + \
            ([self.w] if self.w is not None else [])

    def setw(self, tok):
        self.w = tok
        self.readers = {}

    def rd(self):
        return [self.w] if self.w is not None else []

    def setr(self, tok):
        if tok is not None:
            e, v = tok
            if self.readers.get(e, 0) < v:
                self.readers[e] = v


class Bld:
    def __init__(self):
        self.ops = {e: [] for e in ENGINES}
        self.cnt = {e: 0 for e in ENGINES}
        self.bufs = {}

    def B(self, key):
        if key not in self.bufs:
            self.bufs[key] = Buf()
        return self.bufs[key]

    def emit(self, eng, fn, waits=(), sig=True, sig_as=None):
        ws = []
        for t in waits:
            if t is None:
                continue
            pe, v = t
            if pe == eng == "tensor" and sig_as is None:
                continue
            ws.append((pe, v))
        if sig:
            sname = sig_as or eng
            self.cnt.setdefault(sname, 0)
            self.cnt[sname] += 16 if sname == "sync" else 1
            tok = (sname, self.cnt[sname])
        else:
            sname = None
            tok = None
        self.ops[eng].append((fn, ws, sname))
        return tok

    def op(self, eng, fn, reads=(), writes=(), sig_as=None):
        waits = []
        for k in reads:
            waits += self.B(k).rd()
        for k in writes:
            waits += self.B(k).wr()
        tok = self.emit(eng, fn, waits=waits, sig_as=sig_as)
        for k in reads:
            self.B(k).setr(tok)
        for k in writes:
            self.B(k).setw(tok)
        return tok

    def dma(self, out, in_, reads=(), writes=()):
        def fn(e):
            return e.dma_start(out=out, in_=in_)
        return self.op("sync", fn, reads=reads, writes=writes)

    def replay(self, block, sems):
        for eng_name in ENGINES:
            ops = self.ops[eng_name]
            if not ops:
                continue

            def make_body(ename, olist):
                def body(eng):
                    seen = {}
                    cum = 0
                    for fn, ws, sname in olist:
                        for pe, v in ws:
                            if seen.get(pe, 0) >= v:
                                continue
                            seen[pe] = v
                            eng.wait_ge(sems[pe], v)
                        inst = fn(eng)
                        if sname:
                            inst.then_inc(sems[sname], 16 if sname == "sync" else 1)
                        if sname == "sync":
                            cum += 16
                            eng.wait_ge(sems["sync"], cum)
                            seen["sync"] = cum
                return body
            getattr(block, eng_name)(make_body(eng_name, ops))


def build_program():
    nc = bass.Bass("TRN2", target_bir_lowering=False)

    def din(name, shape, dt=F32):
        return nc.dram_tensor(name, shape, dt, kind="ExternalInput").ap()

    x_in = din("x_in", [L, DM])
    xq_in = din("xq_in", [OWN, DM])
    winx = din("winx", [DM, CHQ])
    winz = din("winz", [DM, CHQ])
    wout = din("wout", [CHQ, DM])
    xpw = din("xpw", [2, CHQ, 96])
    dtw = din("dtw", [2, RK, CHQ])
    convw = din("convw", [128, 2 * S * 4])
    convb = din("convb", [128, 2 * S])
    dtb = din("dtb", [128, 2 * S])
    alog = din("alog", [128, 2 * S * NS])
    dvec = din("dvec", [128, 2 * S])
    selb = din("selb", [32, 32 * 128], BF16)
    id128f = din("id128f", [128, 128])
    fc1_q = din("fc1_q", [DM, 4 * DM], I32)
    fc1_sc = din("fc1_sc", [8, 4 * DM])
    fc1_b = din("fc1_b", [128, 32])
    fc2_q = din("fc2_q", [4 * DM, DM], I32)
    fc2_sc = din("fc2_sc", [32, DM])
    fc2_b = din("fc2_b", [128, 8])
    ln_g = din("ln_g", [DM])
    ln_b = din("ln_b", [DM])

    out = nc.dram_tensor("out", [OWN, DM], F32, kind="ExternalOutput").ap()

    def dscr(name, shape, dt=F32):
        return nc.dram_tensor(name, shape, dt, kind="Internal").ap()

    sc_xz = dscr("sc_xz", [NSC, S, 128, NT])
    sc_sz = dscr("sc_sz", [NSC, S, 128, NT])
    sc_yf = dscr("sc_yf", [NSC, S, 128, NT])
    xd_in = dscr("xd_in", [16, 96, NT])
    xd_out = dscr("xd_out", [16, 96, NT])
    op_in = dscr("op_in", [2, 4, 8, 128, NT])
    op_out = dscr("op_out", [2, 8, 128, NT])
    sc_w1 = dscr("sc_w1", [128, 8, 4 * DM], BF16)
    sc_w2 = dscr("sc_w2", [128, 32, DM], BF16)

    dbg = {}
    if DEBUG:
        for nm, shape in [("d_xln", [128, DM]), ("d_xzf", [128, NT]),
                          ("d_uf", [128, NT]), ("d_xdbl", [96, NT]),
                          ("d_delta", [128, NT]), ("d_E", [128, 512]),
                          ("d_bh", [128, 512]), ("d_h", [128, 512]),
                          ("d_yf", [128, NT]), ("d_ur", [128, NT]),
                          ("d_xdblr", [96, NT]),
                          ("d_yr", [128, NT]), ("d_op", [128, NT]),
                          ("d_x1", [128, DM]), ("d_sz", [128, NT]),
                          ("d_dtm", [128, NT]), ("d_Ball", [128, 512]),
                          ("d_dtrb", [128, NT])]:
            dbg[nm] = nc.dram_tensor(nm, shape, F32, kind="ExternalOutput").ap()

    ACOLS = 53180
    _off = [0]

    def A(n):
        o = _off[0]
        _off[0] += n
        assert _off[0] <= ACOLS, f"arena overflow at {_off[0]}"
        return o

    o_winx = A(2048)
    o_winz = A(2048)
    o_wout = A(2048)
    o_xpw = A(384)
    o_dtw = A(512)
    o_selb = A(2048)
    o_idf = A(128)
    o_idb = A(64)
    o_cw = A(32)
    o_cb = A(8)
    o_dtb = A(8)
    o_dv = A(8)
    o_ac = A(128)
    o_mu = A(16)
    o_st = A(320)
    o_x = A(2048)
    o_xln = A(1024)
    o_xlnT = A(4096)
    o_xzp = A(4112)
    o_ub = A(1024)
    o_sz = A(2048)
    o_u = A(8192)
    o_delta = A(4096)
    o_du = A(4096)
    o_xdb = A(1024)
    o_bctb = A(512)
    o_Ball = A(1024)
    o_Call = A(1024)
    o_E = A(1024)
    o_bh = A(1024)
    o_h = A(1024)
    o_q = A(512)
    o_sp = A(2048)
    o_gp = A(1024)
    o_dq = A(1536)
    o_dtrb = A(512)
    assert _off[0] <= ACOLS, _off[0]

    o_yf = o_xlnT
    o_szr = o_x
    o_ymix = o_x + 1024
    o_ops = o_sz
    of_x1 = o_u
    of_xl2 = o_u + 4096
    of_h2T = o_delta
    of_w1b = o_delta + 2048
    of_w2b = o_delta + 3072
    of_y1 = o_xlnT
    of_opsb = o_Ball
    of_gb = o_sz
    of_fb = o_bctb

    def z2sl(dm):
        offs = [o_gp, o_gp + 512, o_dq, o_dq + 512, o_dq + 1024,
                o_ub, o_ub + 512, o_xdb]
        return offs[dm]

    b = Bld()

    import contextlib
    es = contextlib.ExitStack()
    AR = es.enter_context(nc.sbuf_tensor("AR", [128, ACOLS], F32))
    PSF = es.enter_context(nc.psum_tensor("PSF", [128, 4096], F32))
    sems = {e: es.enter_context(nc.semaphore("sem_" + e))
            for e in ENGINES + ("coll",)}

    def fr(off, n):
        return AR[:, off:off + n]

    def fb(off, n_bf16):
        return AR[:, off:off + (n_bf16 + 1) // 2].bitcast(BF16)[:, 0:n_bf16]

    def ps(bank, n=512):
        return PSF[:, bank * 512:bank * 512 + n]

    pk = lambda bank: f"ps{bank}"

    b.op("vector", lambda e: e.memset(AR[:, o_mu + 8:o_mu + 9], EPS), writes=("eps",))
    b.op("vector", lambda e: e.memset(fr(o_st, 320), 0.0),
         writes=("stF0", "stF1", "stR0", "stR1", "stM"))

    b.dma(fb(o_selb, 4096)[0:32, :], selb, writes=("selb",))
    b.dma(fr(o_idf, 128), id128f, writes=("idf",))
    b.dma(fr(o_cw, 32), convw, writes=("cw",))
    b.dma(fr(o_cb, 8), convb, writes=("cb",))
    b.dma(fr(o_dtb, 8), dtb, writes=("dtbk",))
    b.dma(fr(o_dv, 8), dvec, writes=("dv",))
    b.dma(fr(o_sp, 128), alog, writes=("sp0",))
    def mkA(e):
        return e.activation(out=fr(o_ac, 128), in_=fr(o_sp, 128), func=AF.Exp)
    b.op("scalar", mkA, reads=("sp0",), writes=("ac",))
    def negA(e):
        return e.tensor_scalar(out=fr(o_ac, 128), in0=fr(o_ac, 128),
                               scalar1=-1.0, scalar2=None, op0=ALU.mult)
    b.op("vector", negA, reads=(), writes=("ac",))
    def mkidb(e):
        return e.activation(out=fb(o_idb, 128), in_=fr(o_idf, 128), func=AF.Copy)
    b.op("scalar", mkidb, reads=("idf",), writes=("idb",))

    def stage_resident(srcs, ncols, dst_off, key):
        half = ncols // 2
        for hh in range(2):
            b.dma(fr(o_u, half), srcs[hh], writes=("u0_0", "u0_1", "u0_2", "u0_3"))
            def cast(e, hh=hh, half=half, dst_off=dst_off, ncols=ncols):
                return e.activation(out=fb(dst_off, ncols)[:, hh * half:(hh + 1) * half],
                                    in_=fr(o_u, half), func=AF.Copy)
            b.op("scalar", cast, reads=("u0_0",), writes=(key,))

    winx_v = winx.rearrange("(k p) m -> p k m", p=128)
    winz_v = winz.rearrange("(k p) m -> p k m", p=128)
    wout_v = wout.rearrange("(s p) m -> p s m", p=128)
    stage_resident([winx_v[:, 0:4, :], winx_v[:, 4:8, :]], 4096, o_winx, "winx")
    stage_resident([winz_v[:, 0:4, :], winz_v[:, 4:8, :]], 4096, o_winz, "winz")
    stage_resident([wout_v[:, 0:2, :], wout_v[:, 2:4, :]], 4096, o_wout, "wout")
    b.dma(fr(o_u, 768), xpw.rearrange("d (s p) r -> p d s r", p=128),
          writes=("u0_0", "u0_1", "u0_2", "u0_3"))
    def cxp(e):
        return e.activation(out=fb(o_xpw, 768), in_=fr(o_u, 768), func=AF.Copy)
    b.op("scalar", cxp, reads=("u0_0",), writes=("xpw",))
    b.dma(fr(o_u, 1024)[0:64, :], dtw.rearrange("d r m -> r d m"),
          writes=("u0_0", "u0_1", "u0_2", "u0_3"))
    def cdt(e):
        return e.activation(out=fb(o_dtw, 1024)[0:64, :], in_=fr(o_u, 1024)[0:64, :],
                            func=AF.Copy)
    b.op("scalar", cdt, reads=("u0_0",), writes=("dtw",))

    winx_t = lambda k, m: fb(o_winx, 8 * CHQ)[:, k * CHQ + m * 128:k * CHQ + (m + 1) * 128]
    winz_t = lambda k, m: fb(o_winz, 8 * CHQ)[:, k * CHQ + m * 128:k * CHQ + (m + 1) * 128]
    wout_t = lambda s, d: fb(o_wout, S * DM)[:, s * DM + d * 128:s * DM + (d + 1) * 128]
    xpw_t = lambda dir_, m: fb(o_xpw, 768)[:, dir_ * 384 + m * 96:dir_ * 384 + (m + 1) * 96]
    dtw_t = lambda dir_, m: fb(o_dtw, 1024)[0:64, dir_ * 512 + m * 128:dir_ * 512 + (m + 1) * 128]
    selb_t = lambda j: fb(o_selb, 4096)[0:32, j * 128:(j + 1) * 128]
    id_b = lambda: fb(o_idb, 128)
    wcol = lambda dir_, m, j: AR[:, o_cw + dir_ * 16 + m * 4 + j:o_cw + dir_ * 16 + m * 4 + j + 1]
    cbcol = lambda dir_, m: AR[:, o_cb + dir_ * 4 + m:o_cb + dir_ * 4 + m + 1]
    dtbcol = lambda dir_, m: AR[:, o_dtb + dir_ * 4 + m:o_dtb + dir_ * 4 + m + 1]
    dvcol = lambda dir_, m: AR[:, o_dv + dir_ * 4 + m:o_dv + dir_ * 4 + m + 1]
    acol = lambda dir_, m, n: AR[:, o_ac + dir_ * 64 + m * 16 + n:o_ac + dir_ * 64 + m * 16 + n + 1]
    stcol = lambda base, lane: AR[:, o_st + base + lane:o_st + base + lane + 1]
    ST_F = (0, 64)
    ST_R = (128, 192)
    ST_M = 256
    xzp_sl = lambda m: AR[:, o_xzp + m * 1028:o_xzp + m * 1028 + 1028]
    u_sl = lambda par, m: fr(o_u + par * 4096 + m * NT, NT)
    xlnT_k = lambda k: fb(o_xlnT, 8 * NT)[:, k * NT:(k + 1) * NT]

    def ln_stats(src_ap, src_key):
        def s1(e):
            return e.activation(out=fr(o_sp, DM), in_=src_ap, func=AF.Copy,
                                accum_out=AR[:, o_mu:o_mu + 1])
        b.op("scalar", s1, reads=(src_key,), writes=("sp0", "mu"))
        def s2(e):
            return e.activation(out=fr(o_sp, DM), in_=src_ap, func=AF.Square,
                                accum_out=AR[:, o_mu + 1:o_mu + 2])
        b.op("scalar", s2, reads=(src_key,), writes=("sp0", "mu"))
        def mstat(e):
            return e.tensor_scalar(out=AR[:, o_mu + 2:o_mu + 4],
                                   in0=AR[:, o_mu:o_mu + 2], scalar1=1.0 / DM,
                                   scalar2=None, op0=ALU.mult)
        b.op("vector", mstat, reads=(), writes=("mu",))
        def musq(e):
            return e.activation(out=AR[:, o_mu + 9:o_mu + 10],
                                in_=AR[:, o_mu + 2:o_mu + 3], func=AF.Square)
        b.op("scalar", musq, reads=("mu",), writes=("mu",))
        def nvar(e):
            return e.tensor_tensor(out=AR[:, o_mu + 4:o_mu + 5],
                                   in0=AR[:, o_mu + 9:o_mu + 10],
                                   in1=AR[:, o_mu + 3:o_mu + 4], op=ALU.subtract)
        b.op("vector", nvar, reads=(), writes=("mu",))
        def lnv(e):
            return e.activation(out=AR[:, o_mu + 5:o_mu + 6],
                                in_=AR[:, o_mu + 4:o_mu + 5], func=AF.Ln,
                                scale=-1.0, bias=AR[:, o_mu + 8:o_mu + 9])
        b.op("scalar", lnv, reads=("mu", "eps"), writes=("mu",))
        def rstd(e):
            return e.activation(out=AR[:, o_mu + 6:o_mu + 7],
                                in_=AR[:, o_mu + 5:o_mu + 6], func=AF.Exp,
                                scale=-0.5)
        b.op("scalar", rstd, reads=(), writes=("mu",))
        def nb(e):
            return e.tensor_scalar(out=AR[:, o_mu + 7:o_mu + 8],
                                   in0=AR[:, o_mu + 2:o_mu + 3],
                                   scalar1=AR[:, o_mu + 6:o_mu + 7],
                                   scalar2=-1.0, op0=ALU.mult, op1=ALU.mult)
        b.op("vector", nb, reads=(), writes=("mu",))

    def emit_conv(m, dir_):
        taps = [0, 1, 2, 3] if dir_ == 0 else [3, 2, 1, 0]
        kxz = f"xz{m}"
        def c0(e, m=m, dir_=dir_, taps=taps):
            return e.tensor_scalar(out=fr(o_sp, NT), in0=xzp_sl(m)[:, 0:NT],
                                   scalar1=wcol(dir_, m, taps[0]),
                                   scalar2=cbcol(dir_, m),
                                   op0=ALU.mult, op1=ALU.add)
        b.op("vector", c0, reads=(kxz, "cw", "cb"), writes=("sp0",))
        srcdst = [("sp0", "sp1"), ("sp1", "sp0"), ("sp0", "sp1")]
        offd = [o_sp + 1024, o_sp, o_sp + 1024]
        offs_ = [o_sp, o_sp + 1024, o_sp]
        for j in range(1, 4):
            def cj(e, m=m, j=j, dir_=dir_, taps=taps):
                return e.scalar_tensor_tensor(out=fr(offd[j - 1], NT),
                                              in0=xzp_sl(m)[:, j:NT + j],
                                              scalar=wcol(dir_, m, taps[j]),
                                              in1=fr(offs_[j - 1], NT),
                                              op0=ALU.mult, op1=ALU.add)
            b.op("vector", cj, reads=(kxz, "cw", srcdst[j - 1][0]),
                 writes=(srcdst[j - 1][1],))
        return "sp1"

    def emit_u(m, par, kcv):
        ku = f"u{par}_{m}"
        def silu(e, m=m, par=par):
            return e.activation(out=u_sl(par, m), in_=fr(o_sp + 1024, NT), func=AF.Silu)
        b.op("scalar", silu, reads=(kcv,), writes=(ku,))
        kub = f"ub{m % 2}"
        def silub(e, m=m):
            return e.activation(out=fb(o_ub + (m % 2) * 512, NT),
                                in_=fr(o_sp + 1024, NT), func=AF.Silu)
        b.op("scalar", silub, reads=(kcv,), writes=(kub,))
        return ku, kub

    def emit_xp(m, dir_, kub):
        for half in range(2):
            def xp_(e, m=m, half=half, dir_=dir_):
                return e.matmul(ps(half)[0:96, :], xpw_t(dir_, m),
                                fb(o_ub + (m % 2) * 512, NT)[:, half * 512:(half + 1) * 512],
                                start=(m == 0), stop=(m == S - 1))
            b.op("tensor", xp_, reads=("xpw", kub), writes=(pk(half),))

    def emit_xp_finish(c, dir_):
        slot = dir_ * 8 + c
        def xpev(e):
            return e.activation(out=fr(o_ub, 1024)[0:96, :], in_=ps(0, 1024)[0:96, :],
                                func=AF.Copy)
        b.op("scalar", xpev, reads=(pk(0), pk(1)), writes=("ub0", "ub1", "xdsb"))
        b.dma(xd_in[slot], fr(o_ub, 1024)[0:96, :],
              reads=("xdsb", "ub0", "ub1"), writes=("xdw",))
        def coll(e, slot=slot):
            return e.collective_compute(
                "AllReduce", ALU.add, replica_groups=GROUPS,
                ins=[xd_in[slot]], outs=[xd_out[slot]])
        b.op("gpsimd", coll, reads=("xdw",), writes=(f"xdo{slot}",), sig_as="coll")

    def emit_dequant_tile(tile):
        if tile < 8:
            k, W, qd, scd, dst = tile, 4 * DM, fc1_q, fc1_sc, sc_w1
        else:
            k, W, qd, scd, dst = tile - 8, DM, fc2_q, fc2_sc, sc_w2
        for cc in range(W // 512):
            b.dma(AR[:, o_dq:o_dq + 512].bitcast(I32),
                  qd[k * 128:(k + 1) * 128, cc * 512:(cc + 1) * 512],
                  writes=("dq0",))
            b.dma(fr(o_dq + 512, 512),
                  scd[k, cc * 512:(cc + 1) * 512].partition_broadcast(128),
                  writes=("dq1",))
            def cvt(e):
                return e.activation(out=fr(o_dq + 1024, 512),
                                    in_=AR[:, o_dq:o_dq + 512].bitcast(I32),
                                    func=AF.Copy, bias=-128.0)
            b.op("scalar", cvt, reads=("dq0",), writes=("dq2",))
            def mul(e):
                return e.tensor_tensor(out=fb(o_dq, 512), in0=fr(o_dq + 1024, 512),
                                       in1=fr(o_dq + 512, 512), op=ALU.mult)
            b.op("gpsimd", mul, reads=("dq1", "dq2"), writes=("dq0",))
            b.dma(dst[:, k, cc * 512:(cc + 1) * 512], fb(o_dq, 512),
                  reads=("dq0",), writes=("wsc",))

    def emit_A_fwd(c):
        par = c % 2
        for tt in range(8):
            kx = f"x{tt % 2}"
            b.dma(fr(o_x + (tt % 2) * DM, DM),
                  x_in[c * NT + tt * 128:c * NT + (tt + 1) * 128, :], writes=(kx,))
            xa = fr(o_x + (tt % 2) * DM, DM)
            ln_stats(xa, kx)
            def norm(e, xa=xa):
                return e.tensor_scalar(out=fr(o_xln, DM), in0=xa,
                                       scalar1=AR[:, o_mu + 6:o_mu + 7],
                                       scalar2=AR[:, o_mu + 7:o_mu + 8],
                                       op0=ALU.mult, op1=ALU.add)
            b.op("vector", norm, reads=("mu", kx), writes=("xl",))
            if DEBUG and c == 0 and tt == 0:
                b.dma(dbg["d_xln"], fr(o_xln, DM), reads=("xl",))
            for k in range(8):
                bnk = k % 2
                def tr(e, k=k, bnk=bnk):
                    return e.transpose(ps(bnk, 128),
                                       fr(o_xln, DM)[:, k * 128:(k + 1) * 128],
                                       fr(o_idf, 128))
                b.op("tensor", tr, reads=("xl", "idf"), writes=(pk(bnk),))
                def evac(e, tt=tt, k=k, bnk=bnk):
                    return e.activation(
                        out=xlnT_k(k)[:, tt * 128:(tt + 1) * 128],
                        in_=ps(bnk, 128), func=AF.Copy)
                b.op("scalar", evac, reads=(pk(bnk),), writes=("xlnT",))
        for m in range(S):
            pb = 2 + 2 * (m % 2)
            for half in range(2):
                for k in range(8):
                    def mm_(e, k=k, m=m, half=half, pb=pb):
                        return e.matmul(ps(pb + half), winx_t(k, m),
                                        xlnT_k(k)[:, half * 512:(half + 1) * 512],
                                        start=(k == 0), stop=(k == 7))
                    b.op("tensor", mm_, reads=("winx", "xlnT"),
                         writes=(pk(pb + half),))
            kxz = f"xz{m}"
            if c == 0:
                b.op("vector", lambda e, m=m: e.memset(xzp_sl(m)[:, 0:3], 0.0),
                     writes=(kxz,))
            else:
                def tl(e, m=m):
                    return e.tensor_copy(out=xzp_sl(m)[:, 0:3],
                                         in_=xzp_sl(m)[:, NT:NT + 3])
                b.op("vector", tl, reads=(), writes=(kxz,))
            def xev(e, m=m, pb=pb):
                return e.activation(out=xzp_sl(m)[:, 3:NT + 3], in_=ps(pb, 1024),
                                    func=AF.Copy)
            b.op("scalar", xev, reads=(pk(pb), pk(pb + 1)), writes=(kxz,))
            b.dma(sc_xz[c, m], xzp_sl(m)[:, 3:NT + 3], reads=(kxz,), writes=("scxz",))
            if DEBUG and c == 0 and m == 0:
                b.dma(dbg["d_xzf"], xzp_sl(m)[:, 3:NT + 3], reads=(kxz,))
            kcv = emit_conv(m, 0)
            ku, kub = emit_u(m, par, kcv)
            if DEBUG and c == 0 and m == 0:
                b.dma(dbg["d_uf"], u_sl(par, m), reads=(ku,))
            emit_xp(m, 0, kub)
        emit_xp_finish(c, 0)
        for m in range(S):
            pb = 2 + 2 * (m % 2)
            for half in range(2):
                for k in range(8):
                    def mmz(e, k=k, m=m, half=half, pb=pb):
                        return e.matmul(ps(pb + half), winz_t(k, m),
                                        xlnT_k(k)[:, half * 512:(half + 1) * 512],
                                        start=(k == 0), stop=(k == 7))
                    b.op("tensor", mmz, reads=("winz", "xlnT"),
                         writes=(pk(pb + half),))
            ksz = f"sz{m % 2}"
            def zsil(e, m=m, pb=pb):
                return e.activation(out=fr(o_sz + (m % 2) * 1024, NT),
                                    in_=ps(pb, 1024), func=AF.Silu)
            b.op("scalar", zsil, reads=(pk(pb), pk(pb + 1)), writes=(ksz,))
            b.dma(sc_sz[c, m], fr(o_sz + (m % 2) * 1024, NT), reads=(ksz,),
                  writes=("scsz",))
            if DEBUG and c == 0 and m == 0:
                b.dma(dbg["d_sz"], fr(o_sz + (m % 2) * 1024, NT), reads=(ksz,))

    def emit_A_rev(c):
        par = c % 2
        for m in range(S):
            kxz = f"xz{m}"
            if c == NSC - 1:
                b.op("vector", lambda e, m=m: e.memset(xzp_sl(m)[:, NT:NT + 3], 0.0),
                     writes=(kxz,))
            else:
                def hd(e, m=m):
                    return e.tensor_copy(out=xzp_sl(m)[:, NT:NT + 3],
                                         in_=xzp_sl(m)[:, 0:3])
                b.op("vector", hd, reads=(), writes=(kxz,))
            b.dma(xzp_sl(m)[:, 0:NT], sc_xz[c, m], reads=("scxz", kxz), writes=(kxz,))
            kcv = emit_conv(m, 1)
            ku, kub = emit_u(m, par, kcv)
            if DEBUG and c == NSC - 1 and m == 0:
                b.dma(dbg["d_ur"], u_sl(par, m), reads=(ku,))
            emit_xp(m, 1, kub)
        emit_xp_finish(c, 1)

    stF = [0]
    stR = [0]

    def emit_B(c, dir_):
        par = c % 2
        rev = dir_ == 1
        slot = dir_ * 8 + c
        b.dma(fr(o_xdb, NT)[0:96, :], xd_out[slot], reads=(f"xdo{slot}",),
              writes=("xdb",))
        if DEBUG and ((not rev and c == 0) or (rev and c == NSC - 1)):
            b.dma(dbg["d_xdblr" if rev else "d_xdbl"], fr(o_xdb, NT)[0:96, :],
                  reads=("xdb",))
        def bct(e):
            return e.activation(out=fb(o_bctb, NT)[0:32, :],
                                in_=fr(o_xdb, NT)[64:96, :], func=AF.Copy)
        b.op("scalar", bct, reads=("xdb",), writes=("bctb",))
        def dtrc(e):
            return e.activation(out=fb(o_dtrb, NT)[0:64, :],
                                in_=fr(o_xdb, NT)[0:64, :], func=AF.Copy)
        b.op("scalar", dtrc, reads=("xdb",), writes=("dtrb",))
        if DEBUG and not rev and c == 0:
            def dtrd(e):
                return e.activation(out=fr(o_sp, NT)[0:64, :],
                                    in_=fb(o_dtrb, NT)[0:64, :], func=AF.Copy)
            b.op("scalar", dtrd, reads=("dtrb",), writes=("sp0",))
            b.dma(dbg["d_dtrb"][0:64, :], fr(o_sp, NT)[0:64, :], reads=("sp0",))
        if rev:
            for m in range(S):
                b.dma(fr(o_yf + m * NT, NT), sc_yf[c, m],
                      reads=("scyf", "revfence"), writes=(f"yf{m}",))
        for m in range(S):
            for half in range(2):
                def dt_(e, m=m, half=half, dir_=dir_):
                    return e.matmul(ps(2 + half), dtw_t(dir_, m),
                                    fb(o_dtrb, NT)[0:64, half * 512:(half + 1) * 512],
                                    start=True, stop=True)
                b.op("tensor", dt_, reads=("dtw", "dtrb"), writes=(pk(2 + half),))
            def vb(e, m=m, dir_=dir_):
                return e.tensor_scalar(out=fr(o_sp, NT), in0=ps(2, 1024),
                                       scalar1=dtbcol(dir_, m), scalar2=None,
                                       op0=ALU.add)
            b.op("vector", vb, reads=(pk(2), pk(3), "dtbk"), writes=("sp0",))
            if DEBUG and not rev and c == 0 and m == 0:
                b.dma(dbg["d_dtm"], fr(o_sp, NT), reads=("sp0",))
            def ab(e):
                return e.scalar_tensor_tensor(out=fr(o_sp + 1024, NT),
                                              in0=fr(o_sp, NT), scalar=-1.0,
                                              in1=fr(o_sp, NT),
                                              op0=ALU.mult, op1=ALU.max)
            b.op("vector", ab, reads=("sp0",), writes=("sp1",))
            def e1(e):
                return e.activation(out=fr(o_E, NT), in_=fr(o_sp + 1024, NT),
                                    func=AF.Exp, scale=-1.0)
            b.op("scalar", e1, reads=("sp1",), writes=("E0", "E1"))
            def l1(e):
                return e.activation(out=fr(o_sp + 1024, NT), in_=fr(o_E, NT),
                                    func=AF.Ln, bias=1.0)
            b.op("scalar", l1, reads=("E0", "E1"), writes=("sp1",))
            def rl(e):
                return e.tensor_scalar(out=fr(o_E, NT), in0=fr(o_sp, NT),
                                       scalar1=0.0, scalar2=None, op0=ALU.max)
            b.op("vector", rl, reads=("sp0",), writes=("E0", "E1"))
            def dl(e, m=m):
                return e.tensor_tensor(out=fr(o_delta + m * NT, NT),
                                       in0=fr(o_sp + 1024, NT),
                                       in1=fr(o_E, NT), op=ALU.add)
            b.op("vector", dl, reads=("sp1", "E0", "E1"), writes=(f"dl{m}",))
            if DEBUG and ((not rev and c == 0) or (rev and c == NSC - 1)) and m == 0:
                b.dma(dbg["d_delta"], fr(o_delta + m * NT, NT), reads=(f"dl{m}",))
            def du_(e, m=m, par=par):
                return e.tensor_tensor(out=fr(o_du + m * NT, NT),
                                       in0=fr(o_delta + m * NT, NT),
                                       in1=u_sl(par, m), op=ALU.mult)
            b.op("gpsimd", du_, reads=(f"dl{m}", f"u{par}_{m}"), writes=(f"du{m}",))
        st_bases, st_idx = (ST_R, stR) if rev else (ST_F, stF)
        fkeys = ("stR0", "stR1") if rev else ("stF0", "stF1")
        for half_i in range(2):
            half = (1 - half_i) if rev else half_i
            first = half_i == 0
            hs = slice(half * 512, (half + 1) * 512)
            for n in range(NS):
                rp = n % 2
                def brep(e, n=n, hs=hs):
                    return e.matmul(ps(6), selb_t(n),
                                    fb(o_bctb, NT)[0:32, hs], start=True, stop=True)
                b.op("tensor", brep, reads=("selb", "bctb"), writes=(pk(6),))
                def crep(e, n=n, hs=hs):
                    return e.matmul(ps(7), selb_t(16 + n),
                                    fb(o_bctb, NT)[0:32, hs], start=True, stop=True)
                b.op("tensor", crep, reads=("selb", "bctb"), writes=(pk(7),))
                def bev(e, rp=rp):
                    return e.activation(out=fr(o_Ball + rp * 512, 512), in_=ps(6),
                                        func=AF.Copy)
                b.op("scalar", bev, reads=(pk(6),), writes=(f"Ba{rp}",))
                def cev(e, rp=rp):
                    return e.activation(out=fr(o_Call + rp * 512, 512), in_=ps(7),
                                        func=AF.Copy)
                b.op("scalar", cev, reads=(pk(7),), writes=(f"Ca{rp}",))
                if DEBUG and not rev and c == 0 and n == 0 and half == 0:
                    b.dma(dbg["d_Ball"], fr(o_Ball + rp * 512, 512), reads=(f"Ba{rp}",))
                for m in range(S):
                    lane = m * 16 + n
                    ep = (n * S + m) % 2
                    def E_(e, m=m, n=n, hs=hs, ep=ep, dir_=dir_):
                        return e.activation(out=fr(o_E + ep * 512, 512),
                                            in_=fr(o_delta + m * NT, NT)[:, hs],
                                            func=AF.Exp, scale=acol(dir_, m, n))
                    b.op("scalar", E_, reads=(f"dl{m}", "ac"), writes=(f"E{ep}",))
                    def bh(e, m=m, hs=hs, ep=ep, rp=rp):
                        return e.tensor_tensor(out=fr(o_bh + ep * 512, 512),
                                               in0=fr(o_du + m * NT, NT)[:, hs],
                                               in1=fr(o_Ball + rp * 512, 512),
                                               op=ALU.mult)
                    b.op("gpsimd", bh, reads=(f"du{m}", f"Ba{rp}"), writes=(f"bh{ep}",))
                    if first:
                        init_ap = stcol(st_bases[st_idx[0]], lane)
                        ikey = fkeys[st_idx[0]]
                    else:
                        init_ap = stcol(ST_M, lane)
                        ikey = "stM"
                    def scan(e, ep=ep, init_ap=init_ap, rev=rev):
                        oh = fr(o_h + ep * 512, 512)
                        d0 = fr(o_E + ep * 512, 512)
                        d1 = fr(o_bh + ep * 512, 512)
                        if rev:
                            oh, d0, d1 = oh[:, ::-1], d0[:, ::-1], d1[:, ::-1]
                        return e.tensor_tensor_scan(out=oh, data0=d0, data1=d1,
                                                    initial=init_ap,
                                                    op0=ALU.mult, op1=ALU.add)
                    b.op("vector", scan, reads=(f"E{ep}", f"bh{ep}", ikey),
                         writes=(f"h{ep}",))
                    ecol = 0 if rev else 511
                    if first:
                        dst, dkey = stcol(ST_M, lane), "stM"
                    else:
                        dst, dkey = stcol(st_bases[1 - st_idx[0]], lane), fkeys[1 - st_idx[0]]
                    def stx(e, ep=ep, dst=dst, ecol=ecol):
                        return e.tensor_copy(out=dst,
                                             in_=fr(o_h + ep * 512, 512)[:, ecol:ecol + 1])
                    b.op("vector", stx, reads=(f"h{ep}",), writes=(dkey,))
                    def q_(e, ep=ep, rp=rp):
                        return e.tensor_tensor(out=fb(o_q + ep * 256, 512),
                                               in0=fr(o_h + ep * 512, 512),
                                               in1=fr(o_Call + rp * 512, 512),
                                               op=ALU.mult)
                    b.op("vector", q_, reads=(f"h{ep}", f"Ca{rp}"), writes=(f"q{ep}",))
                    if DEBUG and ((not rev and c == 0) or (rev and c == NSC - 1)) \
                            and m == 0 and n == 0 and half == (1 if rev else 0):
                        b.dma(dbg["d_E"], fr(o_E + ep * 512, 512), reads=(f"E{ep}",))
                        b.dma(dbg["d_bh"], fr(o_bh + ep * 512, 512), reads=(f"bh{ep}",))
                        b.dma(dbg["d_h"], fr(o_h + ep * 512, 512), reads=(f"h{ep}",))
                    def nsum(e, m=m, n=n, ep=ep):
                        return e.matmul(ps(2 + m), id_b(),
                                        fb(o_q + ep * 256, 512),
                                        start=(n == 0), stop=(n == NS - 1))
                    b.op("tensor", nsum, reads=(f"q{ep}", "idb"), writes=(pk(2 + m),))
            for m in range(S):
                gp_p = m % 2
                def gpre(e, m=m, hs=hs, par=par, dir_=dir_, gp_p=gp_p):
                    return e.scalar_tensor_tensor(
                        out=fr(o_gp + gp_p * 512, 512),
                        in0=u_sl(par, m)[:, hs],
                        scalar=dvcol(dir_, m),
                        in1=ps(2 + m), op0=ALU.mult, op1=ALU.add)
                b.op("vector", gpre, reads=(f"u{par}_{m}", "dv", pk(2 + m)),
                     writes=(f"gp{gp_p}",))
                if not rev:
                    b.dma(sc_yf[c, m][:, hs], fr(o_gp + gp_p * 512, 512),
                          reads=(f"gp{gp_p}",), writes=("scyf",))
                    if DEBUG and c == 0 and m == 0:
                        b.dma(dbg["d_yf"][:, hs], fr(o_gp + gp_p * 512, 512),
                              reads=(f"gp{gp_p}",))
                else:
                    ksz = f"szr{gp_p}"
                    b.dma(fr(o_szr + gp_p * 512, 512), sc_sz[c, m][:, hs],
                          reads=("scsz", "revfence", ksz), writes=(ksz,))
                    if DEBUG and c == NSC - 1 and m == 0:
                        b.dma(dbg["d_yr"][:, hs], fr(o_gp + gp_p * 512, 512),
                              reads=(f"gp{gp_p}",))
                    def add1(e, m=m, hs=hs, gp_p=gp_p):
                        return e.tensor_tensor(out=fr(o_gp + gp_p * 512, 512),
                                               in0=fr(o_gp + gp_p * 512, 512),
                                               in1=fr(o_yf + m * NT, NT)[:, hs],
                                               op=ALU.add)
                    b.op("gpsimd", add1, reads=(f"yf{m}", f"gp{gp_p}"),
                         writes=(f"gp{gp_p}",))
                    def gmul(e, m=m, hs=hs, gp_p=gp_p):
                        return e.tensor_tensor(
                            out=fb(o_ymix + m * 512, NT)[:, hs],
                            in0=fr(o_gp + gp_p * 512, 512),
                            in1=fr(o_szr + gp_p * 512, 512), op=ALU.mult)
                    b.op("gpsimd", gmul, reads=(f"gp{gp_p}", ksz, "revfence"),
                         writes=(f"ym{m}",))
        st_idx[0] = 1 - st_idx[0]
        if rev:
            for dmt in range(8):
                pb = 2 + 2 * (dmt % 2)
                for half in range(2):
                    for s_ in range(S):
                        def op_(e, dmt=dmt, half=half, s_=s_, pb=pb):
                            return e.matmul(ps(pb + half), wout_t(s_, dmt),
                                            fb(o_ymix + s_ * 512, NT)[:, half * 512:(half + 1) * 512],
                                            start=(s_ == 0), stop=(s_ == S - 1))
                        b.op("tensor", op_, reads=("wout", f"ym{s_}"),
                             writes=(pk(pb + half),))
                kop = f"op{dmt % 2}"
                def opev(e, dmt=dmt, pb=pb):
                    return e.activation(out=fr(o_ops + (dmt % 2) * 1024, NT),
                                        in_=ps(pb, 1024), func=AF.Copy)
                b.op("scalar", opev, reads=(pk(pb), pk(pb + 1), "revfence"),
                     writes=(kop,))
                b.dma(op_in[c % 2, c // 2, dmt], fr(o_ops + (dmt % 2) * 1024, NT),
                      reads=(kop,), writes=("opw",))
                if DEBUG and c == NSC - 1 and dmt == 0:
                    b.dma(dbg["d_op"], fr(o_ops + (dmt % 2) * 1024, NT), reads=(kop,))

    emit_A_fwd(0)
    emit_dequant_tile(0)
    emit_dequant_tile(1)
    next_dq = 2
    for c in range(1, NSC):
        emit_A_fwd(c)
        for t in range(next_dq, min(next_dq + 5, 40)):
            emit_dequant_tile(t)
        next_dq = min(next_dq + 5, 40)
        emit_B(c - 1, 0)
    for t in range(next_dq, 40):
        emit_dequant_tile(t)
    emit_B(NSC - 1, 0)
    b.op("vector", lambda e: e.memset(fr(o_mu + 10, 2), 0.0),
         writes=("x0", "x1", "xl", "xlnT", "sz0", "sz1", "szr0", "szr1",
                 "revfence"))
    emit_A_rev(NSC - 1)
    for c in range(NSC - 2, -1, -1):
        emit_A_rev(c)
        emit_B(c + 1, 1)
    emit_B(0, 1)
    for j in range(2):
        def rs(e, j=j):
            return e.collective_compute(
                "ReduceScatter", ALU.add, replica_groups=GROUPS,
                ins=[op_in[j]], outs=[op_out[j]])
        b.op("gpsimd", rs, reads=("opw",), writes=(f"opo{j}",), sig_as="coll")

    fence_keys = tuple(f"u0_{m}" for m in range(S)) + tuple(f"u1_{m}" for m in range(S)) + \
        tuple(f"dl{m}" for m in range(S)) + tuple(f"du{m}" for m in range(S)) + \
        tuple(f"yf{m}" for m in range(S)) + tuple(f"ym{m}" for m in range(S)) + \
        tuple(f"xz{m}" for m in range(S)) + \
        ("xlnT", "xl", "x0", "x1", "sp0", "sp1", "xdb", "bctb",
         "fence", "Ba0", "Ba1", "Ca0", "Ca1", "E0", "E1", "bh0", "bh1",
         "h0", "h1", "q0", "q1", "gp0", "gp1", "sz0", "sz1", "op0", "op1",
         "szr0", "szr1", "xdsb", "dq0", "dq1", "dq2", "ub0", "ub1", "dtrb",
         "stF0", "stF1", "stR0", "stR1", "stM")
    b.op("vector", lambda e: e.memset(fr(o_mu + 12, 2), 0.0), writes=fence_keys)
    b.dma(fr(of_gb, DM), ln_g.partition_broadcast(128), reads=("fence",),
          writes=("sz0",))
    b.dma(fr(of_gb + DM, DM), ln_b.partition_broadcast(128), reads=("fence",),
          writes=("sz1",))
    b.dma(fr(of_fb, 32), fc1_b, reads=("fence",), writes=("bctb",))
    b.dma(fr(of_fb + 32, 8), fc2_b, reads=("fence",), writes=("bctb",))
    def hb2(e):
        return e.tensor_scalar(out=fr(of_fb + 40, 8), in0=fr(of_fb + 32, 8),
                               scalar1=0.5, scalar2=None, op0=ALU.mult)
    b.op("vector", hb2, reads=("bctb",), writes=("bctb",))

    for fch in range(4):
        j = fch // 2
        off = (fch % 2) * 512
        for dmt in range(8):
            b.dma(fr(of_opsb + dmt * 512, 512), op_out[j, dmt][:, off:off + 512],
                  reads=(f"opo{j}", "fence", "opsb"), writes=("opsb",))
        for tt in range(4):
            kx = f"x{tt % 2}"
            b.dma(fr(o_x + (tt % 2) * DM, DM),
                  xq_in[fch * 512 + tt * 128:fch * 512 + (tt + 1) * 128, :],
                  reads=("fence",), writes=(kx,))
            for dmt in range(8):
                bnk = dmt % 2
                def tr(e, dmt=dmt, tt=tt, bnk=bnk):
                    return e.transpose(ps(bnk, 128),
                                       fr(of_opsb + dmt * 512, 512)[:, tt * 128:(tt + 1) * 128],
                                       fr(o_idf, 128))
                b.op("tensor", tr, reads=("opsb", "idf"), writes=(pk(bnk),))
                def ev(e, dmt=dmt, bnk=bnk):
                    return e.activation(out=fr(o_xln, DM)[:, dmt * 128:(dmt + 1) * 128],
                                        in_=ps(bnk, 128), func=AF.Copy)
                b.op("scalar", ev, reads=(pk(bnk),), writes=("xl",))
            def addx(e, tt=tt):
                return e.tensor_tensor(out=fr(of_x1 + tt * DM, DM),
                                       in0=fr(o_xln, DM),
                                       in1=fr(o_x + (tt % 2) * DM, DM),
                                       op=ALU.add)
            b.op("vector", addx, reads=("xl", kx), writes=("fx1",))
            if DEBUG and fch == 0 and tt == 0:
                b.dma(dbg["d_x1"], fr(of_x1 + tt * DM, DM), reads=("fx1",))
        for tt in range(4):
            ln_stats(fr(of_x1 + tt * DM, DM), "fx1")
            def n1(e, tt=tt):
                return e.tensor_scalar(out=fr(of_xl2 + tt * DM, DM),
                                       in0=fr(of_x1 + tt * DM, DM),
                                       scalar1=AR[:, o_mu + 6:o_mu + 7],
                                       scalar2=AR[:, o_mu + 7:o_mu + 8],
                                       op0=ALU.mult, op1=ALU.add)
            b.op("vector", n1, reads=("fx1", "mu"), writes=("fxl",))
            ln_stats(fr(of_xl2 + tt * DM, DM), "fxl")
            def n2(e, tt=tt):
                return e.tensor_scalar(out=fr(o_sp, DM), in0=fr(of_xl2 + tt * DM, DM),
                                       scalar1=AR[:, o_mu + 6:o_mu + 7],
                                       scalar2=AR[:, o_mu + 7:o_mu + 8],
                                       op0=ALU.mult, op1=ALU.add)
            b.op("vector", n2, reads=("fxl", "mu"), writes=("sp0",))
            def gmul(e):
                return e.tensor_tensor(out=fr(o_sp + 1024, DM), in0=fr(o_sp, DM),
                                       in1=fr(of_gb, DM), op=ALU.mult)
            b.op("vector", gmul, reads=("sp0", "sz0"), writes=("sp1",))
            def badd(e):
                return e.tensor_tensor(out=fr(o_sp, DM), in0=fr(o_sp + 1024, DM),
                                       in1=fr(of_gb + DM, DM), op=ALU.add)
            b.op("vector", badd, reads=("sp1", "sz1"), writes=("sp0",))
            for k in range(8):
                bnk = k % 2
                def tr(e, k=k, bnk=bnk):
                    return e.transpose(ps(bnk, 128),
                                       fr(o_sp, DM)[:, k * 128:(k + 1) * 128],
                                       fr(o_idf, 128))
                b.op("tensor", tr, reads=("sp0", "idf"), writes=(pk(bnk),))
                def ev(e, tt=tt, k=k, bnk=bnk):
                    return e.activation(
                        out=fb(of_h2T, 8 * 512)[:, k * 512 + tt * 128:k * 512 + (tt + 1) * 128],
                        in_=ps(bnk, 128), func=AF.Copy)
                b.op("scalar", ev, reads=(pk(bnk),), writes=("h2T",))
        h2T_k = lambda k: fb(of_h2T, 8 * 512)[:, k * 512:(k + 1) * 512]
        for Ft in range(32):
            kw = f"w1b{Ft % 2}"
            b.dma(fb(of_w1b + (Ft % 2) * 512, 8 * 128).rearrange("p (k m) -> p k m", k=8),
                  sc_w1[:, :, Ft * 128:(Ft + 1) * 128],
                  reads=("wsc", "fence", kw), writes=(kw,))
            pb = 2 + (Ft % 2)
            for k in range(8):
                def mm_(e, Ft=Ft, k=k, pb=pb):
                    wt = fb(of_w1b + (Ft % 2) * 512, 8 * 128)[:, k * 128:(k + 1) * 128]
                    return e.matmul(ps(pb), wt, h2T_k(k), start=(k == 0), stop=(k == 7))
                b.op("tensor", mm_, reads=(kw, "h2T"), writes=(pk(pb),))
            def sl(e, Ft=Ft, pb=pb):
                return e.activation(out=fb(of_y1, 32 * 512)[:, Ft * 512:(Ft + 1) * 512],
                                    in_=ps(pb), func=AF.Silu,
                                    bias=AR[:, of_fb + Ft:of_fb + Ft + 1])
            b.op("scalar", sl, reads=("bctb", "fence", pk(pb)), writes=("y1",))
        for dm in range(8):
            kw = f"w2b{dm % 2}"
            b.dma(fb(of_w2b + (dm % 2) * 2048, 32 * 128).rearrange("p (k m) -> p k m", k=32),
                  sc_w2[:, :, dm * 128:(dm + 1) * 128],
                  reads=("wsc", "fence", kw), writes=(kw,))
            pb = 4 + (dm % 2)
            for k in range(32):
                def mm_(e, dm=dm, k=k, pb=pb):
                    wt = fb(of_w2b + (dm % 2) * 2048, 32 * 128)[:, k * 128:(k + 1) * 128]
                    y1k = fb(of_y1, 32 * 512)[:, k * 512:(k + 1) * 512]
                    return e.matmul(ps(pb), wt, y1k, start=(k == 0), stop=(k == 31))
                b.op("tensor", mm_, reads=(kw, "y1"), writes=(pk(pb),))
            def hv(e, dm=dm, pb=pb):
                return e.tensor_scalar(out=fr(z2sl(dm), 512), in0=ps(pb),
                                       scalar1=0.5,
                                       scalar2=AR[:, of_fb + 40 + dm:of_fb + 41 + dm],
                                       op0=ALU.mult, op1=ALU.add)
            b.op("vector", hv, reads=("bctb", "fence", pk(pb)), writes=("z2",))
        for tt in range(4):
            for dm in range(8):
                bnk = dm % 2
                def tr(e, dm=dm, tt=tt, bnk=bnk):
                    return e.transpose(ps(bnk, 128),
                                       fr(z2sl(dm), 512)[:, tt * 128:(tt + 1) * 128],
                                       fr(o_idf, 128))
                b.op("tensor", tr, reads=("z2", "idf"), writes=(pk(bnk),))
                def zev(e, dm=dm, bnk=bnk):
                    return e.activation(out=fr(o_sp + dm * 128, 128),
                                        in_=ps(bnk, 128), func=AF.Copy)
                b.op("scalar", zev, reads=(pk(bnk),), writes=("sp0",))
            def f1(e, tt=tt):
                return e.tensor_tensor(out=fr(o_sp + 1024, DM),
                                       in0=fr(of_x1 + tt * DM, DM),
                                       in1=fr(of_xl2 + tt * DM, DM), op=ALU.add)
            b.op("vector", f1, reads=("fx1", "fxl"), writes=("sp1",))
            def f2(e, tt=tt):
                return e.tensor_tensor(out=fr(o_xln, DM), in0=fr(o_sp + 1024, DM),
                                       in1=fr(o_sp, DM), op=ALU.add)
            b.op("vector", f2, reads=("sp0", "sp1"), writes=("xl",))
            trow = fch * 512 + tt * 128
            b.dma(out[trow:trow + 128, :], fr(o_xln, DM), reads=("xl",),
                  writes=("outw",))

    b.emit("sync", lambda e: e.nop(),
           waits=[("sync", b.cnt["sync"]), ("gpsimd", b.cnt["gpsimd"]),
                  ("coll", b.cnt.get("coll", 0))], sig=False)

    with nc.Block() as block:
        b.replay(block, sems)
    es.close()
    return nc


def kernel(**inputs):
    import ml_dtypes
    inputs = {k: np.asarray(v) for k, v in inputs.items()}
    nc = build_program()

    selb = np.zeros((32, 32 * 128), np.float32)
    for n in range(32):
        selb[n, n * 128:(n + 1) * 128] = 1.0
    selb = selb.astype(ml_dtypes.bfloat16)

    win = inputs["L0_attn_in_proj_w"]
    wop = inputs["L0_attn_out_proj_w"]
    xpf = inputs["L0_attn_x_proj_w"]
    xpr = inputs["L0_attn_x_proj_w_b"]
    dtf = inputs["L0_attn_dt_proj_w"]
    dtr = inputs["L0_attn_dt_proj_w_b"]

    def chsl(a, q, axis=0):
        sl = [slice(None)] * a.ndim
        sl[axis] = slice(512 * q, 512 * (q + 1))
        return np.ascontiguousarray(a[tuple(sl)])

    def pmaj(a, q):
        a = chsl(a, q)
        if a.ndim == 1:
            return np.ascontiguousarray(a.reshape(4, 128).T)
        k = a.shape[1]
        return np.ascontiguousarray(
            a.reshape(4, 128, k).transpose(1, 0, 2).reshape(128, 4 * k))

    shared = {
        "selb": selb,
        "id128f": np.eye(128, dtype=np.float32),
        "fc1_q": np.ascontiguousarray(inputs["L0_ffn_fc1_q"]),
        "fc1_sc": np.ascontiguousarray(inputs["L0_ffn_fc1_scales"]),
        "fc1_b": np.ascontiguousarray(inputs["L0_ffn_fc1_bias"].reshape(32, 128).T),
        "fc2_q": np.ascontiguousarray(inputs["L0_ffn_fc2_q"]),
        "fc2_sc": np.ascontiguousarray(inputs["L0_ffn_fc2_scales"]),
        "fc2_b": np.ascontiguousarray(inputs["L0_ffn_fc2_bias"].reshape(8, 128).T),
        "ln_g": np.ascontiguousarray(inputs["L0_ffn_ln_gamma"]),
        "ln_b": np.ascontiguousarray(inputs["L0_ffn_ln_beta"]),
    }
    in_maps = []
    for cc in range(8):
        bq, q = divmod(cc, 4)
        x = np.ascontiguousarray(inputs["L0_attn_hidden_states"][bq])
        m = dict(shared)
        m.update({
            "x_in": x,
            "xq_in": np.ascontiguousarray(x[OWN * q:OWN * (q + 1)]),
            "winx": np.ascontiguousarray(chsl(win[:DI], q).T),
            "winz": np.ascontiguousarray(chsl(win[DI:], q).T),
            "wout": np.ascontiguousarray(chsl(wop.T, q)),
            "xpw": np.stack([chsl(np.ascontiguousarray(xpf.T), q),
                             chsl(np.ascontiguousarray(xpr.T), q)], 0),
            "dtw": np.stack([np.ascontiguousarray(chsl(dtf, q).T),
                             np.ascontiguousarray(chsl(dtr, q).T)], 0),
            "convw": np.concatenate([pmaj(inputs["L0_attn_conv_w"][:, 0, :], q),
                                     pmaj(inputs["L0_attn_conv_w_b"][:, 0, :], q)], 1),
            "convb": np.concatenate([pmaj(inputs["L0_attn_conv_b"], q),
                                     pmaj(inputs["L0_attn_conv_b_b"], q)], 1),
            "dtb": np.concatenate([pmaj(inputs["L0_attn_dt_proj_b"], q),
                                   pmaj(inputs["L0_attn_dt_proj_b_b"], q)], 1),
            "alog": np.concatenate([pmaj(inputs["L0_attn_A_log"], q),
                                    pmaj(inputs["L0_attn_A_b_log"], q)], 1),
            "dvec": np.concatenate([pmaj(inputs["L0_attn_D"], q),
                                    pmaj(inputs["L0_attn_D_b"], q)], 1),
        })
        in_maps.append(m)

    res = run_bass_kernel_spmd(nc, in_maps, core_ids=list(range(8)))
    outp = np.empty((2, L, DM), np.float32)
    for cc in range(8):
        bq, q = divmod(cc, 4)
        outp[bq, OWN * q:OWN * (q + 1)] = np.asarray(res.results[cc]["out"])
    if DEBUG:
        kernel.last_results = res
    return outp


# version 12
# speedup vs baseline: 1.3907x; 1.0042x over previous
import os
import sys
import numpy as np

sys.path.insert(0, "/opt/trn_rl_repo")
import concourse.bass as bass
import concourse.mybir as mybir
from concourse.bass_utils import run_bass_kernel_spmd

F32 = mybir.dt.float32
BF16 = mybir.dt.bfloat16
I32 = mybir.dt.int32
AF = mybir.ActivationFunctionType
ALU = mybir.AluOpType

DM = 1024
DI = 2048
CHQ = 512
S = 4
NS = 16
RK = 64
L = 8192
NT = 1024
NSC = 8
OWN = 2048
EPS = 1e-5
DEBUG = bool(int(os.environ.get("BASSMK_DEBUG", "0")))

ENGINES = ("sync", "tensor", "scalar", "vector", "gpsimd")
GROUPS = [[0, 1, 2, 3], [4, 5, 6, 7]]


class Buf:
    def __init__(self):
        self.w = None
        self.readers = {}

    def wr(self):
        return [(e, v) for e, v in self.readers.items()] + \
            ([self.w] if self.w is not None else [])

    def setw(self, tok):
        self.w = tok
        self.readers = {}

    def rd(self):
        return [self.w] if self.w is not None else []

    def setr(self, tok):
        if tok is not None:
            e, v = tok
            if self.readers.get(e, 0) < v:
                self.readers[e] = v


class Bld:
    def __init__(self):
        self.ops = {e: [] for e in ENGINES}
        self.cnt = {e: 0 for e in ENGINES}
        self.bufs = {}

    def B(self, key):
        if key not in self.bufs:
            self.bufs[key] = Buf()
        return self.bufs[key]

    def emit(self, eng, fn, waits=(), sig=True, sig_as=None):
        ws = []
        for t in waits:
            if t is None:
                continue
            pe, v = t
            if pe == eng and sig_as is None:
                if pe == "tensor":
                    continue
                if pe != "sync" and self.cnt[eng] + 1 - v > 4:
                    continue
            ws.append((pe, v))
        if sig:
            sname = sig_as or eng
            self.cnt.setdefault(sname, 0)
            self.cnt[sname] += 16 if sname == "sync" else 1
            tok = (sname, self.cnt[sname])
        else:
            sname = None
            tok = None
        self.ops[eng].append((fn, ws, sname))
        return tok

    def op(self, eng, fn, reads=(), writes=(), sig_as=None):
        waits = []
        for k in reads:
            waits += self.B(k).rd()
        for k in writes:
            waits += self.B(k).wr()
        tok = self.emit(eng, fn, waits=waits, sig_as=sig_as)
        for k in reads:
            self.B(k).setr(tok)
        for k in writes:
            self.B(k).setw(tok)
        return tok

    def dma(self, out, in_, reads=(), writes=()):
        def fn(e):
            return e.dma_start(out=out, in_=in_)
        return self.op("sync", fn, reads=reads, writes=writes)

    def replay(self, block, sems):
        for eng_name in ENGINES:
            ops = self.ops[eng_name]
            if not ops:
                continue

            def make_body(ename, olist):
                def body(eng):
                    seen = {}
                    cum = 0
                    for fn, ws, sname in olist:
                        for pe, v in ws:
                            if seen.get(pe, 0) >= v:
                                continue
                            seen[pe] = v
                            eng.wait_ge(sems[pe], v)
                        inst = fn(eng)
                        if sname:
                            inst.then_inc(sems[sname], 16 if sname == "sync" else 1)
                        if sname == "sync":
                            cum += 16
                            eng.wait_ge(sems["sync"], cum)
                            seen["sync"] = cum
                return body
            getattr(block, eng_name)(make_body(eng_name, ops))


def build_program():
    nc = bass.Bass("TRN2", target_bir_lowering=False)

    def din(name, shape, dt=F32):
        return nc.dram_tensor(name, shape, dt, kind="ExternalInput").ap()

    x_in = din("x_in", [L, DM])
    xq_in = din("xq_in", [OWN, DM])
    winx = din("winx", [DM, CHQ])
    winz = din("winz", [DM, CHQ])
    wout = din("wout", [CHQ, DM])
    xpw = din("xpw", [2, CHQ, 96])
    dtw = din("dtw", [2, RK, CHQ])
    convw = din("convw", [128, 2 * S * 4])
    convb = din("convb", [128, 2 * S])
    dtb = din("dtb", [128, 2 * S])
    alog = din("alog", [128, 2 * S * NS])
    dvec = din("dvec", [128, 2 * S])
    selb = din("selb", [32, 32 * 128], BF16)
    id128f = din("id128f", [128, 128])
    fc1_q = din("fc1_q", [DM, 4 * DM], I32)
    fc1_sc = din("fc1_sc", [8, 4 * DM])
    fc1_b = din("fc1_b", [128, 32])
    fc2_q = din("fc2_q", [4 * DM, DM], I32)
    fc2_sc = din("fc2_sc", [32, DM])
    fc2_b = din("fc2_b", [128, 8])
    ln_g = din("ln_g", [DM])
    ln_b = din("ln_b", [DM])

    out = nc.dram_tensor("out", [OWN, DM], F32, kind="ExternalOutput").ap()

    def dscr(name, shape, dt=F32):
        return nc.dram_tensor(name, shape, dt, kind="Internal").ap()

    sc_xz = dscr("sc_xz", [NSC, S, 128, NT])
    sc_sz = dscr("sc_sz", [NSC, S, 128, NT])
    sc_yf = dscr("sc_yf", [NSC, S, 128, NT])
    xd_in = dscr("xd_in", [16, 96, NT])
    xd_out = dscr("xd_out", [16, 96, NT])
    op_in = dscr("op_in", [2, 4, 8, 128, NT])
    op_out = dscr("op_out", [2, 8, 128, NT])
    sc_w1 = dscr("sc_w1", [128, 8, 4 * DM], BF16)
    sc_w2 = dscr("sc_w2", [128, 32, DM], BF16)

    dbg = {}
    if DEBUG:
        for nm, shape in [("d_xln", [128, DM]), ("d_xzf", [128, NT]),
                          ("d_uf", [128, NT]), ("d_xdbl", [96, NT]),
                          ("d_delta", [128, NT]), ("d_E", [128, 512]),
                          ("d_bh", [128, 512]), ("d_h", [128, 512]),
                          ("d_yf", [128, NT]), ("d_ur", [128, NT]),
                          ("d_xdblr", [96, NT]),
                          ("d_yr", [128, NT]), ("d_op", [128, NT]),
                          ("d_x1", [128, DM]), ("d_sz", [128, NT]),
                          ("d_dtm", [128, NT]), ("d_Ball", [128, 512]),
                          ("d_dtrb", [128, NT])]:
            dbg[nm] = nc.dram_tensor(nm, shape, F32, kind="ExternalOutput").ap()

    ACOLS = 53180
    _off = [0]

    def A(n):
        o = _off[0]
        _off[0] += n
        assert _off[0] <= ACOLS, f"arena overflow at {_off[0]}"
        return o

    o_winx = A(2048)
    o_winz = A(2048)
    o_wout = A(2048)
    o_xpw = A(384)
    o_dtw = A(512)
    o_selb = A(2048)
    o_idf = A(128)
    o_idb = A(64)
    o_cw = A(32)
    o_cb = A(8)
    o_dtb = A(8)
    o_dv = A(8)
    o_ac = A(128)
    o_mu = A(16)
    o_st = A(320)
    o_x = A(2048)
    o_xln = A(1024)
    o_xlnT = A(4096)
    o_xzp = A(4112)
    o_ub = A(1024)
    o_sz = A(2048)
    o_u = A(8192)
    o_delta = A(4096)
    o_du = A(4096)
    o_xdb = A(1024)
    o_bctb = A(512)
    o_Ball = A(1024)
    o_Call = A(1024)
    o_E = A(1024)
    o_bh = A(1024)
    o_h = A(1024)
    o_q = A(512)
    o_sp = A(2048)
    o_gp = A(1024)
    o_dq = A(1536)
    o_dtrb = A(512)
    assert _off[0] <= ACOLS, _off[0]

    o_yf = o_xlnT
    o_szr = o_x
    o_ymix = o_x + 1024
    o_ops = o_sz
    of_x1 = o_u
    of_xl2 = o_u + 4096
    of_h2T = o_delta
    of_w1b = o_delta + 2048
    of_w2b = o_delta + 3072
    of_y1 = o_xlnT
    of_opsb = o_Ball
    of_gb = o_sz
    of_fb = o_bctb

    def z2sl(dm):
        offs = [o_gp, o_gp + 512, o_dq, o_dq + 512, o_dq + 1024,
                o_ub, o_ub + 512, o_xdb]
        return offs[dm]

    b = Bld()

    import contextlib
    es = contextlib.ExitStack()
    AR = es.enter_context(nc.sbuf_tensor("AR", [128, ACOLS], F32))
    PSF = es.enter_context(nc.psum_tensor("PSF", [128, 4096], F32))
    sems = {e: es.enter_context(nc.semaphore("sem_" + e))
            for e in ENGINES + ("coll",)}

    def fr(off, n):
        return AR[:, off:off + n]

    def fb(off, n_bf16):
        return AR[:, off:off + (n_bf16 + 1) // 2].bitcast(BF16)[:, 0:n_bf16]

    def ps(bank, n=512):
        return PSF[:, bank * 512:bank * 512 + n]

    pk = lambda bank: f"ps{bank}"

    b.op("vector", lambda e: e.memset(AR[:, o_mu + 8:o_mu + 9], EPS), writes=("eps",))
    b.op("vector", lambda e: e.memset(fr(o_st, 320), 0.0),
         writes=("stF0", "stF1", "stR0", "stR1", "stM"))

    b.dma(fb(o_selb, 4096)[0:32, :], selb, writes=("selb",))
    b.dma(fr(o_idf, 128), id128f, writes=("idf",))
    b.dma(fr(o_cw, 32), convw, writes=("cw",))
    b.dma(fr(o_cb, 8), convb, writes=("cb",))
    b.dma(fr(o_dtb, 8), dtb, writes=("dtbk",))
    b.dma(fr(o_dv, 8), dvec, writes=("dv",))
    b.dma(fr(o_sp, 128), alog, writes=("sp0",))
    def mkA(e):
        return e.activation(out=fr(o_ac, 128), in_=fr(o_sp, 128), func=AF.Exp)
    b.op("scalar", mkA, reads=("sp0",), writes=("ac",))
    def negA(e):
        return e.tensor_scalar(out=fr(o_ac, 128), in0=fr(o_ac, 128),
                               scalar1=-1.0, scalar2=None, op0=ALU.mult)
    b.op("vector", negA, reads=(), writes=("ac",))
    def mkidb(e):
        return e.activation(out=fb(o_idb, 128), in_=fr(o_idf, 128), func=AF.Copy)
    b.op("scalar", mkidb, reads=("idf",), writes=("idb",))

    def stage_resident(srcs, ncols, dst_off, key):
        half = ncols // 2
        for hh in range(2):
            b.dma(fr(o_u, half), srcs[hh], writes=("u0_0", "u0_1", "u0_2", "u0_3"))
            def cast(e, hh=hh, half=half, dst_off=dst_off, ncols=ncols):
                return e.activation(out=fb(dst_off, ncols)[:, hh * half:(hh + 1) * half],
                                    in_=fr(o_u, half), func=AF.Copy)
            b.op("scalar", cast, reads=("u0_0",), writes=(key,))

    winx_v = winx.rearrange("(k p) m -> p k m", p=128)
    winz_v = winz.rearrange("(k p) m -> p k m", p=128)
    wout_v = wout.rearrange("(s p) m -> p s m", p=128)
    stage_resident([winx_v[:, 0:4, :], winx_v[:, 4:8, :]], 4096, o_winx, "winx")
    stage_resident([winz_v[:, 0:4, :], winz_v[:, 4:8, :]], 4096, o_winz, "winz")
    stage_resident([wout_v[:, 0:2, :], wout_v[:, 2:4, :]], 4096, o_wout, "wout")
    b.dma(fr(o_u, 768), xpw.rearrange("d (s p) r -> p d s r", p=128),
          writes=("u0_0", "u0_1", "u0_2", "u0_3"))
    def cxp(e):
        return e.activation(out=fb(o_xpw, 768), in_=fr(o_u, 768), func=AF.Copy)
    b.op("scalar", cxp, reads=("u0_0",), writes=("xpw",))
    b.dma(fr(o_u, 1024)[0:64, :], dtw.rearrange("d r m -> r d m"),
          writes=("u0_0", "u0_1", "u0_2", "u0_3"))
    def cdt(e):
        return e.activation(out=fb(o_dtw, 1024)[0:64, :], in_=fr(o_u, 1024)[0:64, :],
                            func=AF.Copy)
    b.op("scalar", cdt, reads=("u0_0",), writes=("dtw",))

    winx_t = lambda k, m: fb(o_winx, 8 * CHQ)[:, k * CHQ + m * 128:k * CHQ + (m + 1) * 128]
    winz_t = lambda k, m: fb(o_winz, 8 * CHQ)[:, k * CHQ + m * 128:k * CHQ + (m + 1) * 128]
    wout_t = lambda s, d: fb(o_wout, S * DM)[:, s * DM + d * 128:s * DM + (d + 1) * 128]
    xpw_t = lambda dir_, m: fb(o_xpw, 768)[:, dir_ * 384 + m * 96:dir_ * 384 + (m + 1) * 96]
    dtw_t = lambda dir_, m: fb(o_dtw, 1024)[0:64, dir_ * 512 + m * 128:dir_ * 512 + (m + 1) * 128]
    selb_t = lambda j: fb(o_selb, 4096)[0:32, j * 128:(j + 1) * 128]
    id_b = lambda: fb(o_idb, 128)
    wcol = lambda dir_, m, j: AR[:, o_cw + dir_ * 16 + m * 4 + j:o_cw + dir_ * 16 + m * 4 + j + 1]
    cbcol = lambda dir_, m: AR[:, o_cb + dir_ * 4 + m:o_cb + dir_ * 4 + m + 1]
    dtbcol = lambda dir_, m: AR[:, o_dtb + dir_ * 4 + m:o_dtb + dir_ * 4 + m + 1]
    dvcol = lambda dir_, m: AR[:, o_dv + dir_ * 4 + m:o_dv + dir_ * 4 + m + 1]
    acol = lambda dir_, m, n: AR[:, o_ac + dir_ * 64 + m * 16 + n:o_ac + dir_ * 64 + m * 16 + n + 1]
    stcol = lambda base, lane: AR[:, o_st + base + lane:o_st + base + lane + 1]
    ST_F = (0, 64)
    ST_R = (128, 192)
    ST_M = 256
    xzp_sl = lambda m: AR[:, o_xzp + m * 1028:o_xzp + m * 1028 + 1028]
    u_sl = lambda par, m: fr(o_u + par * 4096 + m * NT, NT)
    xlnT_k = lambda k: fb(o_xlnT, 8 * NT)[:, k * NT:(k + 1) * NT]

    def ln_stats(src_ap, src_key):
        def s1(e):
            return e.activation(out=fr(o_sp, DM), in_=src_ap, func=AF.Copy,
                                accum_out=AR[:, o_mu:o_mu + 1])
        b.op("scalar", s1, reads=(src_key,), writes=("sp0", "mu"))
        def s2(e):
            return e.activation(out=fr(o_sp, DM), in_=src_ap, func=AF.Square,
                                accum_out=AR[:, o_mu + 1:o_mu + 2])
        b.op("scalar", s2, reads=(src_key,), writes=("sp0", "mu"))
        def mstat(e):
            return e.tensor_scalar(out=AR[:, o_mu + 2:o_mu + 4],
                                   in0=AR[:, o_mu:o_mu + 2], scalar1=1.0 / DM,
                                   scalar2=None, op0=ALU.mult)
        b.op("vector", mstat, reads=(), writes=("mu",))
        def musq(e):
            return e.activation(out=AR[:, o_mu + 9:o_mu + 10],
                                in_=AR[:, o_mu + 2:o_mu + 3], func=AF.Square)
        b.op("scalar", musq, reads=("mu",), writes=("mu",))
        def nvar(e):
            return e.tensor_tensor(out=AR[:, o_mu + 4:o_mu + 5],
                                   in0=AR[:, o_mu + 9:o_mu + 10],
                                   in1=AR[:, o_mu + 3:o_mu + 4], op=ALU.subtract)
        b.op("vector", nvar, reads=(), writes=("mu",))
        def lnv(e):
            return e.activation(out=AR[:, o_mu + 5:o_mu + 6],
                                in_=AR[:, o_mu + 4:o_mu + 5], func=AF.Ln,
                                scale=-1.0, bias=AR[:, o_mu + 8:o_mu + 9])
        b.op("scalar", lnv, reads=("mu", "eps"), writes=("mu",))
        def rstd(e):
            return e.activation(out=AR[:, o_mu + 6:o_mu + 7],
                                in_=AR[:, o_mu + 5:o_mu + 6], func=AF.Exp,
                                scale=-0.5)
        b.op("scalar", rstd, reads=(), writes=("mu",))
        def nb(e):
            return e.tensor_scalar(out=AR[:, o_mu + 7:o_mu + 8],
                                   in0=AR[:, o_mu + 2:o_mu + 3],
                                   scalar1=AR[:, o_mu + 6:o_mu + 7],
                                   scalar2=-1.0, op0=ALU.mult, op1=ALU.mult)
        b.op("vector", nb, reads=(), writes=("mu",))

    def emit_conv(m, dir_):
        taps = [0, 1, 2, 3] if dir_ == 0 else [3, 2, 1, 0]
        kxz = f"xz{m}"
        def c0(e, m=m, dir_=dir_, taps=taps):
            return e.tensor_scalar(out=fr(o_sp, NT), in0=xzp_sl(m)[:, 0:NT],
                                   scalar1=wcol(dir_, m, taps[0]),
                                   scalar2=cbcol(dir_, m),
                                   op0=ALU.mult, op1=ALU.add)
        b.op("vector", c0, reads=(kxz, "cw", "cb"), writes=("sp0",))
        srcdst = [("sp0", "sp1"), ("sp1", "sp0"), ("sp0", "sp1")]
        offd = [o_sp + 1024, o_sp, o_sp + 1024]
        offs_ = [o_sp, o_sp + 1024, o_sp]
        for j in range(1, 4):
            def cj(e, m=m, j=j, dir_=dir_, taps=taps):
                return e.scalar_tensor_tensor(out=fr(offd[j - 1], NT),
                                              in0=xzp_sl(m)[:, j:NT + j],
                                              scalar=wcol(dir_, m, taps[j]),
                                              in1=fr(offs_[j - 1], NT),
                                              op0=ALU.mult, op1=ALU.add)
            b.op("vector", cj, reads=(kxz, "cw", srcdst[j - 1][0]),
                 writes=(srcdst[j - 1][1],))
        return "sp1"

    def emit_u(m, par, kcv):
        ku = f"u{par}_{m}"
        def silu(e, m=m, par=par):
            return e.activation(out=u_sl(par, m), in_=fr(o_sp + 1024, NT), func=AF.Silu)
        b.op("scalar", silu, reads=(kcv,), writes=(ku,))
        kub = f"ub{m % 2}"
        def silub(e, m=m):
            return e.activation(out=fb(o_ub + (m % 2) * 512, NT),
                                in_=fr(o_sp + 1024, NT), func=AF.Silu)
        b.op("scalar", silub, reads=(kcv,), writes=(kub,))
        return ku, kub

    def emit_xp(m, dir_, kub):
        for half in range(2):
            def xp_(e, m=m, half=half, dir_=dir_):
                return e.matmul(ps(half)[0:96, :], xpw_t(dir_, m),
                                fb(o_ub + (m % 2) * 512, NT)[:, half * 512:(half + 1) * 512],
                                start=(m == 0), stop=(m == S - 1))
            b.op("tensor", xp_, reads=("xpw", kub), writes=(pk(half),))

    def emit_xp_finish(c, dir_):
        slot = dir_ * 8 + c
        def xpev(e):
            return e.activation(out=fr(o_ub, 1024)[0:96, :], in_=ps(0, 1024)[0:96, :],
                                func=AF.Copy)
        b.op("scalar", xpev, reads=(pk(0), pk(1)), writes=("ub0", "ub1", "xdsb"))
        b.dma(xd_in[slot], fr(o_ub, 1024)[0:96, :],
              reads=("xdsb", "ub0", "ub1"), writes=("xdw",))
        def coll(e, slot=slot):
            return e.collective_compute(
                "AllReduce", ALU.add, replica_groups=GROUPS,
                ins=[xd_in[slot]], outs=[xd_out[slot]])
        b.op("gpsimd", coll, reads=("xdw",), writes=(f"xdo{slot}",), sig_as="coll")

    def emit_dequant_tile(tile):
        if tile < 8:
            k, W, qd, scd, dst = tile, 4 * DM, fc1_q, fc1_sc, sc_w1
        else:
            k, W, qd, scd, dst = tile - 8, DM, fc2_q, fc2_sc, sc_w2
        for cc in range(W // 512):
            b.dma(AR[:, o_dq:o_dq + 512].bitcast(I32),
                  qd[k * 128:(k + 1) * 128, cc * 512:(cc + 1) * 512],
                  writes=("dq0",))
            b.dma(fr(o_dq + 512, 512),
                  scd[k, cc * 512:(cc + 1) * 512].partition_broadcast(128),
                  writes=("dq1",))
            def cvt(e):
                return e.activation(out=fr(o_dq + 1024, 512),
                                    in_=AR[:, o_dq:o_dq + 512].bitcast(I32),
                                    func=AF.Copy, bias=-128.0)
            b.op("scalar", cvt, reads=("dq0",), writes=("dq2",))
            def mul(e):
                return e.tensor_tensor(out=fb(o_dq, 512), in0=fr(o_dq + 1024, 512),
                                       in1=fr(o_dq + 512, 512), op=ALU.mult)
            b.op("gpsimd", mul, reads=("dq1", "dq2"), writes=("dq0",))
            b.dma(dst[:, k, cc * 512:(cc + 1) * 512], fb(o_dq, 512),
                  reads=("dq0",), writes=("wsc",))

    def emit_A_fwd(c):
        par = c % 2
        for tt in range(8):
            kx = f"x{tt % 2}"
            b.dma(fr(o_x + (tt % 2) * DM, DM),
                  x_in[c * NT + tt * 128:c * NT + (tt + 1) * 128, :], writes=(kx,))
            xa = fr(o_x + (tt % 2) * DM, DM)
            ln_stats(xa, kx)
            def norm(e, xa=xa):
                return e.tensor_scalar(out=fr(o_xln, DM), in0=xa,
                                       scalar1=AR[:, o_mu + 6:o_mu + 7],
                                       scalar2=AR[:, o_mu + 7:o_mu + 8],
                                       op0=ALU.mult, op1=ALU.add)
            b.op("vector", norm, reads=("mu", kx), writes=("xl",))
            if DEBUG and c == 0 and tt == 0:
                b.dma(dbg["d_xln"], fr(o_xln, DM), reads=("xl",))
            for k in range(8):
                bnk = k % 2
                def tr(e, k=k, bnk=bnk):
                    return e.transpose(ps(bnk, 128),
                                       fr(o_xln, DM)[:, k * 128:(k + 1) * 128],
                                       fr(o_idf, 128))
                b.op("tensor", tr, reads=("xl", "idf"), writes=(pk(bnk),))
                def evac(e, tt=tt, k=k, bnk=bnk):
                    return e.activation(
                        out=xlnT_k(k)[:, tt * 128:(tt + 1) * 128],
                        in_=ps(bnk, 128), func=AF.Copy)
                b.op("scalar", evac, reads=(pk(bnk),), writes=("xlnT",))
        for m in range(S):
            pb = 2 + 2 * (m % 2)
            for half in range(2):
                for k in range(8):
                    def mm_(e, k=k, m=m, half=half, pb=pb):
                        return e.matmul(ps(pb + half), winx_t(k, m),
                                        xlnT_k(k)[:, half * 512:(half + 1) * 512],
                                        start=(k == 0), stop=(k == 7))
                    b.op("tensor", mm_, reads=("winx", "xlnT"),
                         writes=(pk(pb + half),))
            kxz = f"xz{m}"
            if c == 0:
                b.op("vector", lambda e, m=m: e.memset(xzp_sl(m)[:, 0:3], 0.0),
                     writes=(kxz,))
            else:
                def tl(e, m=m):
                    return e.tensor_copy(out=xzp_sl(m)[:, 0:3],
                                         in_=xzp_sl(m)[:, NT:NT + 3])
                b.op("vector", tl, reads=(), writes=(kxz,))
            def xev(e, m=m, pb=pb):
                return e.activation(out=xzp_sl(m)[:, 3:NT + 3], in_=ps(pb, 1024),
                                    func=AF.Copy)
            b.op("scalar", xev, reads=(pk(pb), pk(pb + 1)), writes=(kxz,))
            b.dma(sc_xz[c, m], xzp_sl(m)[:, 3:NT + 3], reads=(kxz,), writes=("scxz",))
            if DEBUG and c == 0 and m == 0:
                b.dma(dbg["d_xzf"], xzp_sl(m)[:, 3:NT + 3], reads=(kxz,))
            kcv = emit_conv(m, 0)
            ku, kub = emit_u(m, par, kcv)
            if DEBUG and c == 0 and m == 0:
                b.dma(dbg["d_uf"], u_sl(par, m), reads=(ku,))
            emit_xp(m, 0, kub)
        emit_xp_finish(c, 0)
        for m in range(S):
            pb = 2 + 2 * (m % 2)
            for half in range(2):
                for k in range(8):
                    def mmz(e, k=k, m=m, half=half, pb=pb):
                        return e.matmul(ps(pb + half), winz_t(k, m),
                                        xlnT_k(k)[:, half * 512:(half + 1) * 512],
                                        start=(k == 0), stop=(k == 7))
                    b.op("tensor", mmz, reads=("winz", "xlnT"),
                         writes=(pk(pb + half),))
            ksz = f"sz{m % 2}"
            def zsil(e, m=m, pb=pb):
                return e.activation(out=fr(o_sz + (m % 2) * 1024, NT),
                                    in_=ps(pb, 1024), func=AF.Silu)
            b.op("scalar", zsil, reads=(pk(pb), pk(pb + 1)), writes=(ksz,))
            b.dma(sc_sz[c, m], fr(o_sz + (m % 2) * 1024, NT), reads=(ksz,),
                  writes=("scsz",))
            if DEBUG and c == 0 and m == 0:
                b.dma(dbg["d_sz"], fr(o_sz + (m % 2) * 1024, NT), reads=(ksz,))

    def emit_A_rev(c):
        par = c % 2
        for m in range(S):
            kxz = f"xz{m}"
            if c == NSC - 1:
                b.op("vector", lambda e, m=m: e.memset(xzp_sl(m)[:, NT:NT + 3], 0.0),
                     writes=(kxz,))
            else:
                def hd(e, m=m):
                    return e.tensor_copy(out=xzp_sl(m)[:, NT:NT + 3],
                                         in_=xzp_sl(m)[:, 0:3])
                b.op("vector", hd, reads=(), writes=(kxz,))
            b.dma(xzp_sl(m)[:, 0:NT], sc_xz[c, m], reads=("scxz", kxz), writes=(kxz,))
            kcv = emit_conv(m, 1)
            ku, kub = emit_u(m, par, kcv)
            if DEBUG and c == NSC - 1 and m == 0:
                b.dma(dbg["d_ur"], u_sl(par, m), reads=(ku,))
            emit_xp(m, 1, kub)
        emit_xp_finish(c, 1)

    stF = [0]
    stR = [0]

    def emit_B(c, dir_):
        par = c % 2
        rev = dir_ == 1
        slot = dir_ * 8 + c
        b.dma(fr(o_xdb, NT)[0:96, :], xd_out[slot], reads=(f"xdo{slot}",),
              writes=("xdb",))
        if DEBUG and ((not rev and c == 0) or (rev and c == NSC - 1)):
            b.dma(dbg["d_xdblr" if rev else "d_xdbl"], fr(o_xdb, NT)[0:96, :],
                  reads=("xdb",))
        def bct(e):
            return e.activation(out=fb(o_bctb, NT)[0:32, :],
                                in_=fr(o_xdb, NT)[64:96, :], func=AF.Copy)
        b.op("scalar", bct, reads=("xdb",), writes=("bctb",))
        def dtrc(e):
            return e.activation(out=fb(o_dtrb, NT)[0:64, :],
                                in_=fr(o_xdb, NT)[0:64, :], func=AF.Copy)
        b.op("scalar", dtrc, reads=("xdb",), writes=("dtrb",))
        if DEBUG and not rev and c == 0:
            def dtrd(e):
                return e.activation(out=fr(o_sp, NT)[0:64, :],
                                    in_=fb(o_dtrb, NT)[0:64, :], func=AF.Copy)
            b.op("scalar", dtrd, reads=("dtrb",), writes=("sp0",))
            b.dma(dbg["d_dtrb"][0:64, :], fr(o_sp, NT)[0:64, :], reads=("sp0",))
        if rev:
            for m in range(S):
                b.dma(fr(o_yf + m * NT, NT), sc_yf[c, m],
                      reads=("scyf", "revfence"), writes=(f"yf{m}",))
        for m in range(S):
            for half in range(2):
                def dt_(e, m=m, half=half, dir_=dir_):
                    return e.matmul(ps(2 + half), dtw_t(dir_, m),
                                    fb(o_dtrb, NT)[0:64, half * 512:(half + 1) * 512],
                                    start=True, stop=True)
                b.op("tensor", dt_, reads=("dtw", "dtrb"), writes=(pk(2 + half),))
            def vb(e, m=m, dir_=dir_):
                return e.tensor_scalar(out=fr(o_sp, NT), in0=ps(2, 1024),
                                       scalar1=dtbcol(dir_, m), scalar2=None,
                                       op0=ALU.add)
            b.op("vector", vb, reads=(pk(2), pk(3), "dtbk"), writes=("sp0",))
            if DEBUG and not rev and c == 0 and m == 0:
                b.dma(dbg["d_dtm"], fr(o_sp, NT), reads=("sp0",))
            def ab(e):
                return e.scalar_tensor_tensor(out=fr(o_sp + 1024, NT),
                                              in0=fr(o_sp, NT), scalar=-1.0,
                                              in1=fr(o_sp, NT),
                                              op0=ALU.mult, op1=ALU.max)
            b.op("vector", ab, reads=("sp0",), writes=("sp1",))
            def e1(e):
                return e.activation(out=fr(o_E, NT), in_=fr(o_sp + 1024, NT),
                                    func=AF.Exp, scale=-1.0)
            b.op("scalar", e1, reads=("sp1",), writes=("E0", "E1"))
            def l1(e):
                return e.activation(out=fr(o_sp + 1024, NT), in_=fr(o_E, NT),
                                    func=AF.Ln, bias=1.0)
            b.op("scalar", l1, reads=("E0", "E1"), writes=("sp1",))
            def rl(e):
                return e.tensor_scalar(out=fr(o_E, NT), in0=fr(o_sp, NT),
                                       scalar1=0.0, scalar2=None, op0=ALU.max)
            b.op("vector", rl, reads=("sp0",), writes=("E0", "E1"))
            def dl(e, m=m):
                return e.tensor_tensor(out=fr(o_delta + m * NT, NT),
                                       in0=fr(o_sp + 1024, NT),
                                       in1=fr(o_E, NT), op=ALU.add)
            b.op("vector", dl, reads=("sp1", "E0", "E1"), writes=(f"dl{m}",))
            if DEBUG and ((not rev and c == 0) or (rev and c == NSC - 1)) and m == 0:
                b.dma(dbg["d_delta"], fr(o_delta + m * NT, NT), reads=(f"dl{m}",))
            def du_(e, m=m, par=par):
                return e.tensor_tensor(out=fr(o_du + m * NT, NT),
                                       in0=fr(o_delta + m * NT, NT),
                                       in1=u_sl(par, m), op=ALU.mult)
            b.op("gpsimd", du_, reads=(f"dl{m}", f"u{par}_{m}"), writes=(f"du{m}",))
        st_bases, st_idx = (ST_R, stR) if rev else (ST_F, stF)
        fkeys = ("stR0", "stR1") if rev else ("stF0", "stF1")
        for half_i in range(2):
            half = (1 - half_i) if rev else half_i
            first = half_i == 0
            hs = slice(half * 512, (half + 1) * 512)
            for n in range(NS):
                rp = n % 2
                def brep(e, n=n, hs=hs):
                    return e.matmul(ps(6), selb_t(n),
                                    fb(o_bctb, NT)[0:32, hs], start=True, stop=True)
                b.op("tensor", brep, reads=("selb", "bctb"), writes=(pk(6),))
                def crep(e, n=n, hs=hs):
                    return e.matmul(ps(7), selb_t(16 + n),
                                    fb(o_bctb, NT)[0:32, hs], start=True, stop=True)
                b.op("tensor", crep, reads=("selb", "bctb"), writes=(pk(7),))
                def bev(e, rp=rp):
                    return e.activation(out=fr(o_Ball + rp * 512, 512), in_=ps(6),
                                        func=AF.Copy)
                b.op("scalar", bev, reads=(pk(6),), writes=(f"Ba{rp}",))
                def cev(e, rp=rp):
                    return e.activation(out=fr(o_Call + rp * 512, 512), in_=ps(7),
                                        func=AF.Copy)
                b.op("scalar", cev, reads=(pk(7),), writes=(f"Ca{rp}",))
                if DEBUG and not rev and c == 0 and n == 0 and half == 0:
                    b.dma(dbg["d_Ball"], fr(o_Ball + rp * 512, 512), reads=(f"Ba{rp}",))
                for m in range(S):
                    lane = m * 16 + n
                    ep = (n * S + m) % 2
                    def E_(e, m=m, n=n, hs=hs, ep=ep, dir_=dir_):
                        return e.activation(out=fr(o_E + ep * 512, 512),
                                            in_=fr(o_delta + m * NT, NT)[:, hs],
                                            func=AF.Exp, scale=acol(dir_, m, n))
                    b.op("scalar", E_, reads=(f"dl{m}", "ac"), writes=(f"E{ep}",))
                    def bh(e, m=m, hs=hs, ep=ep, rp=rp):
                        return e.tensor_tensor(out=fr(o_bh + ep * 512, 512),
                                               in0=fr(o_du + m * NT, NT)[:, hs],
                                               in1=fr(o_Ball + rp * 512, 512),
                                               op=ALU.mult)
                    b.op("gpsimd", bh, reads=(f"du{m}", f"Ba{rp}"), writes=(f"bh{ep}",))
                    if first:
                        init_ap = stcol(st_bases[st_idx[0]], lane)
                        ikey = fkeys[st_idx[0]]
                    else:
                        init_ap = stcol(ST_M, lane)
                        ikey = "stM"
                    def scan(e, ep=ep, init_ap=init_ap, rev=rev):
                        oh = fr(o_h + ep * 512, 512)
                        d0 = fr(o_E + ep * 512, 512)
                        d1 = fr(o_bh + ep * 512, 512)
                        if rev:
                            oh, d0, d1 = oh[:, ::-1], d0[:, ::-1], d1[:, ::-1]
                        return e.tensor_tensor_scan(out=oh, data0=d0, data1=d1,
                                                    initial=init_ap,
                                                    op0=ALU.mult, op1=ALU.add)
                    b.op("vector", scan, reads=(f"E{ep}", f"bh{ep}", ikey),
                         writes=(f"h{ep}",))
                    ecol = 0 if rev else 511
                    if first:
                        dst, dkey = stcol(ST_M, lane), "stM"
                    else:
                        dst, dkey = stcol(st_bases[1 - st_idx[0]], lane), fkeys[1 - st_idx[0]]
                    def stx(e, ep=ep, dst=dst, ecol=ecol):
                        return e.tensor_copy(out=dst,
                                             in_=fr(o_h + ep * 512, 512)[:, ecol:ecol + 1])
                    b.op("vector", stx, reads=(f"h{ep}",), writes=(dkey,))
                    def q_(e, ep=ep, rp=rp):
                        return e.tensor_tensor(out=fb(o_q + ep * 256, 512),
                                               in0=fr(o_h + ep * 512, 512),
                                               in1=fr(o_Call + rp * 512, 512),
                                               op=ALU.mult)
                    b.op("vector", q_, reads=(f"h{ep}", f"Ca{rp}"), writes=(f"q{ep}",))
                    if DEBUG and ((not rev and c == 0) or (rev and c == NSC - 1)) \
                            and m == 0 and n == 0 and half == (1 if rev else 0):
                        b.dma(dbg["d_E"], fr(o_E + ep * 512, 512), reads=(f"E{ep}",))
                        b.dma(dbg["d_bh"], fr(o_bh + ep * 512, 512), reads=(f"bh{ep}",))
                        b.dma(dbg["d_h"], fr(o_h + ep * 512, 512), reads=(f"h{ep}",))
                    def nsum(e, m=m, n=n, ep=ep):
                        return e.matmul(ps(2 + m), id_b(),
                                        fb(o_q + ep * 256, 512),
                                        start=(n == 0), stop=(n == NS - 1))
                    b.op("tensor", nsum, reads=(f"q{ep}", "idb"), writes=(pk(2 + m),))
            for m in range(S):
                gp_p = m % 2
                def gpre(e, m=m, hs=hs, par=par, dir_=dir_, gp_p=gp_p):
                    return e.scalar_tensor_tensor(
                        out=fr(o_gp + gp_p * 512, 512),
                        in0=u_sl(par, m)[:, hs],
                        scalar=dvcol(dir_, m),
                        in1=ps(2 + m), op0=ALU.mult, op1=ALU.add)
                b.op("vector", gpre, reads=(f"u{par}_{m}", "dv", pk(2 + m)),
                     writes=(f"gp{gp_p}",))
                if not rev:
                    b.dma(sc_yf[c, m][:, hs], fr(o_gp + gp_p * 512, 512),
                          reads=(f"gp{gp_p}",), writes=("scyf",))
                    if DEBUG and c == 0 and m == 0:
                        b.dma(dbg["d_yf"][:, hs], fr(o_gp + gp_p * 512, 512),
                              reads=(f"gp{gp_p}",))
                else:
                    ksz = f"szr{gp_p}"
                    b.dma(fr(o_szr + gp_p * 512, 512), sc_sz[c, m][:, hs],
                          reads=("scsz", "revfence", ksz), writes=(ksz,))
                    if DEBUG and c == NSC - 1 and m == 0:
                        b.dma(dbg["d_yr"][:, hs], fr(o_gp + gp_p * 512, 512),
                              reads=(f"gp{gp_p}",))
                    def add1(e, m=m, hs=hs, gp_p=gp_p):
                        return e.tensor_tensor(out=fr(o_gp + gp_p * 512, 512),
                                               in0=fr(o_gp + gp_p * 512, 512),
                                               in1=fr(o_yf + m * NT, NT)[:, hs],
                                               op=ALU.add)
                    b.op("gpsimd", add1, reads=(f"yf{m}", f"gp{gp_p}"),
                         writes=(f"gp{gp_p}",))
                    def gmul(e, m=m, hs=hs, gp_p=gp_p):
                        return e.tensor_tensor(
                            out=fb(o_ymix + m * 512, NT)[:, hs],
                            in0=fr(o_gp + gp_p * 512, 512),
                            in1=fr(o_szr + gp_p * 512, 512), op=ALU.mult)
                    b.op("gpsimd", gmul, reads=(f"gp{gp_p}", ksz, "revfence"),
                         writes=(f"ym{m}",))
        st_idx[0] = 1 - st_idx[0]
        if rev:
            for dmt in range(8):
                pb = 2 + 2 * (dmt % 2)
                for half in range(2):
                    for s_ in range(S):
                        def op_(e, dmt=dmt, half=half, s_=s_, pb=pb):
                            return e.matmul(ps(pb + half), wout_t(s_, dmt),
                                            fb(o_ymix + s_ * 512, NT)[:, half * 512:(half + 1) * 512],
                                            start=(s_ == 0), stop=(s_ == S - 1))
                        b.op("tensor", op_, reads=("wout", f"ym{s_}"),
                             writes=(pk(pb + half),))
                kop = f"op{dmt % 2}"
                def opev(e, dmt=dmt, pb=pb):
                    return e.activation(out=fr(o_ops + (dmt % 2) * 1024, NT),
                                        in_=ps(pb, 1024), func=AF.Copy)
                b.op("scalar", opev, reads=(pk(pb), pk(pb + 1), "revfence"),
                     writes=(kop,))
                b.dma(op_in[c % 2, c // 2, dmt], fr(o_ops + (dmt % 2) * 1024, NT),
                      reads=(kop,), writes=("opw",))
                if DEBUG and c == NSC - 1 and dmt == 0:
                    b.dma(dbg["d_op"], fr(o_ops + (dmt % 2) * 1024, NT), reads=(kop,))

    emit_A_fwd(0)
    emit_dequant_tile(0)
    emit_dequant_tile(1)
    next_dq = 2
    for c in range(1, NSC):
        emit_A_fwd(c)
        for t in range(next_dq, min(next_dq + 5, 40)):
            emit_dequant_tile(t)
        next_dq = min(next_dq + 5, 40)
        emit_B(c - 1, 0)
    for t in range(next_dq, 40):
        emit_dequant_tile(t)
    emit_B(NSC - 1, 0)
    b.op("vector", lambda e: e.memset(fr(o_mu + 10, 2), 0.0),
         writes=("x0", "x1", "xl", "xlnT", "sz0", "sz1", "szr0", "szr1",
                 "revfence"))
    emit_A_rev(NSC - 1)
    for c in range(NSC - 2, -1, -1):
        emit_A_rev(c)
        emit_B(c + 1, 1)
    emit_B(0, 1)
    for j in range(2):
        def rs(e, j=j):
            return e.collective_compute(
                "ReduceScatter", ALU.add, replica_groups=GROUPS,
                ins=[op_in[j]], outs=[op_out[j]])
        b.op("gpsimd", rs, reads=("opw",), writes=(f"opo{j}",), sig_as="coll")

    fence_keys = tuple(f"u0_{m}" for m in range(S)) + tuple(f"u1_{m}" for m in range(S)) + \
        tuple(f"dl{m}" for m in range(S)) + tuple(f"du{m}" for m in range(S)) + \
        tuple(f"yf{m}" for m in range(S)) + tuple(f"ym{m}" for m in range(S)) + \
        tuple(f"xz{m}" for m in range(S)) + \
        ("xlnT", "xl", "x0", "x1", "sp0", "sp1", "xdb", "bctb",
         "fence", "Ba0", "Ba1", "Ca0", "Ca1", "E0", "E1", "bh0", "bh1",
         "h0", "h1", "q0", "q1", "gp0", "gp1", "sz0", "sz1", "op0", "op1",
         "szr0", "szr1", "xdsb", "dq0", "dq1", "dq2", "ub0", "ub1", "dtrb",
         "stF0", "stF1", "stR0", "stR1", "stM")
    b.op("vector", lambda e: e.memset(fr(o_mu + 12, 2), 0.0), writes=fence_keys)
    b.dma(fr(of_gb, DM), ln_g.partition_broadcast(128), reads=("fence",),
          writes=("sz0",))
    b.dma(fr(of_gb + DM, DM), ln_b.partition_broadcast(128), reads=("fence",),
          writes=("sz1",))
    b.dma(fr(of_fb, 32), fc1_b, reads=("fence",), writes=("bctb",))
    b.dma(fr(of_fb + 32, 8), fc2_b, reads=("fence",), writes=("bctb",))
    def hb2(e):
        return e.tensor_scalar(out=fr(of_fb + 40, 8), in0=fr(of_fb + 32, 8),
                               scalar1=0.5, scalar2=None, op0=ALU.mult)
    b.op("vector", hb2, reads=("bctb",), writes=("bctb",))

    for fch in range(4):
        j = fch // 2
        off = (fch % 2) * 512
        for dmt in range(8):
            b.dma(fr(of_opsb + dmt * 512, 512), op_out[j, dmt][:, off:off + 512],
                  reads=(f"opo{j}", "fence", "opsb"), writes=("opsb",))
        for tt in range(4):
            kx = f"x{tt % 2}"
            b.dma(fr(o_x + (tt % 2) * DM, DM),
                  xq_in[fch * 512 + tt * 128:fch * 512 + (tt + 1) * 128, :],
                  reads=("fence",), writes=(kx,))
            for dmt in range(8):
                bnk = dmt % 2
                def tr(e, dmt=dmt, tt=tt, bnk=bnk):
                    return e.transpose(ps(bnk, 128),
                                       fr(of_opsb + dmt * 512, 512)[:, tt * 128:(tt + 1) * 128],
                                       fr(o_idf, 128))
                b.op("tensor", tr, reads=("opsb", "idf"), writes=(pk(bnk),))
                def ev(e, dmt=dmt, bnk=bnk):
                    return e.activation(out=fr(o_xln, DM)[:, dmt * 128:(dmt + 1) * 128],
                                        in_=ps(bnk, 128), func=AF.Copy)
                b.op("scalar", ev, reads=(pk(bnk),), writes=("xl",))
            def addx(e, tt=tt):
                return e.tensor_tensor(out=fr(of_x1 + tt * DM, DM),
                                       in0=fr(o_xln, DM),
                                       in1=fr(o_x + (tt % 2) * DM, DM),
                                       op=ALU.add)
            b.op("vector", addx, reads=("xl", kx), writes=("fx1",))
            if DEBUG and fch == 0 and tt == 0:
                b.dma(dbg["d_x1"], fr(of_x1 + tt * DM, DM), reads=("fx1",))
        for tt in range(4):
            ln_stats(fr(of_x1 + tt * DM, DM), "fx1")
            def n1(e, tt=tt):
                return e.tensor_scalar(out=fr(of_xl2 + tt * DM, DM),
                                       in0=fr(of_x1 + tt * DM, DM),
                                       scalar1=AR[:, o_mu + 6:o_mu + 7],
                                       scalar2=AR[:, o_mu + 7:o_mu + 8],
                                       op0=ALU.mult, op1=ALU.add)
            b.op("vector", n1, reads=("fx1", "mu"), writes=("fxl",))
            ln_stats(fr(of_xl2 + tt * DM, DM), "fxl")
            def n2(e, tt=tt):
                return e.tensor_scalar(out=fr(o_sp, DM), in0=fr(of_xl2 + tt * DM, DM),
                                       scalar1=AR[:, o_mu + 6:o_mu + 7],
                                       scalar2=AR[:, o_mu + 7:o_mu + 8],
                                       op0=ALU.mult, op1=ALU.add)
            b.op("vector", n2, reads=("fxl", "mu"), writes=("sp0",))
            def gmul(e):
                return e.tensor_tensor(out=fr(o_sp + 1024, DM), in0=fr(o_sp, DM),
                                       in1=fr(of_gb, DM), op=ALU.mult)
            b.op("vector", gmul, reads=("sp0", "sz0"), writes=("sp1",))
            def badd(e):
                return e.tensor_tensor(out=fr(o_sp, DM), in0=fr(o_sp + 1024, DM),
                                       in1=fr(of_gb + DM, DM), op=ALU.add)
            b.op("vector", badd, reads=("sp1", "sz1"), writes=("sp0",))
            for k in range(8):
                bnk = k % 2
                def tr(e, k=k, bnk=bnk):
                    return e.transpose(ps(bnk, 128),
                                       fr(o_sp, DM)[:, k * 128:(k + 1) * 128],
                                       fr(o_idf, 128))
                b.op("tensor", tr, reads=("sp0", "idf"), writes=(pk(bnk),))
                def ev(e, tt=tt, k=k, bnk=bnk):
                    return e.activation(
                        out=fb(of_h2T, 8 * 512)[:, k * 512 + tt * 128:k * 512 + (tt + 1) * 128],
                        in_=ps(bnk, 128), func=AF.Copy)
                b.op("scalar", ev, reads=(pk(bnk),), writes=("h2T",))
        h2T_k = lambda k: fb(of_h2T, 8 * 512)[:, k * 512:(k + 1) * 512]
        for Ft in range(32):
            kw = f"w1b{Ft % 2}"
            b.dma(fb(of_w1b + (Ft % 2) * 512, 8 * 128).rearrange("p (k m) -> p k m", k=8),
                  sc_w1[:, :, Ft * 128:(Ft + 1) * 128],
                  reads=("wsc", "fence", kw), writes=(kw,))
            pb = 2 + (Ft % 2)
            for k in range(8):
                def mm_(e, Ft=Ft, k=k, pb=pb):
                    wt = fb(of_w1b + (Ft % 2) * 512, 8 * 128)[:, k * 128:(k + 1) * 128]
                    return e.matmul(ps(pb), wt, h2T_k(k), start=(k == 0), stop=(k == 7))
                b.op("tensor", mm_, reads=(kw, "h2T"), writes=(pk(pb),))
            def sl(e, Ft=Ft, pb=pb):
                return e.activation(out=fb(of_y1, 32 * 512)[:, Ft * 512:(Ft + 1) * 512],
                                    in_=ps(pb), func=AF.Silu,
                                    bias=AR[:, of_fb + Ft:of_fb + Ft + 1])
            b.op("scalar", sl, reads=("bctb", "fence", pk(pb)), writes=("y1",))
        for dm in range(8):
            kw = f"w2b{dm % 2}"
            b.dma(fb(of_w2b + (dm % 2) * 2048, 32 * 128).rearrange("p (k m) -> p k m", k=32),
                  sc_w2[:, :, dm * 128:(dm + 1) * 128],
                  reads=("wsc", "fence", kw), writes=(kw,))
            pb = 4 + (dm % 2)
            for k in range(32):
                def mm_(e, dm=dm, k=k, pb=pb):
                    wt = fb(of_w2b + (dm % 2) * 2048, 32 * 128)[:, k * 128:(k + 1) * 128]
                    y1k = fb(of_y1, 32 * 512)[:, k * 512:(k + 1) * 512]
                    return e.matmul(ps(pb), wt, y1k, start=(k == 0), stop=(k == 31))
                b.op("tensor", mm_, reads=(kw, "y1"), writes=(pk(pb),))
            def hv(e, dm=dm, pb=pb):
                return e.tensor_scalar(out=fr(z2sl(dm), 512), in0=ps(pb),
                                       scalar1=0.5,
                                       scalar2=AR[:, of_fb + 40 + dm:of_fb + 41 + dm],
                                       op0=ALU.mult, op1=ALU.add)
            b.op("vector", hv, reads=("bctb", "fence", pk(pb)), writes=("z2",))
        for tt in range(4):
            for dm in range(8):
                bnk = dm % 2
                def tr(e, dm=dm, tt=tt, bnk=bnk):
                    return e.transpose(ps(bnk, 128),
                                       fr(z2sl(dm), 512)[:, tt * 128:(tt + 1) * 128],
                                       fr(o_idf, 128))
                b.op("tensor", tr, reads=("z2", "idf"), writes=(pk(bnk),))
                def zev(e, dm=dm, bnk=bnk):
                    return e.activation(out=fr(o_sp + dm * 128, 128),
                                        in_=ps(bnk, 128), func=AF.Copy)
                b.op("scalar", zev, reads=(pk(bnk),), writes=("sp0",))
            def f1(e, tt=tt):
                return e.tensor_tensor(out=fr(o_sp + 1024, DM),
                                       in0=fr(of_x1 + tt * DM, DM),
                                       in1=fr(of_xl2 + tt * DM, DM), op=ALU.add)
            b.op("vector", f1, reads=("fx1", "fxl"), writes=("sp1",))
            def f2(e, tt=tt):
                return e.tensor_tensor(out=fr(o_xln, DM), in0=fr(o_sp + 1024, DM),
                                       in1=fr(o_sp, DM), op=ALU.add)
            b.op("vector", f2, reads=("sp0", "sp1"), writes=("xl",))
            trow = fch * 512 + tt * 128
            b.dma(out[trow:trow + 128, :], fr(o_xln, DM), reads=("xl",),
                  writes=("outw",))

    b.emit("sync", lambda e: e.nop(),
           waits=[("sync", b.cnt["sync"]), ("gpsimd", b.cnt["gpsimd"]),
                  ("coll", b.cnt.get("coll", 0))], sig=False)

    with nc.Block() as block:
        b.replay(block, sems)
    es.close()
    return nc


def kernel(**inputs):
    import ml_dtypes
    inputs = {k: np.asarray(v) for k, v in inputs.items()}
    nc = build_program()

    selb = np.zeros((32, 32 * 128), np.float32)
    for n in range(32):
        selb[n, n * 128:(n + 1) * 128] = 1.0
    selb = selb.astype(ml_dtypes.bfloat16)

    win = inputs["L0_attn_in_proj_w"]
    wop = inputs["L0_attn_out_proj_w"]
    xpf = inputs["L0_attn_x_proj_w"]
    xpr = inputs["L0_attn_x_proj_w_b"]
    dtf = inputs["L0_attn_dt_proj_w"]
    dtr = inputs["L0_attn_dt_proj_w_b"]

    def chsl(a, q, axis=0):
        sl = [slice(None)] * a.ndim
        sl[axis] = slice(512 * q, 512 * (q + 1))
        return np.ascontiguousarray(a[tuple(sl)])

    def pmaj(a, q):
        a = chsl(a, q)
        if a.ndim == 1:
            return np.ascontiguousarray(a.reshape(4, 128).T)
        k = a.shape[1]
        return np.ascontiguousarray(
            a.reshape(4, 128, k).transpose(1, 0, 2).reshape(128, 4 * k))

    shared = {
        "selb": selb,
        "id128f": np.eye(128, dtype=np.float32),
        "fc1_q": np.ascontiguousarray(inputs["L0_ffn_fc1_q"]),
        "fc1_sc": np.ascontiguousarray(inputs["L0_ffn_fc1_scales"]),
        "fc1_b": np.ascontiguousarray(inputs["L0_ffn_fc1_bias"].reshape(32, 128).T),
        "fc2_q": np.ascontiguousarray(inputs["L0_ffn_fc2_q"]),
        "fc2_sc": np.ascontiguousarray(inputs["L0_ffn_fc2_scales"]),
        "fc2_b": np.ascontiguousarray(inputs["L0_ffn_fc2_bias"].reshape(8, 128).T),
        "ln_g": np.ascontiguousarray(inputs["L0_ffn_ln_gamma"]),
        "ln_b": np.ascontiguousarray(inputs["L0_ffn_ln_beta"]),
    }
    in_maps = []
    for cc in range(8):
        bq, q = divmod(cc, 4)
        x = np.ascontiguousarray(inputs["L0_attn_hidden_states"][bq])
        m = dict(shared)
        m.update({
            "x_in": x,
            "xq_in": np.ascontiguousarray(x[OWN * q:OWN * (q + 1)]),
            "winx": np.ascontiguousarray(chsl(win[:DI], q).T),
            "winz": np.ascontiguousarray(chsl(win[DI:], q).T),
            "wout": np.ascontiguousarray(chsl(wop.T, q)),
            "xpw": np.stack([chsl(np.ascontiguousarray(xpf.T), q),
                             chsl(np.ascontiguousarray(xpr.T), q)], 0),
            "dtw": np.stack([np.ascontiguousarray(chsl(dtf, q).T),
                             np.ascontiguousarray(chsl(dtr, q).T)], 0),
            "convw": np.concatenate([pmaj(inputs["L0_attn_conv_w"][:, 0, :], q),
                                     pmaj(inputs["L0_attn_conv_w_b"][:, 0, :], q)], 1),
            "convb": np.concatenate([pmaj(inputs["L0_attn_conv_b"], q),
                                     pmaj(inputs["L0_attn_conv_b_b"], q)], 1),
            "dtb": np.concatenate([pmaj(inputs["L0_attn_dt_proj_b"], q),
                                   pmaj(inputs["L0_attn_dt_proj_b_b"], q)], 1),
            "alog": np.concatenate([pmaj(inputs["L0_attn_A_log"], q),
                                    pmaj(inputs["L0_attn_A_b_log"], q)], 1),
            "dvec": np.concatenate([pmaj(inputs["L0_attn_D"], q),
                                    pmaj(inputs["L0_attn_D_b"], q)], 1),
        })
        in_maps.append(m)

    res = run_bass_kernel_spmd(nc, in_maps, core_ids=list(range(8)))
    outp = np.empty((2, L, DM), np.float32)
    for cc in range(8):
        bq, q = divmod(cc, 4)
        outp[bq, OWN * q:OWN * (q + 1)] = np.asarray(res.results[cc]["out"])
    if DEBUG:
        kernel.last_results = res
    return outp


# version 13
# speedup vs baseline: 1.3982x; 1.0054x over previous
import os
import sys
import numpy as np

sys.path.insert(0, "/opt/trn_rl_repo")
import concourse.bass as bass
import concourse.mybir as mybir
from concourse.bass_utils import run_bass_kernel_spmd

F32 = mybir.dt.float32
BF16 = mybir.dt.bfloat16
I32 = mybir.dt.int32
AF = mybir.ActivationFunctionType
ALU = mybir.AluOpType

DM = 1024
DI = 2048
CHQ = 512
S = 4
NS = 16
RK = 64
L = 8192
NT = 1024
NSC = 8
OWN = 2048
EPS = 1e-5
DEBUG = bool(int(os.environ.get("BASSMK_DEBUG", "0")))

ENGINES = ("sync", "tensor", "scalar", "vector", "gpsimd")
GROUPS = [[0, 1, 2, 3], [4, 5, 6, 7]]


class Buf:
    def __init__(self):
        self.w = None
        self.readers = {}

    def wr(self):
        return [(e, v) for e, v in self.readers.items()] + \
            ([self.w] if self.w is not None else [])

    def setw(self, tok):
        self.w = tok
        self.readers = {}

    def rd(self):
        return [self.w] if self.w is not None else []

    def setr(self, tok):
        if tok is not None:
            e, v = tok
            if self.readers.get(e, 0) < v:
                self.readers[e] = v


class Bld:
    def __init__(self):
        self.ops = {e: [] for e in ENGINES}
        self.cnt = {e: 0 for e in ENGINES}
        self.bufs = {}

    def B(self, key):
        if key not in self.bufs:
            self.bufs[key] = Buf()
        return self.bufs[key]

    def emit(self, eng, fn, waits=(), sig=True, sig_as=None):
        ws = []
        for t in waits:
            if t is None:
                continue
            pe, v = t
            if pe == eng and sig_as is None:
                if pe == "tensor":
                    continue
                if pe != "sync" and self.cnt[eng] + 1 - v > 4:
                    continue
            ws.append((pe, v))
        if sig:
            sname = sig_as or eng
            self.cnt.setdefault(sname, 0)
            self.cnt[sname] += 16 if sname == "sync" else 1
            tok = (sname, self.cnt[sname])
        else:
            sname = None
            tok = None
        self.ops[eng].append((fn, ws, sname))
        return tok

    def op(self, eng, fn, reads=(), writes=(), sig_as=None):
        waits = []
        for k in reads:
            waits += self.B(k).rd()
        for k in writes:
            waits += self.B(k).wr()
        tok = self.emit(eng, fn, waits=waits, sig_as=sig_as)
        for k in reads:
            self.B(k).setr(tok)
        for k in writes:
            self.B(k).setw(tok)
        return tok

    def dma(self, out, in_, reads=(), writes=()):
        def fn(e):
            return e.dma_start(out=out, in_=in_)
        return self.op("sync", fn, reads=reads, writes=writes)

    def replay(self, block, sems):
        for eng_name in ENGINES:
            ops = self.ops[eng_name]
            if not ops:
                continue

            def make_body(ename, olist):
                def body(eng):
                    seen = {}
                    cum = 0
                    for fn, ws, sname in olist:
                        for pe, v in ws:
                            if seen.get(pe, 0) >= v:
                                continue
                            seen[pe] = v
                            eng.wait_ge(sems[pe], v)
                        inst = fn(eng)
                        if sname:
                            inst.then_inc(sems[sname], 16 if sname == "sync" else 1)
                        if sname == "sync":
                            cum += 16
                            eng.wait_ge(sems["sync"], cum)
                            seen["sync"] = cum
                return body
            getattr(block, eng_name)(make_body(eng_name, ops))


def build_program():
    nc = bass.Bass("TRN2", target_bir_lowering=False)

    def din(name, shape, dt=F32):
        return nc.dram_tensor(name, shape, dt, kind="ExternalInput").ap()

    x_in = din("x_in", [L, DM])
    xq_in = din("xq_in", [OWN, DM])
    winx = din("winx", [DM, CHQ])
    winz = din("winz", [DM, CHQ])
    wout = din("wout", [CHQ, DM])
    xpw = din("xpw", [2, CHQ, 96])
    dtw = din("dtw", [2, RK, CHQ])
    convw = din("convw", [128, 2 * S * 4])
    convb = din("convb", [128, 2 * S])
    dtb = din("dtb", [128, 2 * S])
    alog = din("alog", [128, 2 * S * NS])
    dvec = din("dvec", [128, 2 * S])
    selb = din("selb", [32, 32 * 128], BF16)
    id128f = din("id128f", [128, 128])
    fc1_q = din("fc1_q", [DM, 4 * DM], I32)
    fc1_sc = din("fc1_sc", [8, 4 * DM])
    fc1_b = din("fc1_b", [128, 32])
    fc2_q = din("fc2_q", [4 * DM, DM], I32)
    fc2_sc = din("fc2_sc", [32, DM])
    fc2_b = din("fc2_b", [128, 8])
    ln_g = din("ln_g", [DM])
    ln_b = din("ln_b", [DM])

    out = nc.dram_tensor("out", [OWN, DM], F32, kind="ExternalOutput").ap()

    def dscr(name, shape, dt=F32):
        return nc.dram_tensor(name, shape, dt, kind="Internal").ap()

    sc_xz = dscr("sc_xz", [NSC, S, 128, NT])
    sc_sz = dscr("sc_sz", [NSC, S, 128, NT])
    sc_yf = dscr("sc_yf", [NSC, S, 128, NT])
    xd_in = dscr("xd_in", [16, 96, NT])
    xd_out = dscr("xd_out", [16, 96, NT])
    op_in = dscr("op_in", [2, 4, 8, 128, NT])
    op_out = dscr("op_out", [2, 8, 128, NT])
    sc_w1 = dscr("sc_w1", [128, 8, 4 * DM], BF16)
    sc_w2 = dscr("sc_w2", [128, 32, DM], BF16)

    dbg = {}
    if DEBUG:
        for nm, shape in [("d_xln", [128, DM]), ("d_xzf", [128, NT]),
                          ("d_uf", [128, NT]), ("d_xdbl", [96, NT]),
                          ("d_delta", [128, NT]), ("d_E", [128, 512]),
                          ("d_bh", [128, 512]), ("d_h", [128, 512]),
                          ("d_yf", [128, NT]), ("d_ur", [128, NT]),
                          ("d_xdblr", [96, NT]),
                          ("d_yr", [128, NT]), ("d_op", [128, NT]),
                          ("d_x1", [128, DM]), ("d_sz", [128, NT]),
                          ("d_dtm", [128, NT]), ("d_Ball", [128, 512]),
                          ("d_dtrb", [128, NT])]:
            dbg[nm] = nc.dram_tensor(nm, shape, F32, kind="ExternalOutput").ap()

    ACOLS = 53180
    _off = [0]

    def A(n):
        o = _off[0]
        _off[0] += n
        assert _off[0] <= ACOLS, f"arena overflow at {_off[0]}"
        return o

    o_winx = A(2048)
    o_winz = A(2048)
    o_wout = A(2048)
    o_xpw = A(384)
    o_dtw = A(512)
    o_selb = A(2048)
    o_idf = A(128)
    o_idb = A(64)
    o_cw = A(32)
    o_cb = A(8)
    o_dtb = A(8)
    o_dv = A(8)
    o_ac = A(128)
    o_mu = A(16)
    o_st = A(320)
    o_x = A(2048)
    o_xln = A(1024)
    o_xlnT = A(4096)
    o_xzp = A(4112)
    o_ub = A(1024)
    o_sz = A(2048)
    o_u = A(8192)
    o_delta = A(4096)
    o_du = A(4096)
    o_xdb = A(1024)
    o_bctb = A(512)
    o_Ball = A(1024)
    o_Call = A(1024)
    o_E = A(1024)
    o_bh = A(1024)
    o_h = A(1024)
    o_q = A(512)
    o_sp = A(2048)
    o_gp = A(1024)
    o_dq = A(1536)
    o_dtrb = A(512)
    assert _off[0] <= ACOLS, _off[0]

    o_yf = o_xlnT
    o_szr = o_x
    o_ymix = o_x + 1024
    o_ops = o_sz
    of_x1 = o_u
    of_xl2 = o_u + 4096
    of_h2T = o_delta
    of_w1b = o_delta + 2048
    of_w2b = o_delta + 3072
    of_y1 = o_xlnT
    of_opsb = o_Ball
    of_gb = o_sz
    of_fb = o_bctb

    def z2sl(dm):
        offs = [o_gp, o_gp + 512, o_dq, o_dq + 512, o_dq + 1024,
                o_ub, o_ub + 512, o_xdb]
        return offs[dm]

    b = Bld()

    import contextlib
    es = contextlib.ExitStack()
    AR = es.enter_context(nc.sbuf_tensor("AR", [128, ACOLS], F32))
    PSF = es.enter_context(nc.psum_tensor("PSF", [128, 4096], F32))
    sems = {e: es.enter_context(nc.semaphore("sem_" + e))
            for e in ENGINES + ("coll",)}

    def fr(off, n):
        return AR[:, off:off + n]

    def fb(off, n_bf16):
        return AR[:, off:off + (n_bf16 + 1) // 2].bitcast(BF16)[:, 0:n_bf16]

    def ps(bank, n=512):
        return PSF[:, bank * 512:bank * 512 + n]

    pk = lambda bank: f"ps{bank}"

    b.op("vector", lambda e: e.memset(AR[:, o_mu + 8:o_mu + 9], EPS), writes=("eps",))
    b.op("vector", lambda e: e.memset(fr(o_st, 320), 0.0),
         writes=("stF0", "stF1", "stR0", "stR1", "stM"))

    b.dma(fb(o_selb, 4096)[0:32, :], selb, writes=("selb",))
    b.dma(fr(o_idf, 128), id128f, writes=("idf",))
    b.dma(fr(o_cw, 32), convw, writes=("cw",))
    b.dma(fr(o_cb, 8), convb, writes=("cb",))
    b.dma(fr(o_dtb, 8), dtb, writes=("dtbk",))
    b.dma(fr(o_dv, 8), dvec, writes=("dv",))
    b.dma(fr(o_sp, 128), alog, writes=("sp0",))
    def mkA(e):
        return e.activation(out=fr(o_ac, 128), in_=fr(o_sp, 128), func=AF.Exp)
    b.op("scalar", mkA, reads=("sp0",), writes=("ac",))
    def negA(e):
        return e.tensor_scalar(out=fr(o_ac, 128), in0=fr(o_ac, 128),
                               scalar1=-1.0, scalar2=None, op0=ALU.mult)
    b.op("vector", negA, reads=(), writes=("ac",))
    def mkidb(e):
        return e.activation(out=fb(o_idb, 128), in_=fr(o_idf, 128), func=AF.Copy)
    b.op("scalar", mkidb, reads=("idf",), writes=("idb",))

    def stage_resident(srcs, ncols, dst_off, key):
        half = ncols // 2
        for hh in range(2):
            b.dma(fr(o_u, half), srcs[hh], writes=("u0_0", "u0_1", "u0_2", "u0_3"))
            def cast(e, hh=hh, half=half, dst_off=dst_off, ncols=ncols):
                return e.activation(out=fb(dst_off, ncols)[:, hh * half:(hh + 1) * half],
                                    in_=fr(o_u, half), func=AF.Copy)
            b.op("scalar", cast, reads=("u0_0",), writes=(key,))

    winx_v = winx.rearrange("(k p) m -> p k m", p=128)
    winz_v = winz.rearrange("(k p) m -> p k m", p=128)
    wout_v = wout.rearrange("(s p) m -> p s m", p=128)
    stage_resident([winx_v[:, 0:4, :], winx_v[:, 4:8, :]], 4096, o_winx, "winx")
    stage_resident([winz_v[:, 0:4, :], winz_v[:, 4:8, :]], 4096, o_winz, "winz")
    stage_resident([wout_v[:, 0:2, :], wout_v[:, 2:4, :]], 4096, o_wout, "wout")
    b.dma(fr(o_u, 768), xpw.rearrange("d (s p) r -> p d s r", p=128),
          writes=("u0_0", "u0_1", "u0_2", "u0_3"))
    def cxp(e):
        return e.activation(out=fb(o_xpw, 768), in_=fr(o_u, 768), func=AF.Copy)
    b.op("scalar", cxp, reads=("u0_0",), writes=("xpw",))
    b.dma(fr(o_u, 1024)[0:64, :], dtw.rearrange("d r m -> r d m"),
          writes=("u0_0", "u0_1", "u0_2", "u0_3"))
    def cdt(e):
        return e.activation(out=fb(o_dtw, 1024)[0:64, :], in_=fr(o_u, 1024)[0:64, :],
                            func=AF.Copy)
    b.op("scalar", cdt, reads=("u0_0",), writes=("dtw",))

    winx_t = lambda k, m: fb(o_winx, 8 * CHQ)[:, k * CHQ + m * 128:k * CHQ + (m + 1) * 128]
    winz_t = lambda k, m: fb(o_winz, 8 * CHQ)[:, k * CHQ + m * 128:k * CHQ + (m + 1) * 128]
    wout_t = lambda s, d: fb(o_wout, S * DM)[:, s * DM + d * 128:s * DM + (d + 1) * 128]
    xpw_t = lambda dir_, m: fb(o_xpw, 768)[:, dir_ * 384 + m * 96:dir_ * 384 + (m + 1) * 96]
    dtw_t = lambda dir_, m: fb(o_dtw, 1024)[0:64, dir_ * 512 + m * 128:dir_ * 512 + (m + 1) * 128]
    selb_t = lambda j: fb(o_selb, 4096)[0:32, j * 128:(j + 1) * 128]
    id_b = lambda: fb(o_idb, 128)
    wcol = lambda dir_, m, j: AR[:, o_cw + dir_ * 16 + m * 4 + j:o_cw + dir_ * 16 + m * 4 + j + 1]
    cbcol = lambda dir_, m: AR[:, o_cb + dir_ * 4 + m:o_cb + dir_ * 4 + m + 1]
    dtbcol = lambda dir_, m: AR[:, o_dtb + dir_ * 4 + m:o_dtb + dir_ * 4 + m + 1]
    dvcol = lambda dir_, m: AR[:, o_dv + dir_ * 4 + m:o_dv + dir_ * 4 + m + 1]
    acol = lambda dir_, m, n: AR[:, o_ac + dir_ * 64 + m * 16 + n:o_ac + dir_ * 64 + m * 16 + n + 1]
    stcol = lambda base, lane: AR[:, o_st + base + lane:o_st + base + lane + 1]
    ST_F = (0, 64)
    ST_R = (128, 192)
    ST_M = 256
    xzp_sl = lambda m: AR[:, o_xzp + m * 1028:o_xzp + m * 1028 + 1028]
    u_sl = lambda par, m: fr(o_u + par * 4096 + m * NT, NT)
    xlnT_k = lambda k: fb(o_xlnT, 8 * NT)[:, k * NT:(k + 1) * NT]

    def ln_stats(src_ap, src_key):
        def s1(e):
            return e.activation(out=fr(o_sp, DM), in_=src_ap, func=AF.Copy,
                                accum_out=AR[:, o_mu:o_mu + 1])
        b.op("scalar", s1, reads=(src_key,), writes=("sp0", "mu"))
        def s2(e):
            return e.activation(out=fr(o_sp, DM), in_=src_ap, func=AF.Square,
                                accum_out=AR[:, o_mu + 1:o_mu + 2])
        b.op("scalar", s2, reads=(src_key,), writes=("sp0", "mu"))
        def mstat(e):
            return e.tensor_scalar(out=AR[:, o_mu + 2:o_mu + 4],
                                   in0=AR[:, o_mu:o_mu + 2], scalar1=1.0 / DM,
                                   scalar2=None, op0=ALU.mult)
        b.op("vector", mstat, reads=(), writes=("mu",))
        def musq(e):
            return e.activation(out=AR[:, o_mu + 9:o_mu + 10],
                                in_=AR[:, o_mu + 2:o_mu + 3], func=AF.Square)
        b.op("scalar", musq, reads=("mu",), writes=("mu",))
        def nvar(e):
            return e.tensor_tensor(out=AR[:, o_mu + 4:o_mu + 5],
                                   in0=AR[:, o_mu + 9:o_mu + 10],
                                   in1=AR[:, o_mu + 3:o_mu + 4], op=ALU.subtract)
        b.op("vector", nvar, reads=(), writes=("mu",))
        def lnv(e):
            return e.activation(out=AR[:, o_mu + 5:o_mu + 6],
                                in_=AR[:, o_mu + 4:o_mu + 5], func=AF.Ln,
                                scale=-1.0, bias=AR[:, o_mu + 8:o_mu + 9])
        b.op("scalar", lnv, reads=("mu", "eps"), writes=("mu",))
        def rstd(e):
            return e.activation(out=AR[:, o_mu + 6:o_mu + 7],
                                in_=AR[:, o_mu + 5:o_mu + 6], func=AF.Exp,
                                scale=-0.5)
        b.op("scalar", rstd, reads=(), writes=("mu",))
        def nb(e):
            return e.tensor_scalar(out=AR[:, o_mu + 7:o_mu + 8],
                                   in0=AR[:, o_mu + 2:o_mu + 3],
                                   scalar1=AR[:, o_mu + 6:o_mu + 7],
                                   scalar2=-1.0, op0=ALU.mult, op1=ALU.mult)
        b.op("vector", nb, reads=(), writes=("mu",))

    def emit_conv(m, dir_):
        taps = [0, 1, 2, 3] if dir_ == 0 else [3, 2, 1, 0]
        kxz = f"xz{m}"
        def c0(e, m=m, dir_=dir_, taps=taps):
            return e.tensor_scalar(out=fr(o_sp, NT), in0=xzp_sl(m)[:, 0:NT],
                                   scalar1=wcol(dir_, m, taps[0]),
                                   scalar2=cbcol(dir_, m),
                                   op0=ALU.mult, op1=ALU.add)
        b.op("vector", c0, reads=(kxz, "cw", "cb"), writes=("sp0",))
        srcdst = [("sp0", "sp1"), ("sp1", "sp0"), ("sp0", "sp1")]
        offd = [o_sp + 1024, o_sp, o_sp + 1024]
        offs_ = [o_sp, o_sp + 1024, o_sp]
        for j in range(1, 4):
            def cj(e, m=m, j=j, dir_=dir_, taps=taps):
                return e.scalar_tensor_tensor(out=fr(offd[j - 1], NT),
                                              in0=xzp_sl(m)[:, j:NT + j],
                                              scalar=wcol(dir_, m, taps[j]),
                                              in1=fr(offs_[j - 1], NT),
                                              op0=ALU.mult, op1=ALU.add)
            b.op("vector", cj, reads=(kxz, "cw", srcdst[j - 1][0]),
                 writes=(srcdst[j - 1][1],))
        return "sp1"

    def emit_u(m, par, kcv):
        ku = f"u{par}_{m}"
        def silu(e, m=m, par=par):
            return e.activation(out=u_sl(par, m), in_=fr(o_sp + 1024, NT), func=AF.Silu)
        b.op("scalar", silu, reads=(kcv,), writes=(ku,))
        kub = f"ub{m % 2}"
        def silub(e, m=m):
            return e.activation(out=fb(o_ub + (m % 2) * 512, NT),
                                in_=fr(o_sp + 1024, NT), func=AF.Silu)
        b.op("scalar", silub, reads=(kcv,), writes=(kub,))
        return ku, kub

    def emit_xp(m, dir_, kub):
        for half in range(2):
            def xp_(e, m=m, half=half, dir_=dir_):
                return e.matmul(ps(half)[0:96, :], xpw_t(dir_, m),
                                fb(o_ub + (m % 2) * 512, NT)[:, half * 512:(half + 1) * 512],
                                start=(m == 0), stop=(m == S - 1))
            b.op("tensor", xp_, reads=("xpw", kub), writes=(pk(half),))

    def emit_xp_finish(c, dir_):
        slot = dir_ * 8 + c
        def xpev(e):
            return e.activation(out=fr(o_ub, 1024)[0:96, :], in_=ps(0, 1024)[0:96, :],
                                func=AF.Copy)
        b.op("scalar", xpev, reads=(pk(0), pk(1)), writes=("ub0", "ub1", "xdsb"))
        b.dma(xd_in[slot], fr(o_ub, 1024)[0:96, :],
              reads=("xdsb", "ub0", "ub1"), writes=("xdw",))
        def coll(e, slot=slot):
            return e.collective_compute(
                "AllReduce", ALU.add, replica_groups=GROUPS,
                ins=[xd_in[slot]], outs=[xd_out[slot]])
        b.op("gpsimd", coll, reads=("xdw",), writes=(f"xdo{slot}",), sig_as="coll")

    def emit_dequant_tile(tile):
        if tile < 8:
            k, W, qd, scd, dst = tile, 4 * DM, fc1_q, fc1_sc, sc_w1
        else:
            k, W, qd, scd, dst = tile - 8, DM, fc2_q, fc2_sc, sc_w2
        for cc in range(W // 512):
            b.dma(AR[:, o_dq:o_dq + 512].bitcast(I32),
                  qd[k * 128:(k + 1) * 128, cc * 512:(cc + 1) * 512],
                  writes=("dq0",))
            b.dma(fr(o_dq + 512, 512),
                  scd[k, cc * 512:(cc + 1) * 512].partition_broadcast(128),
                  writes=("dq1",))
            def cvt(e):
                return e.activation(out=fr(o_dq + 1024, 512),
                                    in_=AR[:, o_dq:o_dq + 512].bitcast(I32),
                                    func=AF.Copy, bias=-128.0)
            b.op("scalar", cvt, reads=("dq0",), writes=("dq2",))
            def mul(e):
                return e.tensor_tensor(out=fb(o_dq, 512), in0=fr(o_dq + 1024, 512),
                                       in1=fr(o_dq + 512, 512), op=ALU.mult)
            b.op("gpsimd", mul, reads=("dq1", "dq2"), writes=("dq0",))
            b.dma(dst[:, k, cc * 512:(cc + 1) * 512], fb(o_dq, 512),
                  reads=("dq0",), writes=("wsc",))

    def emit_A_fwd(c):
        par = c % 2
        for tt in range(8):
            kx = f"x{tt % 2}"
            b.dma(fr(o_x + (tt % 2) * DM, DM),
                  x_in[c * NT + tt * 128:c * NT + (tt + 1) * 128, :], writes=(kx,))
            xa = fr(o_x + (tt % 2) * DM, DM)
            ln_stats(xa, kx)
            def norm(e, xa=xa):
                return e.tensor_scalar(out=fr(o_xln, DM), in0=xa,
                                       scalar1=AR[:, o_mu + 6:o_mu + 7],
                                       scalar2=AR[:, o_mu + 7:o_mu + 8],
                                       op0=ALU.mult, op1=ALU.add)
            b.op("vector", norm, reads=("mu", kx), writes=("xl",))
            if DEBUG and c == 0 and tt == 0:
                b.dma(dbg["d_xln"], fr(o_xln, DM), reads=("xl",))
            for k in range(8):
                bnk = k % 2
                def tr(e, k=k, bnk=bnk):
                    return e.transpose(ps(bnk, 128),
                                       fr(o_xln, DM)[:, k * 128:(k + 1) * 128],
                                       fr(o_idf, 128))
                b.op("tensor", tr, reads=("xl", "idf"), writes=(pk(bnk),))
                def evac(e, tt=tt, k=k, bnk=bnk):
                    return e.activation(
                        out=xlnT_k(k)[:, tt * 128:(tt + 1) * 128],
                        in_=ps(bnk, 128), func=AF.Copy)
                b.op("scalar", evac, reads=(pk(bnk),), writes=("xlnT",))
        for m in range(S):
            pb = 2 + 2 * (m % 2)
            for half in range(2):
                for k in range(8):
                    def mm_(e, k=k, m=m, half=half, pb=pb):
                        return e.matmul(ps(pb + half), winx_t(k, m),
                                        xlnT_k(k)[:, half * 512:(half + 1) * 512],
                                        start=(k == 0), stop=(k == 7))
                    b.op("tensor", mm_, reads=("winx", "xlnT"),
                         writes=(pk(pb + half),))
            kxz = f"xz{m}"
            if c == 0:
                b.op("vector", lambda e, m=m: e.memset(xzp_sl(m)[:, 0:3], 0.0),
                     writes=(kxz,))
            else:
                def tl(e, m=m):
                    return e.tensor_copy(out=xzp_sl(m)[:, 0:3],
                                         in_=xzp_sl(m)[:, NT:NT + 3])
                b.op("vector", tl, reads=(), writes=(kxz,))
            def xev(e, m=m, pb=pb):
                return e.activation(out=xzp_sl(m)[:, 3:NT + 3], in_=ps(pb, 1024),
                                    func=AF.Copy)
            b.op("scalar", xev, reads=(pk(pb), pk(pb + 1)), writes=(kxz,))
            b.dma(sc_xz[c, m], xzp_sl(m)[:, 3:NT + 3], reads=(kxz,), writes=("scxz",))
            if DEBUG and c == 0 and m == 0:
                b.dma(dbg["d_xzf"], xzp_sl(m)[:, 3:NT + 3], reads=(kxz,))
            kcv = emit_conv(m, 0)
            ku, kub = emit_u(m, par, kcv)
            if DEBUG and c == 0 and m == 0:
                b.dma(dbg["d_uf"], u_sl(par, m), reads=(ku,))
            emit_xp(m, 0, kub)
        emit_xp_finish(c, 0)
        for m in range(S):
            pb = 2 + 2 * (m % 2)
            for half in range(2):
                for k in range(8):
                    def mmz(e, k=k, m=m, half=half, pb=pb):
                        return e.matmul(ps(pb + half), winz_t(k, m),
                                        xlnT_k(k)[:, half * 512:(half + 1) * 512],
                                        start=(k == 0), stop=(k == 7))
                    b.op("tensor", mmz, reads=("winz", "xlnT"),
                         writes=(pk(pb + half),))
            ksz = f"sz{m % 2}"
            def zsil(e, m=m, pb=pb):
                return e.activation(out=fr(o_sz + (m % 2) * 1024, NT),
                                    in_=ps(pb, 1024), func=AF.Silu)
            b.op("scalar", zsil, reads=(pk(pb), pk(pb + 1)), writes=(ksz,))
            b.dma(sc_sz[c, m], fr(o_sz + (m % 2) * 1024, NT), reads=(ksz,),
                  writes=("scsz",))
            if DEBUG and c == 0 and m == 0:
                b.dma(dbg["d_sz"], fr(o_sz + (m % 2) * 1024, NT), reads=(ksz,))

    def emit_A_rev(c):
        par = c % 2
        for m in range(S):
            kxz = f"xz{m}"
            if c == NSC - 1:
                b.op("vector", lambda e, m=m: e.memset(xzp_sl(m)[:, NT:NT + 3], 0.0),
                     writes=(kxz,))
            else:
                def hd(e, m=m):
                    return e.tensor_copy(out=xzp_sl(m)[:, NT:NT + 3],
                                         in_=xzp_sl(m)[:, 0:3])
                b.op("vector", hd, reads=(), writes=(kxz,))
            b.dma(xzp_sl(m)[:, 0:NT], sc_xz[c, m], reads=("scxz", kxz), writes=(kxz,))
            kcv = emit_conv(m, 1)
            ku, kub = emit_u(m, par, kcv)
            if DEBUG and c == NSC - 1 and m == 0:
                b.dma(dbg["d_ur"], u_sl(par, m), reads=(ku,))
            emit_xp(m, 1, kub)
        emit_xp_finish(c, 1)

    stF = [0]
    stR = [0]

    def emit_B(c, dir_):
        par = c % 2
        rev = dir_ == 1
        slot = dir_ * 8 + c
        b.dma(fr(o_xdb, NT)[0:96, :], xd_out[slot], reads=(f"xdo{slot}",),
              writes=("xdb",))
        if DEBUG and ((not rev and c == 0) or (rev and c == NSC - 1)):
            b.dma(dbg["d_xdblr" if rev else "d_xdbl"], fr(o_xdb, NT)[0:96, :],
                  reads=("xdb",))
        def bct(e):
            return e.activation(out=fb(o_bctb, NT)[0:32, :],
                                in_=fr(o_xdb, NT)[64:96, :], func=AF.Copy)
        b.op("scalar", bct, reads=("xdb",), writes=("bctb",))
        def dtrc(e):
            return e.activation(out=fb(o_dtrb, NT)[0:64, :],
                                in_=fr(o_xdb, NT)[0:64, :], func=AF.Copy)
        b.op("scalar", dtrc, reads=("xdb",), writes=("dtrb",))
        if DEBUG and not rev and c == 0:
            def dtrd(e):
                return e.activation(out=fr(o_sp, NT)[0:64, :],
                                    in_=fb(o_dtrb, NT)[0:64, :], func=AF.Copy)
            b.op("scalar", dtrd, reads=("dtrb",), writes=("sp0",))
            b.dma(dbg["d_dtrb"][0:64, :], fr(o_sp, NT)[0:64, :], reads=("sp0",))
        if rev:
            for m in range(S):
                b.dma(fr(o_yf + m * NT, NT), sc_yf[c, m],
                      reads=("scyf", "revfence"), writes=(f"yf{m}",))
        for m in range(S):
            for half in range(2):
                def dt_(e, m=m, half=half, dir_=dir_):
                    return e.matmul(ps(2 + half), dtw_t(dir_, m),
                                    fb(o_dtrb, NT)[0:64, half * 512:(half + 1) * 512],
                                    start=True, stop=True)
                b.op("tensor", dt_, reads=("dtw", "dtrb"), writes=(pk(2 + half),))
            def vb(e, m=m, dir_=dir_):
                return e.tensor_scalar(out=fr(o_sp, NT), in0=ps(2, 1024),
                                       scalar1=dtbcol(dir_, m), scalar2=None,
                                       op0=ALU.add)
            b.op("vector", vb, reads=(pk(2), pk(3), "dtbk"), writes=("sp0",))
            if DEBUG and not rev and c == 0 and m == 0:
                b.dma(dbg["d_dtm"], fr(o_sp, NT), reads=("sp0",))
            def ab(e):
                return e.scalar_tensor_tensor(out=fr(o_sp + 1024, NT),
                                              in0=fr(o_sp, NT), scalar=-1.0,
                                              in1=fr(o_sp, NT),
                                              op0=ALU.mult, op1=ALU.max)
            b.op("vector", ab, reads=("sp0",), writes=("sp1",))
            def e1(e):
                return e.activation(out=fr(o_E, NT), in_=fr(o_sp + 1024, NT),
                                    func=AF.Exp, scale=-1.0)
            b.op("scalar", e1, reads=("sp1",), writes=("E0", "E1"))
            def l1(e):
                return e.activation(out=fr(o_sp + 1024, NT), in_=fr(o_E, NT),
                                    func=AF.Ln, bias=1.0)
            b.op("scalar", l1, reads=("E0", "E1"), writes=("sp1",))
            def rl(e):
                return e.tensor_scalar(out=fr(o_E, NT), in0=fr(o_sp, NT),
                                       scalar1=0.0, scalar2=None, op0=ALU.max)
            b.op("vector", rl, reads=("sp0",), writes=("E0", "E1"))
            def dl(e, m=m):
                return e.tensor_tensor(out=fr(o_delta + m * NT, NT),
                                       in0=fr(o_sp + 1024, NT),
                                       in1=fr(o_E, NT), op=ALU.add)
            b.op("vector", dl, reads=("sp1", "E0", "E1"), writes=(f"dl{m}",))
            if DEBUG and ((not rev and c == 0) or (rev and c == NSC - 1)) and m == 0:
                b.dma(dbg["d_delta"], fr(o_delta + m * NT, NT), reads=(f"dl{m}",))
            def du_(e, m=m, par=par):
                return e.tensor_tensor(out=fr(o_du + m * NT, NT),
                                       in0=fr(o_delta + m * NT, NT),
                                       in1=u_sl(par, m), op=ALU.mult)
            b.op("gpsimd", du_, reads=(f"dl{m}", f"u{par}_{m}"), writes=(f"du{m}",))
        st_bases, st_idx = (ST_R, stR) if rev else (ST_F, stF)
        fkeys = ("stR0", "stR1") if rev else ("stF0", "stF1")
        for half_i in range(2):
            half = (1 - half_i) if rev else half_i
            first = half_i == 0
            hs = slice(half * 512, (half + 1) * 512)
            for n in range(NS):
                rp = n % 2
                def brep(e, n=n, hs=hs):
                    return e.matmul(ps(6), selb_t(n),
                                    fb(o_bctb, NT)[0:32, hs], start=True, stop=True)
                b.op("tensor", brep, reads=("selb", "bctb"), writes=(pk(6),))
                def crep(e, n=n, hs=hs):
                    return e.matmul(ps(7), selb_t(16 + n),
                                    fb(o_bctb, NT)[0:32, hs], start=True, stop=True)
                b.op("tensor", crep, reads=("selb", "bctb"), writes=(pk(7),))
                def bev(e, rp=rp):
                    return e.activation(out=fr(o_Ball + rp * 512, 512), in_=ps(6),
                                        func=AF.Copy)
                b.op("scalar", bev, reads=(pk(6),), writes=(f"Ba{rp}",))
                def cev(e, rp=rp):
                    return e.activation(out=fr(o_Call + rp * 512, 512), in_=ps(7),
                                        func=AF.Copy)
                b.op("scalar", cev, reads=(pk(7),), writes=(f"Ca{rp}",))
                if DEBUG and not rev and c == 0 and n == 0 and half == 0:
                    b.dma(dbg["d_Ball"], fr(o_Ball + rp * 512, 512), reads=(f"Ba{rp}",))
                for m in range(S):
                    lane = m * 16 + n
                    ep = (n * S + m) % 2
                    es = m
                    e_off = o_E + es * 512 if es < 2 else o_xdb + (es - 2) * 512
                    e_keys = (f"E{es}",) if es < 2 else (f"E{es}", "xdb")
                    bs = m % 3
                    b_off = o_bh + bs * 512 if bs < 2 else o_dtrb
                    b_keys = (f"bh{bs}",) if bs < 2 else (f"bh{bs}", "dtrb")
                    def E_(e, m=m, n=n, hs=hs, e_off=e_off, dir_=dir_):
                        return e.activation(out=fr(e_off, 512),
                                            in_=fr(o_delta + m * NT, NT)[:, hs],
                                            func=AF.Exp, scale=acol(dir_, m, n))
                    b.op("scalar", E_, reads=(f"dl{m}", "ac"), writes=e_keys)
                    def bh(e, m=m, hs=hs, b_off=b_off, rp=rp):
                        return e.tensor_tensor(out=fr(b_off, 512),
                                               in0=fr(o_du + m * NT, NT)[:, hs],
                                               in1=fr(o_Ball + rp * 512, 512),
                                               op=ALU.mult)
                    b.op("gpsimd", bh, reads=(f"du{m}", f"Ba{rp}"), writes=b_keys)
                    if first:
                        init_ap = stcol(st_bases[st_idx[0]], lane)
                        ikey = fkeys[st_idx[0]]
                    else:
                        init_ap = stcol(ST_M, lane)
                        ikey = "stM"
                    def scan(e, ep=ep, e_off=e_off, b_off=b_off,
                             init_ap=init_ap, rev=rev):
                        oh = fr(o_h + ep * 512, 512)
                        d0 = fr(e_off, 512)
                        d1 = fr(b_off, 512)
                        if rev:
                            oh, d0, d1 = oh[:, ::-1], d0[:, ::-1], d1[:, ::-1]
                        return e.tensor_tensor_scan(out=oh, data0=d0, data1=d1,
                                                    initial=init_ap,
                                                    op0=ALU.mult, op1=ALU.add)
                    b.op("vector", scan, reads=e_keys + b_keys + (ikey,),
                         writes=(f"h{ep}",))
                    ecol = 0 if rev else 511
                    if first:
                        dst, dkey = stcol(ST_M, lane), "stM"
                    else:
                        dst, dkey = stcol(st_bases[1 - st_idx[0]], lane), fkeys[1 - st_idx[0]]
                    def stx(e, ep=ep, dst=dst, ecol=ecol):
                        return e.tensor_copy(out=dst,
                                             in_=fr(o_h + ep * 512, 512)[:, ecol:ecol + 1])
                    b.op("vector", stx, reads=(f"h{ep}",), writes=(dkey,))
                    def q_(e, ep=ep, rp=rp):
                        return e.tensor_tensor(out=fb(o_q + ep * 256, 512),
                                               in0=fr(o_h + ep * 512, 512),
                                               in1=fr(o_Call + rp * 512, 512),
                                               op=ALU.mult)
                    b.op("vector", q_, reads=(f"h{ep}", f"Ca{rp}"), writes=(f"q{ep}",))
                    if DEBUG and ((not rev and c == 0) or (rev and c == NSC - 1)) \
                            and m == 0 and n == 0 and half == (1 if rev else 0):
                        b.dma(dbg["d_E"], fr(e_off, 512), reads=e_keys)
                        b.dma(dbg["d_bh"], fr(b_off, 512), reads=b_keys)
                        b.dma(dbg["d_h"], fr(o_h + ep * 512, 512), reads=(f"h{ep}",))
                    def nsum(e, m=m, n=n, ep=ep):
                        return e.matmul(ps(2 + m), id_b(),
                                        fb(o_q + ep * 256, 512),
                                        start=(n == 0), stop=(n == NS - 1))
                    b.op("tensor", nsum, reads=(f"q{ep}", "idb"), writes=(pk(2 + m),))
            for m in range(S):
                gp_p = m % 2
                def gpre(e, m=m, hs=hs, par=par, dir_=dir_, gp_p=gp_p):
                    return e.scalar_tensor_tensor(
                        out=fr(o_gp + gp_p * 512, 512),
                        in0=u_sl(par, m)[:, hs],
                        scalar=dvcol(dir_, m),
                        in1=ps(2 + m), op0=ALU.mult, op1=ALU.add)
                b.op("vector", gpre, reads=(f"u{par}_{m}", "dv", pk(2 + m)),
                     writes=(f"gp{gp_p}",))
                if not rev:
                    b.dma(sc_yf[c, m][:, hs], fr(o_gp + gp_p * 512, 512),
                          reads=(f"gp{gp_p}",), writes=("scyf",))
                    if DEBUG and c == 0 and m == 0:
                        b.dma(dbg["d_yf"][:, hs], fr(o_gp + gp_p * 512, 512),
                              reads=(f"gp{gp_p}",))
                else:
                    ksz = f"szr{gp_p}"
                    b.dma(fr(o_szr + gp_p * 512, 512), sc_sz[c, m][:, hs],
                          reads=("scsz", "revfence", ksz), writes=(ksz,))
                    if DEBUG and c == NSC - 1 and m == 0:
                        b.dma(dbg["d_yr"][:, hs], fr(o_gp + gp_p * 512, 512),
                              reads=(f"gp{gp_p}",))
                    def add1(e, m=m, hs=hs, gp_p=gp_p):
                        return e.tensor_tensor(out=fr(o_gp + gp_p * 512, 512),
                                               in0=fr(o_gp + gp_p * 512, 512),
                                               in1=fr(o_yf + m * NT, NT)[:, hs],
                                               op=ALU.add)
                    b.op("gpsimd", add1, reads=(f"yf{m}", f"gp{gp_p}"),
                         writes=(f"gp{gp_p}",))
                    def gmul(e, m=m, hs=hs, gp_p=gp_p):
                        return e.tensor_tensor(
                            out=fb(o_ymix + m * 512, NT)[:, hs],
                            in0=fr(o_gp + gp_p * 512, 512),
                            in1=fr(o_szr + gp_p * 512, 512), op=ALU.mult)
                    b.op("gpsimd", gmul, reads=(f"gp{gp_p}", ksz, "revfence"),
                         writes=(f"ym{m}",))
        st_idx[0] = 1 - st_idx[0]
        if rev:
            for dmt in range(8):
                pb = 2 + 2 * (dmt % 2)
                for half in range(2):
                    for s_ in range(S):
                        def op_(e, dmt=dmt, half=half, s_=s_, pb=pb):
                            return e.matmul(ps(pb + half), wout_t(s_, dmt),
                                            fb(o_ymix + s_ * 512, NT)[:, half * 512:(half + 1) * 512],
                                            start=(s_ == 0), stop=(s_ == S - 1))
                        b.op("tensor", op_, reads=("wout", f"ym{s_}"),
                             writes=(pk(pb + half),))
                kop = f"op{dmt % 2}"
                def opev(e, dmt=dmt, pb=pb):
                    return e.activation(out=fr(o_ops + (dmt % 2) * 1024, NT),
                                        in_=ps(pb, 1024), func=AF.Copy)
                b.op("scalar", opev, reads=(pk(pb), pk(pb + 1), "revfence"),
                     writes=(kop,))
                b.dma(op_in[c % 2, c // 2, dmt], fr(o_ops + (dmt % 2) * 1024, NT),
                      reads=(kop,), writes=("opw",))
                if DEBUG and c == NSC - 1 and dmt == 0:
                    b.dma(dbg["d_op"], fr(o_ops + (dmt % 2) * 1024, NT), reads=(kop,))

    emit_A_fwd(0)
    emit_dequant_tile(0)
    emit_dequant_tile(1)
    next_dq = 2
    for c in range(1, NSC):
        emit_A_fwd(c)
        for t in range(next_dq, min(next_dq + 5, 40)):
            emit_dequant_tile(t)
        next_dq = min(next_dq + 5, 40)
        emit_B(c - 1, 0)
    for t in range(next_dq, 40):
        emit_dequant_tile(t)
    emit_B(NSC - 1, 0)
    b.op("vector", lambda e: e.memset(fr(o_mu + 10, 2), 0.0),
         writes=("x0", "x1", "xl", "xlnT", "sz0", "sz1", "szr0", "szr1",
                 "revfence"))
    emit_A_rev(NSC - 1)
    for c in range(NSC - 2, -1, -1):
        emit_A_rev(c)
        emit_B(c + 1, 1)
    emit_B(0, 1)
    for j in range(2):
        def rs(e, j=j):
            return e.collective_compute(
                "ReduceScatter", ALU.add, replica_groups=GROUPS,
                ins=[op_in[j]], outs=[op_out[j]])
        b.op("gpsimd", rs, reads=("opw",), writes=(f"opo{j}",), sig_as="coll")

    fence_keys = tuple(f"u0_{m}" for m in range(S)) + tuple(f"u1_{m}" for m in range(S)) + \
        tuple(f"dl{m}" for m in range(S)) + tuple(f"du{m}" for m in range(S)) + \
        tuple(f"yf{m}" for m in range(S)) + tuple(f"ym{m}" for m in range(S)) + \
        tuple(f"xz{m}" for m in range(S)) + \
        ("xlnT", "xl", "x0", "x1", "sp0", "sp1", "xdb", "bctb",
         "fence", "Ba0", "Ba1", "Ca0", "Ca1", "E0", "E1", "E2", "E3",
         "bh0", "bh1", "bh2",
         "h0", "h1", "q0", "q1", "gp0", "gp1", "sz0", "sz1", "op0", "op1",
         "szr0", "szr1", "xdsb", "dq0", "dq1", "dq2", "ub0", "ub1", "dtrb",
         "stF0", "stF1", "stR0", "stR1", "stM")
    b.op("vector", lambda e: e.memset(fr(o_mu + 12, 2), 0.0), writes=fence_keys)
    b.dma(fr(of_gb, DM), ln_g.partition_broadcast(128), reads=("fence",),
          writes=("sz0",))
    b.dma(fr(of_gb + DM, DM), ln_b.partition_broadcast(128), reads=("fence",),
          writes=("sz1",))
    b.dma(fr(of_fb, 32), fc1_b, reads=("fence",), writes=("bctb",))
    b.dma(fr(of_fb + 32, 8), fc2_b, reads=("fence",), writes=("bctb",))
    def hb2(e):
        return e.tensor_scalar(out=fr(of_fb + 40, 8), in0=fr(of_fb + 32, 8),
                               scalar1=0.5, scalar2=None, op0=ALU.mult)
    b.op("vector", hb2, reads=("bctb",), writes=("bctb",))

    for fch in range(4):
        j = fch // 2
        off = (fch % 2) * 512
        for dmt in range(8):
            b.dma(fr(of_opsb + dmt * 512, 512), op_out[j, dmt][:, off:off + 512],
                  reads=(f"opo{j}", "fence", "opsb"), writes=("opsb",))
        for tt in range(4):
            kx = f"x{tt % 2}"
            b.dma(fr(o_x + (tt % 2) * DM, DM),
                  xq_in[fch * 512 + tt * 128:fch * 512 + (tt + 1) * 128, :],
                  reads=("fence",), writes=(kx,))
            for dmt in range(8):
                bnk = dmt % 2
                def tr(e, dmt=dmt, tt=tt, bnk=bnk):
                    return e.transpose(ps(bnk, 128),
                                       fr(of_opsb + dmt * 512, 512)[:, tt * 128:(tt + 1) * 128],
                                       fr(o_idf, 128))
                b.op("tensor", tr, reads=("opsb", "idf"), writes=(pk(bnk),))
                def ev(e, dmt=dmt, bnk=bnk):
                    return e.activation(out=fr(o_xln, DM)[:, dmt * 128:(dmt + 1) * 128],
                                        in_=ps(bnk, 128), func=AF.Copy)
                b.op("scalar", ev, reads=(pk(bnk),), writes=("xl",))
            def addx(e, tt=tt):
                return e.tensor_tensor(out=fr(of_x1 + tt * DM, DM),
                                       in0=fr(o_xln, DM),
                                       in1=fr(o_x + (tt % 2) * DM, DM),
                                       op=ALU.add)
            b.op("vector", addx, reads=("xl", kx), writes=("fx1",))
            if DEBUG and fch == 0 and tt == 0:
                b.dma(dbg["d_x1"], fr(of_x1 + tt * DM, DM), reads=("fx1",))
        for tt in range(4):
            ln_stats(fr(of_x1 + tt * DM, DM), "fx1")
            def n1(e, tt=tt):
                return e.tensor_scalar(out=fr(of_xl2 + tt * DM, DM),
                                       in0=fr(of_x1 + tt * DM, DM),
                                       scalar1=AR[:, o_mu + 6:o_mu + 7],
                                       scalar2=AR[:, o_mu + 7:o_mu + 8],
                                       op0=ALU.mult, op1=ALU.add)
            b.op("vector", n1, reads=("fx1", "mu"), writes=("fxl",))
            ln_stats(fr(of_xl2 + tt * DM, DM), "fxl")
            def n2(e, tt=tt):
                return e.tensor_scalar(out=fr(o_sp, DM), in0=fr(of_xl2 + tt * DM, DM),
                                       scalar1=AR[:, o_mu + 6:o_mu + 7],
                                       scalar2=AR[:, o_mu + 7:o_mu + 8],
                                       op0=ALU.mult, op1=ALU.add)
            b.op("vector", n2, reads=("fxl", "mu"), writes=("sp0",))
            def gmul(e):
                return e.tensor_tensor(out=fr(o_sp + 1024, DM), in0=fr(o_sp, DM),
                                       in1=fr(of_gb, DM), op=ALU.mult)
            b.op("vector", gmul, reads=("sp0", "sz0"), writes=("sp1",))
            def badd(e):
                return e.tensor_tensor(out=fr(o_sp, DM), in0=fr(o_sp + 1024, DM),
                                       in1=fr(of_gb + DM, DM), op=ALU.add)
            b.op("vector", badd, reads=("sp1", "sz1"), writes=("sp0",))
            for k in range(8):
                bnk = k % 2
                def tr(e, k=k, bnk=bnk):
                    return e.transpose(ps(bnk, 128),
                                       fr(o_sp, DM)[:, k * 128:(k + 1) * 128],
                                       fr(o_idf, 128))
                b.op("tensor", tr, reads=("sp0", "idf"), writes=(pk(bnk),))
                def ev(e, tt=tt, k=k, bnk=bnk):
                    return e.activation(
                        out=fb(of_h2T, 8 * 512)[:, k * 512 + tt * 128:k * 512 + (tt + 1) * 128],
                        in_=ps(bnk, 128), func=AF.Copy)
                b.op("scalar", ev, reads=(pk(bnk),), writes=("h2T",))
        h2T_k = lambda k: fb(of_h2T, 8 * 512)[:, k * 512:(k + 1) * 512]
        for Ft in range(32):
            kw = f"w1b{Ft % 2}"
            b.dma(fb(of_w1b + (Ft % 2) * 512, 8 * 128).rearrange("p (k m) -> p k m", k=8),
                  sc_w1[:, :, Ft * 128:(Ft + 1) * 128],
                  reads=("wsc", "fence", kw), writes=(kw,))
            pb = 2 + (Ft % 2)
            for k in range(8):
                def mm_(e, Ft=Ft, k=k, pb=pb):
                    wt = fb(of_w1b + (Ft % 2) * 512, 8 * 128)[:, k * 128:(k + 1) * 128]
                    return e.matmul(ps(pb), wt, h2T_k(k), start=(k == 0), stop=(k == 7))
                b.op("tensor", mm_, reads=(kw, "h2T"), writes=(pk(pb),))
            def sl(e, Ft=Ft, pb=pb):
                return e.activation(out=fb(of_y1, 32 * 512)[:, Ft * 512:(Ft + 1) * 512],
                                    in_=ps(pb), func=AF.Silu,
                                    bias=AR[:, of_fb + Ft:of_fb + Ft + 1])
            b.op("scalar", sl, reads=("bctb", "fence", pk(pb)), writes=("y1",))
        for dm in range(8):
            kw = f"w2b{dm % 2}"
            b.dma(fb(of_w2b + (dm % 2) * 2048, 32 * 128).rearrange("p (k m) -> p k m", k=32),
                  sc_w2[:, :, dm * 128:(dm + 1) * 128],
                  reads=("wsc", "fence", kw), writes=(kw,))
            pb = 4 + (dm % 2)
            for k in range(32):
                def mm_(e, dm=dm, k=k, pb=pb):
                    wt = fb(of_w2b + (dm % 2) * 2048, 32 * 128)[:, k * 128:(k + 1) * 128]
                    y1k = fb(of_y1, 32 * 512)[:, k * 512:(k + 1) * 512]
                    return e.matmul(ps(pb), wt, y1k, start=(k == 0), stop=(k == 31))
                b.op("tensor", mm_, reads=(kw, "y1"), writes=(pk(pb),))
            def hv(e, dm=dm, pb=pb):
                return e.tensor_scalar(out=fr(z2sl(dm), 512), in0=ps(pb),
                                       scalar1=0.5,
                                       scalar2=AR[:, of_fb + 40 + dm:of_fb + 41 + dm],
                                       op0=ALU.mult, op1=ALU.add)
            b.op("vector", hv, reads=("bctb", "fence", pk(pb)), writes=("z2",))
        for tt in range(4):
            for dm in range(8):
                bnk = dm % 2
                def tr(e, dm=dm, tt=tt, bnk=bnk):
                    return e.transpose(ps(bnk, 128),
                                       fr(z2sl(dm), 512)[:, tt * 128:(tt + 1) * 128],
                                       fr(o_idf, 128))
                b.op("tensor", tr, reads=("z2", "idf"), writes=(pk(bnk),))
                def zev(e, dm=dm, bnk=bnk):
                    return e.activation(out=fr(o_sp + dm * 128, 128),
                                        in_=ps(bnk, 128), func=AF.Copy)
                b.op("scalar", zev, reads=(pk(bnk),), writes=("sp0",))
            def f1(e, tt=tt):
                return e.tensor_tensor(out=fr(o_sp + 1024, DM),
                                       in0=fr(of_x1 + tt * DM, DM),
                                       in1=fr(of_xl2 + tt * DM, DM), op=ALU.add)
            b.op("vector", f1, reads=("fx1", "fxl"), writes=("sp1",))
            def f2(e, tt=tt):
                return e.tensor_tensor(out=fr(o_xln, DM), in0=fr(o_sp + 1024, DM),
                                       in1=fr(o_sp, DM), op=ALU.add)
            b.op("vector", f2, reads=("sp0", "sp1"), writes=("xl",))
            trow = fch * 512 + tt * 128
            b.dma(out[trow:trow + 128, :], fr(o_xln, DM), reads=("xl",),
                  writes=("outw",))

    b.emit("sync", lambda e: e.nop(),
           waits=[("sync", b.cnt["sync"]), ("gpsimd", b.cnt["gpsimd"]),
                  ("coll", b.cnt.get("coll", 0))], sig=False)

    with nc.Block() as block:
        b.replay(block, sems)
    es.close()
    return nc


def kernel(**inputs):
    import ml_dtypes
    inputs = {k: np.asarray(v) for k, v in inputs.items()}
    nc = build_program()

    selb = np.zeros((32, 32 * 128), np.float32)
    for n in range(32):
        selb[n, n * 128:(n + 1) * 128] = 1.0
    selb = selb.astype(ml_dtypes.bfloat16)

    win = inputs["L0_attn_in_proj_w"]
    wop = inputs["L0_attn_out_proj_w"]
    xpf = inputs["L0_attn_x_proj_w"]
    xpr = inputs["L0_attn_x_proj_w_b"]
    dtf = inputs["L0_attn_dt_proj_w"]
    dtr = inputs["L0_attn_dt_proj_w_b"]

    def chsl(a, q, axis=0):
        sl = [slice(None)] * a.ndim
        sl[axis] = slice(512 * q, 512 * (q + 1))
        return np.ascontiguousarray(a[tuple(sl)])

    def pmaj(a, q):
        a = chsl(a, q)
        if a.ndim == 1:
            return np.ascontiguousarray(a.reshape(4, 128).T)
        k = a.shape[1]
        return np.ascontiguousarray(
            a.reshape(4, 128, k).transpose(1, 0, 2).reshape(128, 4 * k))

    shared = {
        "selb": selb,
        "id128f": np.eye(128, dtype=np.float32),
        "fc1_q": np.ascontiguousarray(inputs["L0_ffn_fc1_q"]),
        "fc1_sc": np.ascontiguousarray(inputs["L0_ffn_fc1_scales"]),
        "fc1_b": np.ascontiguousarray(inputs["L0_ffn_fc1_bias"].reshape(32, 128).T),
        "fc2_q": np.ascontiguousarray(inputs["L0_ffn_fc2_q"]),
        "fc2_sc": np.ascontiguousarray(inputs["L0_ffn_fc2_scales"]),
        "fc2_b": np.ascontiguousarray(inputs["L0_ffn_fc2_bias"].reshape(8, 128).T),
        "ln_g": np.ascontiguousarray(inputs["L0_ffn_ln_gamma"]),
        "ln_b": np.ascontiguousarray(inputs["L0_ffn_ln_beta"]),
    }
    in_maps = []
    for cc in range(8):
        bq, q = divmod(cc, 4)
        x = np.ascontiguousarray(inputs["L0_attn_hidden_states"][bq])
        m = dict(shared)
        m.update({
            "x_in": x,
            "xq_in": np.ascontiguousarray(x[OWN * q:OWN * (q + 1)]),
            "winx": np.ascontiguousarray(chsl(win[:DI], q).T),
            "winz": np.ascontiguousarray(chsl(win[DI:], q).T),
            "wout": np.ascontiguousarray(chsl(wop.T, q)),
            "xpw": np.stack([chsl(np.ascontiguousarray(xpf.T), q),
                             chsl(np.ascontiguousarray(xpr.T), q)], 0),
            "dtw": np.stack([np.ascontiguousarray(chsl(dtf, q).T),
                             np.ascontiguousarray(chsl(dtr, q).T)], 0),
            "convw": np.concatenate([pmaj(inputs["L0_attn_conv_w"][:, 0, :], q),
                                     pmaj(inputs["L0_attn_conv_w_b"][:, 0, :], q)], 1),
            "convb": np.concatenate([pmaj(inputs["L0_attn_conv_b"], q),
                                     pmaj(inputs["L0_attn_conv_b_b"], q)], 1),
            "dtb": np.concatenate([pmaj(inputs["L0_attn_dt_proj_b"], q),
                                   pmaj(inputs["L0_attn_dt_proj_b_b"], q)], 1),
            "alog": np.concatenate([pmaj(inputs["L0_attn_A_log"], q),
                                    pmaj(inputs["L0_attn_A_b_log"], q)], 1),
            "dvec": np.concatenate([pmaj(inputs["L0_attn_D"], q),
                                    pmaj(inputs["L0_attn_D_b"], q)], 1),
        })
        in_maps.append(m)

    res = run_bass_kernel_spmd(nc, in_maps, core_ids=list(range(8)))
    outp = np.empty((2, L, DM), np.float32)
    for cc in range(8):
        bq, q = divmod(cc, 4)
        outp[bq, OWN * q:OWN * (q + 1)] = np.asarray(res.results[cc]["out"])
    if DEBUG:
        kernel.last_results = res
    return outp


# version 16
# speedup vs baseline: 1.4014x; 1.0022x over previous
import os
import sys
import numpy as np

sys.path.insert(0, "/opt/trn_rl_repo")
import concourse.bass as bass
import concourse.mybir as mybir
from concourse.bass_utils import run_bass_kernel_spmd

F32 = mybir.dt.float32
BF16 = mybir.dt.bfloat16
I32 = mybir.dt.int32
AF = mybir.ActivationFunctionType
ALU = mybir.AluOpType

DM = 1024
DI = 2048
CHQ = 512
S = 4
NS = 16
RK = 64
L = 8192
NT = 1024
NSC = 8
OWN = 2048
EPS = 1e-5
DEBUG = bool(int(os.environ.get("BASSMK_DEBUG", "0")))

ENGINES = ("sync", "tensor", "scalar", "vector", "gpsimd")
GROUPS = [[0, 1, 2, 3], [4, 5, 6, 7]]


class Buf:
    def __init__(self):
        self.w = None
        self.readers = {}

    def wr(self):
        return [(e, v) for e, v in self.readers.items()] + \
            ([self.w] if self.w is not None else [])

    def setw(self, tok):
        self.w = tok
        self.readers = {}

    def rd(self):
        return [self.w] if self.w is not None else []

    def setr(self, tok):
        if tok is not None:
            e, v = tok
            if self.readers.get(e, 0) < v:
                self.readers[e] = v


class Bld:
    def __init__(self):
        self.ops = {e: [] for e in ENGINES}
        self.cnt = {e: 0 for e in ENGINES}
        self.bufs = {}

    def B(self, key):
        if key not in self.bufs:
            self.bufs[key] = Buf()
        return self.bufs[key]

    def emit(self, eng, fn, waits=(), sig=True, sig_as=None):
        ws = []
        for t in waits:
            if t is None:
                continue
            pe, v = t
            if pe == eng and sig_as is None:
                if pe == "tensor":
                    continue
                if pe != "sync" and self.cnt[eng] + 1 - v > 4:
                    continue
            ws.append((pe, v))
        if sig:
            sname = sig_as or eng
            self.cnt.setdefault(sname, 0)
            self.cnt[sname] += 16 if sname == "sync" else 1
            tok = (sname, self.cnt[sname])
        else:
            sname = None
            tok = None
        self.ops[eng].append((fn, ws, sname))
        return tok

    def op(self, eng, fn, reads=(), writes=(), sig_as=None):
        waits = []
        for k in reads:
            waits += self.B(k).rd()
        for k in writes:
            waits += self.B(k).wr()
        tok = self.emit(eng, fn, waits=waits, sig_as=sig_as)
        for k in reads:
            self.B(k).setr(tok)
        for k in writes:
            self.B(k).setw(tok)
        return tok

    def dma(self, out, in_, reads=(), writes=()):
        def fn(e):
            return e.dma_start(out=out, in_=in_)
        return self.op("sync", fn, reads=reads, writes=writes)

    def replay(self, block, sems):
        for eng_name in ENGINES:
            ops = self.ops[eng_name]
            if not ops:
                continue

            def make_body(ename, olist):
                def body(eng):
                    seen = {}
                    cum = 0
                    for fn, ws, sname in olist:
                        for pe, v in ws:
                            if seen.get(pe, 0) >= v:
                                continue
                            seen[pe] = v
                            eng.wait_ge(sems[pe], v)
                        inst = fn(eng)
                        if sname:
                            inst.then_inc(sems[sname], 16 if sname == "sync" else 1)
                        if sname == "sync":
                            cum += 16
                            eng.wait_ge(sems["sync"], cum)
                            seen["sync"] = cum
                return body
            getattr(block, eng_name)(make_body(eng_name, ops))


def build_program():
    nc = bass.Bass("TRN2", target_bir_lowering=False)

    def din(name, shape, dt=F32):
        return nc.dram_tensor(name, shape, dt, kind="ExternalInput").ap()

    x_in = din("x_in", [L, DM])
    xq_in = din("xq_in", [OWN, DM])
    winx = din("winx", [DM, CHQ])
    winz = din("winz", [DM, CHQ])
    wout = din("wout", [CHQ, DM])
    xpw = din("xpw", [2, CHQ, 96])
    dtw = din("dtw", [2, RK, CHQ])
    convw = din("convw", [128, 2 * S * 4])
    convb = din("convb", [128, 2 * S])
    dtb = din("dtb", [128, 2 * S])
    alog = din("alog", [128, 2 * S * NS])
    dvec = din("dvec", [128, 2 * S])
    selb = din("selb", [32, 32 * 128], BF16)
    id128f = din("id128f", [128, 128])
    fc1_q = din("fc1_q", [DM, 4 * DM], I32)
    fc1_sc = din("fc1_sc", [8, 4 * DM])
    fc1_b = din("fc1_b", [128, 32])
    fc2_q = din("fc2_q", [4 * DM, DM], I32)
    fc2_sc = din("fc2_sc", [32, DM])
    fc2_b = din("fc2_b", [128, 8])
    ln_g = din("ln_g", [DM])
    ln_b = din("ln_b", [DM])

    out = nc.dram_tensor("out", [OWN, DM], F32, kind="ExternalOutput").ap()

    def dscr(name, shape, dt=F32):
        return nc.dram_tensor(name, shape, dt, kind="Internal").ap()

    sc_xz = dscr("sc_xz", [NSC, S, 128, NT])
    sc_sz = dscr("sc_sz", [NSC, S, 128, NT])
    sc_yf = dscr("sc_yf", [NSC, S, 128, NT])
    xd_in = dscr("xd_in", [16, 96, NT])
    xd_out = dscr("xd_out", [16, 96, NT])
    op_in = dscr("op_in", [2, 4, 8, 128, NT])
    op_out = dscr("op_out", [2, 8, 128, NT])
    sc_w1 = dscr("sc_w1", [128, 8, 4 * DM], BF16)
    sc_w2 = dscr("sc_w2", [128, 32, DM], BF16)

    dbg = {}
    if DEBUG:
        for nm, shape in [("d_xln", [128, DM]), ("d_xzf", [128, NT]),
                          ("d_uf", [128, NT]), ("d_xdbl", [96, NT]),
                          ("d_delta", [128, NT]), ("d_E", [128, 512]),
                          ("d_bh", [128, 512]), ("d_h", [128, 512]),
                          ("d_yf", [128, NT]), ("d_ur", [128, NT]),
                          ("d_xdblr", [96, NT]),
                          ("d_yr", [128, NT]), ("d_op", [128, NT]),
                          ("d_x1", [128, DM]), ("d_sz", [128, NT]),
                          ("d_dtm", [128, NT]), ("d_Ball", [128, 512]),
                          ("d_dtrb", [128, NT])]:
            dbg[nm] = nc.dram_tensor(nm, shape, F32, kind="ExternalOutput").ap()

    ACOLS = 53180
    _off = [0]

    def A(n):
        o = _off[0]
        _off[0] += n
        assert _off[0] <= ACOLS, f"arena overflow at {_off[0]}"
        return o

    o_winx = A(2048)
    o_winz = A(2048)
    o_wout = A(2048)
    o_xpw = A(384)
    o_dtw = A(512)
    o_selb = A(2048)
    o_idf = A(128)
    o_idb = A(64)
    o_cw = A(32)
    o_cb = A(8)
    o_dtb = A(8)
    o_dv = A(8)
    o_ac = A(128)
    o_mu = A(16)
    o_st = A(320)
    o_x = A(2048)
    o_xln = A(1024)
    o_xlnT = A(4096)
    o_xzp = A(4112)
    o_ub = A(1024)
    o_sz = A(2048)
    o_u = A(8192)
    o_delta = A(4096)
    o_du = A(4096)
    o_xdb = A(1024)
    o_bctb = A(512)
    o_Ball = A(1024)
    o_Call = A(1024)
    o_E = A(1024)
    o_bh = A(1024)
    o_h = A(1024)
    o_q = A(512)
    o_sp = A(2048)
    o_gp = A(1024)
    o_dq = A(1536)
    o_dtrb = A(512)
    assert _off[0] <= ACOLS, _off[0]

    o_yf = o_xlnT
    o_szr = o_x
    o_ymix = o_x + 1024
    o_ops = o_sz
    of_x1 = o_u
    of_xl2 = o_u + 4096
    of_h2T = o_delta
    of_w1b = o_delta + 2048
    of_w2b = o_delta + 3072
    of_y1 = o_xlnT
    of_opsb = o_Ball
    of_gb = o_sz
    of_fb = o_bctb

    def z2sl(dm):
        offs = [o_gp, o_gp + 512, o_dq, o_dq + 512, o_dq + 1024,
                o_ub, o_ub + 512, o_xdb]
        return offs[dm]

    b = Bld()

    import contextlib
    es = contextlib.ExitStack()
    AR = es.enter_context(nc.sbuf_tensor("AR", [128, ACOLS], F32))
    PSF = es.enter_context(nc.psum_tensor("PSF", [128, 4096], F32))
    sems = {e: es.enter_context(nc.semaphore("sem_" + e))
            for e in ENGINES + ("coll",)}

    def fr(off, n):
        return AR[:, off:off + n]

    def fb(off, n_bf16):
        return AR[:, off:off + (n_bf16 + 1) // 2].bitcast(BF16)[:, 0:n_bf16]

    def ps(bank, n=512):
        return PSF[:, bank * 512:bank * 512 + n]

    pk = lambda bank: f"ps{bank}"

    b.op("vector", lambda e: e.memset(AR[:, o_mu + 8:o_mu + 9], EPS), writes=("eps",))
    b.op("vector", lambda e: e.memset(fr(o_st, 320), 0.0),
         writes=("stF0", "stF1", "stR0", "stR1", "stM"))

    b.dma(fb(o_selb, 4096)[0:32, :], selb, writes=("selb",))
    b.dma(fr(o_idf, 128), id128f, writes=("idf",))
    b.dma(fr(o_cw, 32), convw, writes=("cw",))
    b.dma(fr(o_cb, 8), convb, writes=("cb",))
    b.dma(fr(o_dtb, 8), dtb, writes=("dtbk",))
    b.dma(fr(o_dv, 8), dvec, writes=("dv",))
    b.dma(fr(o_sp, 128), alog, writes=("sp0",))
    def mkA(e):
        return e.activation(out=fr(o_ac, 128), in_=fr(o_sp, 128), func=AF.Exp)
    b.op("scalar", mkA, reads=("sp0",), writes=("ac",))
    def negA(e):
        return e.tensor_scalar(out=fr(o_ac, 128), in0=fr(o_ac, 128),
                               scalar1=-1.0, scalar2=None, op0=ALU.mult)
    b.op("vector", negA, reads=(), writes=("ac",))
    def mkidb(e):
        return e.activation(out=fb(o_idb, 128), in_=fr(o_idf, 128), func=AF.Copy)
    b.op("scalar", mkidb, reads=("idf",), writes=("idb",))

    def stage_resident(srcs, ncols, dst_off, key):
        half = ncols // 2
        for hh in range(2):
            b.dma(fr(o_u, half), srcs[hh], writes=("u0_0", "u0_1", "u0_2", "u0_3"))
            def cast(e, hh=hh, half=half, dst_off=dst_off, ncols=ncols):
                return e.activation(out=fb(dst_off, ncols)[:, hh * half:(hh + 1) * half],
                                    in_=fr(o_u, half), func=AF.Copy)
            b.op("scalar", cast, reads=("u0_0",), writes=(key,))

    winx_v = winx.rearrange("(k p) m -> p k m", p=128)
    winz_v = winz.rearrange("(k p) m -> p k m", p=128)
    wout_v = wout.rearrange("(s p) m -> p s m", p=128)
    stage_resident([winx_v[:, 0:4, :], winx_v[:, 4:8, :]], 4096, o_winx, "winx")
    stage_resident([winz_v[:, 0:4, :], winz_v[:, 4:8, :]], 4096, o_winz, "winz")
    stage_resident([wout_v[:, 0:2, :], wout_v[:, 2:4, :]], 4096, o_wout, "wout")
    b.dma(fr(o_u, 768), xpw.rearrange("d (s p) r -> p d s r", p=128),
          writes=("u0_0", "u0_1", "u0_2", "u0_3"))
    def cxp(e):
        return e.activation(out=fb(o_xpw, 768), in_=fr(o_u, 768), func=AF.Copy)
    b.op("scalar", cxp, reads=("u0_0",), writes=("xpw",))
    b.dma(fr(o_u, 1024)[0:64, :], dtw.rearrange("d r m -> r d m"),
          writes=("u0_0", "u0_1", "u0_2", "u0_3"))
    def cdt(e):
        return e.activation(out=fb(o_dtw, 1024)[0:64, :], in_=fr(o_u, 1024)[0:64, :],
                            func=AF.Copy)
    b.op("scalar", cdt, reads=("u0_0",), writes=("dtw",))

    winx_t = lambda k, m: fb(o_winx, 8 * CHQ)[:, k * CHQ + m * 128:k * CHQ + (m + 1) * 128]
    winz_t = lambda k, m: fb(o_winz, 8 * CHQ)[:, k * CHQ + m * 128:k * CHQ + (m + 1) * 128]
    wout_t = lambda s, d: fb(o_wout, S * DM)[:, s * DM + d * 128:s * DM + (d + 1) * 128]
    xpw_t = lambda dir_, m: fb(o_xpw, 768)[:, dir_ * 384 + m * 96:dir_ * 384 + (m + 1) * 96]
    dtw_t = lambda dir_, m: fb(o_dtw, 1024)[0:64, dir_ * 512 + m * 128:dir_ * 512 + (m + 1) * 128]
    selb_t = lambda j: fb(o_selb, 4096)[0:32, j * 128:(j + 1) * 128]
    id_b = lambda: fb(o_idb, 128)
    wcol = lambda dir_, m, j: AR[:, o_cw + dir_ * 16 + m * 4 + j:o_cw + dir_ * 16 + m * 4 + j + 1]
    cbcol = lambda dir_, m: AR[:, o_cb + dir_ * 4 + m:o_cb + dir_ * 4 + m + 1]
    dtbcol = lambda dir_, m: AR[:, o_dtb + dir_ * 4 + m:o_dtb + dir_ * 4 + m + 1]
    dvcol = lambda dir_, m: AR[:, o_dv + dir_ * 4 + m:o_dv + dir_ * 4 + m + 1]
    acol = lambda dir_, m, n: AR[:, o_ac + dir_ * 64 + m * 16 + n:o_ac + dir_ * 64 + m * 16 + n + 1]
    stcol = lambda base, lane: AR[:, o_st + base + lane:o_st + base + lane + 1]
    ST_F = (0, 64)
    ST_R = (128, 192)
    ST_M = 256
    xzp_sl = lambda m: AR[:, o_xzp + m * 1028:o_xzp + m * 1028 + 1028]
    u_sl = lambda par, m: fr(o_u + par * 4096 + m * NT, NT)
    xlnT_k = lambda k: fb(o_xlnT, 8 * NT)[:, k * NT:(k + 1) * NT]

    def ln_stats(src_ap, src_key):
        def s1(e):
            return e.activation(out=fr(o_sp, DM), in_=src_ap, func=AF.Copy,
                                accum_out=AR[:, o_mu:o_mu + 1])
        b.op("scalar", s1, reads=(src_key,), writes=("sp0", "mu"))
        def s2(e):
            return e.activation(out=fr(o_sp, DM), in_=src_ap, func=AF.Square,
                                accum_out=AR[:, o_mu + 1:o_mu + 2])
        b.op("scalar", s2, reads=(src_key,), writes=("sp0", "mu"))
        def mstat(e):
            return e.tensor_scalar(out=AR[:, o_mu + 2:o_mu + 4],
                                   in0=AR[:, o_mu:o_mu + 2], scalar1=1.0 / DM,
                                   scalar2=None, op0=ALU.mult)
        b.op("vector", mstat, reads=(), writes=("mu",))
        def musq(e):
            return e.activation(out=AR[:, o_mu + 9:o_mu + 10],
                                in_=AR[:, o_mu + 2:o_mu + 3], func=AF.Square)
        b.op("scalar", musq, reads=("mu",), writes=("mu",))
        def nvar(e):
            return e.tensor_tensor(out=AR[:, o_mu + 4:o_mu + 5],
                                   in0=AR[:, o_mu + 9:o_mu + 10],
                                   in1=AR[:, o_mu + 3:o_mu + 4], op=ALU.subtract)
        b.op("vector", nvar, reads=(), writes=("mu",))
        def lnv(e):
            return e.activation(out=AR[:, o_mu + 5:o_mu + 6],
                                in_=AR[:, o_mu + 4:o_mu + 5], func=AF.Ln,
                                scale=-1.0, bias=AR[:, o_mu + 8:o_mu + 9])
        b.op("scalar", lnv, reads=("mu", "eps"), writes=("mu",))
        def rstd(e):
            return e.activation(out=AR[:, o_mu + 6:o_mu + 7],
                                in_=AR[:, o_mu + 5:o_mu + 6], func=AF.Exp,
                                scale=-0.5)
        b.op("scalar", rstd, reads=(), writes=("mu",))
        def nb(e):
            return e.tensor_scalar(out=AR[:, o_mu + 7:o_mu + 8],
                                   in0=AR[:, o_mu + 2:o_mu + 3],
                                   scalar1=AR[:, o_mu + 6:o_mu + 7],
                                   scalar2=-1.0, op0=ALU.mult, op1=ALU.mult)
        b.op("vector", nb, reads=(), writes=("mu",))

    def emit_conv(m, dir_):
        taps = [0, 1, 2, 3] if dir_ == 0 else [3, 2, 1, 0]
        kxz = f"xz{m}"
        def c0(e, m=m, dir_=dir_, taps=taps):
            return e.tensor_scalar(out=fr(o_sp, NT), in0=xzp_sl(m)[:, 0:NT],
                                   scalar1=wcol(dir_, m, taps[0]),
                                   scalar2=cbcol(dir_, m),
                                   op0=ALU.mult, op1=ALU.add)
        b.op("vector", c0, reads=(kxz, "cw", "cb"), writes=("sp0",))
        srcdst = [("sp0", "sp1"), ("sp1", "sp0"), ("sp0", "sp1")]
        offd = [o_sp + 1024, o_sp, o_sp + 1024]
        offs_ = [o_sp, o_sp + 1024, o_sp]
        for j in range(1, 4):
            def cj(e, m=m, j=j, dir_=dir_, taps=taps):
                return e.scalar_tensor_tensor(out=fr(offd[j - 1], NT),
                                              in0=xzp_sl(m)[:, j:NT + j],
                                              scalar=wcol(dir_, m, taps[j]),
                                              in1=fr(offs_[j - 1], NT),
                                              op0=ALU.mult, op1=ALU.add)
            b.op("vector", cj, reads=(kxz, "cw", srcdst[j - 1][0]),
                 writes=(srcdst[j - 1][1],))
        return "sp1"

    def emit_u(m, par, kcv):
        ku = f"u{par}_{m}"
        def silu(e, m=m, par=par):
            return e.activation(out=u_sl(par, m), in_=fr(o_sp + 1024, NT), func=AF.Silu)
        b.op("scalar", silu, reads=(kcv,), writes=(ku,))
        kub = f"ub{m % 2}"
        def silub(e, m=m):
            return e.activation(out=fb(o_ub + (m % 2) * 512, NT),
                                in_=fr(o_sp + 1024, NT), func=AF.Silu)
        b.op("scalar", silub, reads=(kcv,), writes=(kub,))
        return ku, kub

    def emit_xp(m, dir_, kub):
        for half in range(2):
            def xp_(e, m=m, half=half, dir_=dir_):
                return e.matmul(ps(half)[0:96, :], xpw_t(dir_, m),
                                fb(o_ub + (m % 2) * 512, NT)[:, half * 512:(half + 1) * 512],
                                start=(m == 0), stop=(m == S - 1))
            b.op("tensor", xp_, reads=("xpw", kub), writes=(pk(half),))

    def emit_xp_finish(c, dir_):
        slot = dir_ * 8 + c
        def xpev(e):
            return e.activation(out=fr(o_ub, 1024)[0:96, :], in_=ps(0, 1024)[0:96, :],
                                func=AF.Copy)
        b.op("scalar", xpev, reads=(pk(0), pk(1)), writes=("ub0", "ub1", "xdsb"))
        b.dma(xd_in[slot], fr(o_ub, 1024)[0:96, :],
              reads=("xdsb", "ub0", "ub1"), writes=("xdw",))
        def coll(e, slot=slot):
            return e.collective_compute(
                "AllReduce", ALU.add, replica_groups=GROUPS,
                ins=[xd_in[slot]], outs=[xd_out[slot]])
        b.op("gpsimd", coll, reads=("xdw",), writes=(f"xdo{slot}",), sig_as="coll")

    def emit_dequant_tile(tile):
        if tile < 8:
            k, W, qd, scd, dst = tile, 4 * DM, fc1_q, fc1_sc, sc_w1
        else:
            k, W, qd, scd, dst = tile - 8, DM, fc2_q, fc2_sc, sc_w2
        for cc in range(W // 512):
            b.dma(AR[:, o_dq:o_dq + 512].bitcast(I32),
                  qd[k * 128:(k + 1) * 128, cc * 512:(cc + 1) * 512],
                  writes=("dq0",))
            b.dma(fr(o_dq + 512, 512),
                  scd[k, cc * 512:(cc + 1) * 512].partition_broadcast(128),
                  writes=("dq1",))
            def cvt(e):
                return e.activation(out=fr(o_dq + 1024, 512),
                                    in_=AR[:, o_dq:o_dq + 512].bitcast(I32),
                                    func=AF.Copy, bias=-128.0)
            b.op("scalar", cvt, reads=("dq0",), writes=("dq2",))
            def mul(e):
                return e.tensor_tensor(out=fb(o_dq, 512), in0=fr(o_dq + 1024, 512),
                                       in1=fr(o_dq + 512, 512), op=ALU.mult)
            b.op("gpsimd", mul, reads=("dq1", "dq2"), writes=("dq0",))
            b.dma(dst[:, k, cc * 512:(cc + 1) * 512], fb(o_dq, 512),
                  reads=("dq0",), writes=("wsc",))

    def emit_A_fwd(c):
        par = c % 2
        for tt in range(8):
            kx = f"x{tt % 2}"
            b.dma(fr(o_x + (tt % 2) * DM, DM),
                  x_in[c * NT + tt * 128:c * NT + (tt + 1) * 128, :], writes=(kx,))
            xa = fr(o_x + (tt % 2) * DM, DM)
            ln_stats(xa, kx)
            def norm(e, xa=xa):
                return e.tensor_scalar(out=fr(o_xln, DM), in0=xa,
                                       scalar1=AR[:, o_mu + 6:o_mu + 7],
                                       scalar2=AR[:, o_mu + 7:o_mu + 8],
                                       op0=ALU.mult, op1=ALU.add)
            b.op("vector", norm, reads=("mu", kx), writes=("xl",))
            if DEBUG and c == 0 and tt == 0:
                b.dma(dbg["d_xln"], fr(o_xln, DM), reads=("xl",))
            for k in range(8):
                bnk = k % 2
                def tr(e, k=k, bnk=bnk):
                    return e.transpose(ps(bnk, 128),
                                       fr(o_xln, DM)[:, k * 128:(k + 1) * 128],
                                       fr(o_idf, 128))
                b.op("tensor", tr, reads=("xl", "idf"), writes=(pk(bnk),))
                def evac(e, tt=tt, k=k, bnk=bnk):
                    return e.activation(
                        out=xlnT_k(k)[:, tt * 128:(tt + 1) * 128],
                        in_=ps(bnk, 128), func=AF.Copy)
                b.op("scalar", evac, reads=(pk(bnk),), writes=("xlnT",))
        for m in range(S):
            pb = 2 + 2 * (m % 2)
            for half in range(2):
                for k in range(8):
                    def mm_(e, k=k, m=m, half=half, pb=pb):
                        return e.matmul(ps(pb + half), winx_t(k, m),
                                        xlnT_k(k)[:, half * 512:(half + 1) * 512],
                                        start=(k == 0), stop=(k == 7))
                    b.op("tensor", mm_, reads=("winx", "xlnT"),
                         writes=(pk(pb + half),))
            kxz = f"xz{m}"
            if c == 0:
                b.op("vector", lambda e, m=m: e.memset(xzp_sl(m)[:, 0:3], 0.0),
                     writes=(kxz,))
            else:
                def tl(e, m=m):
                    return e.tensor_copy(out=xzp_sl(m)[:, 0:3],
                                         in_=xzp_sl(m)[:, NT:NT + 3])
                b.op("vector", tl, reads=(), writes=(kxz,))
            def xev(e, m=m, pb=pb):
                return e.activation(out=xzp_sl(m)[:, 3:NT + 3], in_=ps(pb, 1024),
                                    func=AF.Copy)
            b.op("scalar", xev, reads=(pk(pb), pk(pb + 1)), writes=(kxz,))
            b.dma(sc_xz[c, m], xzp_sl(m)[:, 3:NT + 3], reads=(kxz,), writes=("scxz",))
            if DEBUG and c == 0 and m == 0:
                b.dma(dbg["d_xzf"], xzp_sl(m)[:, 3:NT + 3], reads=(kxz,))
            kcv = emit_conv(m, 0)
            ku, kub = emit_u(m, par, kcv)
            if DEBUG and c == 0 and m == 0:
                b.dma(dbg["d_uf"], u_sl(par, m), reads=(ku,))
            emit_xp(m, 0, kub)
        emit_xp_finish(c, 0)
        for m in range(S):
            pb = 2 + 2 * (m % 2)
            for half in range(2):
                for k in range(8):
                    def mmz(e, k=k, m=m, half=half, pb=pb):
                        return e.matmul(ps(pb + half), winz_t(k, m),
                                        xlnT_k(k)[:, half * 512:(half + 1) * 512],
                                        start=(k == 0), stop=(k == 7))
                    b.op("tensor", mmz, reads=("winz", "xlnT"),
                         writes=(pk(pb + half),))
            ksz = f"sz{m % 2}"
            def zsil(e, m=m, pb=pb):
                return e.activation(out=fr(o_sz + (m % 2) * 1024, NT),
                                    in_=ps(pb, 1024), func=AF.Silu)
            b.op("scalar", zsil, reads=(pk(pb), pk(pb + 1)), writes=(ksz,))
            b.dma(sc_sz[c, m], fr(o_sz + (m % 2) * 1024, NT), reads=(ksz,),
                  writes=("scsz",))
            if DEBUG and c == 0 and m == 0:
                b.dma(dbg["d_sz"], fr(o_sz + (m % 2) * 1024, NT), reads=(ksz,))

    def emit_A_rev(c):
        par = c % 2
        for m in range(S):
            kxz = f"xz{m}"
            if c == NSC - 1:
                b.op("vector", lambda e, m=m: e.memset(xzp_sl(m)[:, NT:NT + 3], 0.0),
                     writes=(kxz,))
            else:
                def hd(e, m=m):
                    return e.tensor_copy(out=xzp_sl(m)[:, NT:NT + 3],
                                         in_=xzp_sl(m)[:, 0:3])
                b.op("vector", hd, reads=(), writes=(kxz,))
            b.dma(xzp_sl(m)[:, 0:NT], sc_xz[c, m], reads=("scxz", kxz), writes=(kxz,))
            kcv = emit_conv(m, 1)
            ku, kub = emit_u(m, par, kcv)
            if DEBUG and c == NSC - 1 and m == 0:
                b.dma(dbg["d_ur"], u_sl(par, m), reads=(ku,))
            emit_xp(m, 1, kub)
        emit_xp_finish(c, 1)

    stF = [0]
    stR = [0]

    def emit_B(c, dir_):
        par = c % 2
        rev = dir_ == 1
        slot = dir_ * 8 + c
        b.dma(fr(o_xdb, NT)[0:96, :], xd_out[slot], reads=(f"xdo{slot}",),
              writes=("xdb",))
        if DEBUG and ((not rev and c == 0) or (rev and c == NSC - 1)):
            b.dma(dbg["d_xdblr" if rev else "d_xdbl"], fr(o_xdb, NT)[0:96, :],
                  reads=("xdb",))
        def bct(e):
            return e.activation(out=fb(o_bctb, NT)[0:32, :],
                                in_=fr(o_xdb, NT)[64:96, :], func=AF.Copy)
        b.op("scalar", bct, reads=("xdb",), writes=("bctb",))
        def dtrc(e):
            return e.activation(out=fb(o_dtrb, NT)[0:64, :],
                                in_=fr(o_xdb, NT)[0:64, :], func=AF.Copy)
        b.op("scalar", dtrc, reads=("xdb",), writes=("dtrb",))
        if DEBUG and not rev and c == 0:
            def dtrd(e):
                return e.activation(out=fr(o_sp, NT)[0:64, :],
                                    in_=fb(o_dtrb, NT)[0:64, :], func=AF.Copy)
            b.op("scalar", dtrd, reads=("dtrb",), writes=("sp0",))
            b.dma(dbg["d_dtrb"][0:64, :], fr(o_sp, NT)[0:64, :], reads=("sp0",))
        if rev:
            for m in range(S):
                b.dma(fr(o_yf + m * NT, NT), sc_yf[c, m],
                      reads=("scyf", "revfence"), writes=(f"yf{m}",))
        for m in range(S):
            for half in range(2):
                def dt_(e, m=m, half=half, dir_=dir_):
                    return e.matmul(ps(2 + half), dtw_t(dir_, m),
                                    fb(o_dtrb, NT)[0:64, half * 512:(half + 1) * 512],
                                    start=True, stop=True)
                b.op("tensor", dt_, reads=("dtw", "dtrb"), writes=(pk(2 + half),))
            def vb(e, m=m, dir_=dir_):
                return e.tensor_scalar(out=fr(o_sp, NT), in0=ps(2, 1024),
                                       scalar1=dtbcol(dir_, m), scalar2=None,
                                       op0=ALU.add)
            b.op("vector", vb, reads=(pk(2), pk(3), "dtbk"), writes=("sp0",))
            if DEBUG and not rev and c == 0 and m == 0:
                b.dma(dbg["d_dtm"], fr(o_sp, NT), reads=("sp0",))
            def ab(e):
                return e.scalar_tensor_tensor(out=fr(o_sp + 1024, NT),
                                              in0=fr(o_sp, NT), scalar=-1.0,
                                              in1=fr(o_sp, NT),
                                              op0=ALU.mult, op1=ALU.max)
            b.op("vector", ab, reads=("sp0",), writes=("sp1",))
            def e1(e):
                return e.activation(out=fr(o_E, NT), in_=fr(o_sp + 1024, NT),
                                    func=AF.Exp, scale=-1.0)
            b.op("scalar", e1, reads=("sp1",), writes=("E0", "E1"))
            def l1(e):
                return e.activation(out=fr(o_sp + 1024, NT), in_=fr(o_E, NT),
                                    func=AF.Ln, bias=1.0)
            b.op("scalar", l1, reads=("E0", "E1"), writes=("sp1",))
            def rl(e):
                return e.tensor_scalar(out=fr(o_E, NT), in0=fr(o_sp, NT),
                                       scalar1=0.0, scalar2=None, op0=ALU.max)
            b.op("vector", rl, reads=("sp0",), writes=("E0", "E1"))
            def dl(e, m=m):
                return e.tensor_tensor(out=fr(o_delta + m * NT, NT),
                                       in0=fr(o_sp + 1024, NT),
                                       in1=fr(o_E, NT), op=ALU.add)
            b.op("vector", dl, reads=("sp1", "E0", "E1"), writes=(f"dl{m}",))
            if DEBUG and ((not rev and c == 0) or (rev and c == NSC - 1)) and m == 0:
                b.dma(dbg["d_delta"], fr(o_delta + m * NT, NT), reads=(f"dl{m}",))
            def du_(e, m=m, par=par):
                return e.tensor_tensor(out=fr(o_du + m * NT, NT),
                                       in0=fr(o_delta + m * NT, NT),
                                       in1=u_sl(par, m), op=ALU.mult)
            b.op("gpsimd", du_, reads=(f"dl{m}", f"u{par}_{m}"), writes=(f"du{m}",))
        st_bases, st_idx = (ST_R, stR) if rev else (ST_F, stF)
        fkeys = ("stR0", "stR1") if rev else ("stF0", "stF1")
        for half_i in range(2):
            half = (1 - half_i) if rev else half_i
            first = half_i == 0
            hs = slice(half * 512, (half + 1) * 512)
            for n in range(NS):
                rp = n % 2
                def brep(e, n=n, hs=hs):
                    return e.matmul(ps(6), selb_t(n),
                                    fb(o_bctb, NT)[0:32, hs], start=True, stop=True)
                b.op("tensor", brep, reads=("selb", "bctb"), writes=(pk(6),))
                def crep(e, n=n, hs=hs):
                    return e.matmul(ps(7), selb_t(16 + n),
                                    fb(o_bctb, NT)[0:32, hs], start=True, stop=True)
                b.op("tensor", crep, reads=("selb", "bctb"), writes=(pk(7),))
                def bev(e, rp=rp):
                    return e.activation(out=fr(o_Ball + rp * 512, 512), in_=ps(6),
                                        func=AF.Copy)
                b.op("scalar", bev, reads=(pk(6),), writes=(f"Ba{rp}",))
                def cev(e, rp=rp):
                    return e.activation(out=fr(o_Call + rp * 512, 512), in_=ps(7),
                                        func=AF.Copy)
                b.op("scalar", cev, reads=(pk(7),), writes=(f"Ca{rp}",))
                if DEBUG and not rev and c == 0 and n == 0 and half == 0:
                    b.dma(dbg["d_Ball"], fr(o_Ball + rp * 512, 512), reads=(f"Ba{rp}",))
                for m in range(S):
                    lane = m * 16 + n
                    ep = (n * S + m) % 2
                    def E_(e, m=m, n=n, hs=hs, ep=ep, dir_=dir_):
                        return e.activation(out=fr(o_E + ep * 512, 512),
                                            in_=fr(o_delta + m * NT, NT)[:, hs],
                                            func=AF.Exp, scale=acol(dir_, m, n))
                    b.op("scalar", E_, reads=(f"dl{m}", "ac"), writes=(f"E{ep}",))
                    def bh(e, m=m, hs=hs, ep=ep, rp=rp):
                        return e.tensor_tensor(out=fr(o_bh + ep * 512, 512),
                                               in0=fr(o_du + m * NT, NT)[:, hs],
                                               in1=fr(o_Ball + rp * 512, 512),
                                               op=ALU.mult)
                    b.op("gpsimd", bh, reads=(f"du{m}", f"Ba{rp}"), writes=(f"bh{ep}",))
                    if first:
                        init_ap = stcol(st_bases[st_idx[0]], lane)
                        ikey = fkeys[st_idx[0]]
                    else:
                        init_ap = stcol(ST_M, lane)
                        ikey = "stM"
                    def scan(e, ep=ep, init_ap=init_ap, rev=rev):
                        oh = fr(o_h + ep * 512, 512)
                        d0 = fr(o_E + ep * 512, 512)
                        d1 = fr(o_bh + ep * 512, 512)
                        if rev:
                            oh, d0, d1 = oh[:, ::-1], d0[:, ::-1], d1[:, ::-1]
                        return e.tensor_tensor_scan(out=oh, data0=d0, data1=d1,
                                                    initial=init_ap,
                                                    op0=ALU.mult, op1=ALU.add)
                    b.op("vector", scan, reads=(f"E{ep}", f"bh{ep}", ikey),
                         writes=(f"h{ep}",))
                    ecol = 0 if rev else 511
                    if first:
                        dst, dkey = stcol(ST_M, lane), "stM"
                    else:
                        dst, dkey = stcol(st_bases[1 - st_idx[0]], lane), fkeys[1 - st_idx[0]]
                    def stx(e, ep=ep, dst=dst, ecol=ecol):
                        return e.tensor_copy(out=dst,
                                             in_=fr(o_h + ep * 512, 512)[:, ecol:ecol + 1])
                    b.op("vector", stx, reads=(f"h{ep}",), writes=(dkey,))
                    def q_(e, ep=ep, rp=rp):
                        return e.tensor_tensor(out=fb(o_q + ep * 256, 512),
                                               in0=fr(o_h + ep * 512, 512),
                                               in1=fr(o_Call + rp * 512, 512),
                                               op=ALU.mult)
                    b.op("vector", q_, reads=(f"h{ep}", f"Ca{rp}"), writes=(f"q{ep}",))
                    if DEBUG and ((not rev and c == 0) or (rev and c == NSC - 1)) \
                            and m == 0 and n == 0 and half == (1 if rev else 0):
                        b.dma(dbg["d_E"], fr(o_E + ep * 512, 512), reads=(f"E{ep}",))
                        b.dma(dbg["d_bh"], fr(o_bh + ep * 512, 512), reads=(f"bh{ep}",))
                        b.dma(dbg["d_h"], fr(o_h + ep * 512, 512), reads=(f"h{ep}",))
                    def nsum(e, m=m, n=n, ep=ep):
                        return e.matmul(ps(2 + m), id_b(),
                                        fb(o_q + ep * 256, 512),
                                        start=(n == 0), stop=(n == NS - 1))
                    b.op("tensor", nsum, reads=(f"q{ep}", "idb"), writes=(pk(2 + m),))
            for m in range(S):
                gp_p = m % 2
                def gpre(e, m=m, hs=hs, par=par, dir_=dir_, gp_p=gp_p):
                    return e.scalar_tensor_tensor(
                        out=fr(o_gp + gp_p * 512, 512),
                        in0=u_sl(par, m)[:, hs],
                        scalar=dvcol(dir_, m),
                        in1=ps(2 + m), op0=ALU.mult, op1=ALU.add)
                b.op("vector", gpre, reads=(f"u{par}_{m}", "dv", pk(2 + m)),
                     writes=(f"gp{gp_p}",))
                if not rev:
                    b.dma(sc_yf[c, m][:, hs], fr(o_gp + gp_p * 512, 512),
                          reads=(f"gp{gp_p}",), writes=("scyf",))
                    if DEBUG and c == 0 and m == 0:
                        b.dma(dbg["d_yf"][:, hs], fr(o_gp + gp_p * 512, 512),
                              reads=(f"gp{gp_p}",))
                else:
                    ksz = f"szr{gp_p}"
                    b.dma(fr(o_szr + gp_p * 512, 512), sc_sz[c, m][:, hs],
                          reads=("scsz", "revfence", ksz), writes=(ksz,))
                    if DEBUG and c == NSC - 1 and m == 0:
                        b.dma(dbg["d_yr"][:, hs], fr(o_gp + gp_p * 512, 512),
                              reads=(f"gp{gp_p}",))
                    def add1(e, m=m, hs=hs, gp_p=gp_p):
                        return e.tensor_tensor(out=fr(o_gp + gp_p * 512, 512),
                                               in0=fr(o_gp + gp_p * 512, 512),
                                               in1=fr(o_yf + m * NT, NT)[:, hs],
                                               op=ALU.add)
                    b.op("gpsimd", add1, reads=(f"yf{m}", f"gp{gp_p}"),
                         writes=(f"gp{gp_p}",))
                    def gmul(e, m=m, hs=hs, gp_p=gp_p):
                        return e.tensor_tensor(
                            out=fb(o_ymix + m * 512, NT)[:, hs],
                            in0=fr(o_gp + gp_p * 512, 512),
                            in1=fr(o_szr + gp_p * 512, 512), op=ALU.mult)
                    b.op("gpsimd", gmul, reads=(f"gp{gp_p}", ksz, "revfence"),
                         writes=(f"ym{m}",))
        st_idx[0] = 1 - st_idx[0]
        if rev:
            for dmt in range(8):
                pb = 2 + 2 * (dmt % 2)
                for half in range(2):
                    for s_ in range(S):
                        def op_(e, dmt=dmt, half=half, s_=s_, pb=pb):
                            return e.matmul(ps(pb + half), wout_t(s_, dmt),
                                            fb(o_ymix + s_ * 512, NT)[:, half * 512:(half + 1) * 512],
                                            start=(s_ == 0), stop=(s_ == S - 1))
                        b.op("tensor", op_, reads=("wout", f"ym{s_}"),
                             writes=(pk(pb + half),))
                kop = f"op{dmt % 2}"
                def opev(e, dmt=dmt, pb=pb):
                    return e.activation(out=fr(o_ops + (dmt % 2) * 1024, NT),
                                        in_=ps(pb, 1024), func=AF.Copy)
                b.op("scalar", opev, reads=(pk(pb), pk(pb + 1), "revfence"),
                     writes=(kop,))
                b.dma(op_in[c % 2, c // 2, dmt], fr(o_ops + (dmt % 2) * 1024, NT),
                      reads=(kop,), writes=("opw",))
                if DEBUG and c == NSC - 1 and dmt == 0:
                    b.dma(dbg["d_op"], fr(o_ops + (dmt % 2) * 1024, NT), reads=(kop,))

    emit_A_fwd(0)
    emit_dequant_tile(0)
    emit_dequant_tile(1)
    next_dq = 2
    for c in range(1, NSC):
        emit_A_fwd(c)
        for t in range(next_dq, min(next_dq + 5, 40)):
            emit_dequant_tile(t)
        next_dq = min(next_dq + 5, 40)
        emit_B(c - 1, 0)
    for t in range(next_dq, 40):
        emit_dequant_tile(t)
    emit_B(NSC - 1, 0)
    b.op("vector", lambda e: e.memset(fr(o_mu + 10, 2), 0.0),
         writes=("x0", "x1", "xl", "xlnT", "sz0", "sz1", "szr0", "szr1",
                 "revfence"))
    emit_A_rev(NSC - 1)
    for c in range(NSC - 2, -1, -1):
        emit_A_rev(c)
        emit_B(c + 1, 1)
    emit_B(0, 1)
    for j in range(2):
        def rs(e, j=j):
            return e.collective_compute(
                "ReduceScatter", ALU.add, replica_groups=GROUPS,
                ins=[op_in[j]], outs=[op_out[j]])
        b.op("gpsimd", rs, reads=("opw",), writes=(f"opo{j}",), sig_as="coll")

    fence_keys = tuple(f"u0_{m}" for m in range(S)) + tuple(f"u1_{m}" for m in range(S)) + \
        tuple(f"dl{m}" for m in range(S)) + tuple(f"du{m}" for m in range(S)) + \
        tuple(f"yf{m}" for m in range(S)) + tuple(f"ym{m}" for m in range(S)) + \
        tuple(f"xz{m}" for m in range(S)) + \
        ("xlnT", "xl", "x0", "x1", "sp0", "sp1", "xdb", "bctb",
         "fence", "Ba0", "Ba1", "Ca0", "Ca1", "E0", "E1", "bh0", "bh1",
         "h0", "h1", "q0", "q1", "gp0", "gp1", "sz0", "sz1", "op0", "op1",
         "szr0", "szr1", "xdsb", "dq0", "dq1", "dq2", "ub0", "ub1", "dtrb",
         "stF0", "stF1", "stR0", "stR1", "stM")
    b.op("vector", lambda e: e.memset(fr(o_mu + 12, 2), 0.0), writes=fence_keys)
    b.dma(fr(of_gb, DM), ln_g.partition_broadcast(128), reads=("fence",),
          writes=("sz0",))
    b.dma(fr(of_gb + DM, DM), ln_b.partition_broadcast(128), reads=("fence",),
          writes=("sz1",))
    b.dma(fr(of_fb, 32), fc1_b, reads=("fence",), writes=("bctb",))
    b.dma(fr(of_fb + 32, 8), fc2_b, reads=("fence",), writes=("bctb",))
    def hb2(e):
        return e.tensor_scalar(out=fr(of_fb + 40, 8), in0=fr(of_fb + 32, 8),
                               scalar1=0.5, scalar2=None, op0=ALU.mult)
    b.op("vector", hb2, reads=("bctb",), writes=("bctb",))

    for fch in range(4):
        j = fch // 2
        off = (fch % 2) * 512
        for dmt in range(8):
            b.dma(fr(of_opsb + dmt * 512, 512), op_out[j, dmt][:, off:off + 512],
                  reads=(f"opo{j}", "fence", "opsb"), writes=("opsb",))
        for tt in range(4):
            kx = f"x{tt % 2}"
            b.dma(fr(o_x + (tt % 2) * DM, DM),
                  xq_in[fch * 512 + tt * 128:fch * 512 + (tt + 1) * 128, :],
                  reads=("fence",), writes=(kx,))
            for dmt in range(8):
                bnk = dmt % 2
                def tr(e, dmt=dmt, tt=tt, bnk=bnk):
                    return e.transpose(ps(bnk, 128),
                                       fr(of_opsb + dmt * 512, 512)[:, tt * 128:(tt + 1) * 128],
                                       fr(o_idf, 128))
                b.op("tensor", tr, reads=("opsb", "idf"), writes=(pk(bnk),))
                def ev(e, dmt=dmt, bnk=bnk):
                    return e.activation(out=fr(o_xln, DM)[:, dmt * 128:(dmt + 1) * 128],
                                        in_=ps(bnk, 128), func=AF.Copy)
                b.op("scalar", ev, reads=(pk(bnk),), writes=("xl",))
            def addx(e, tt=tt):
                return e.tensor_tensor(out=fr(of_x1 + tt * DM, DM),
                                       in0=fr(o_xln, DM),
                                       in1=fr(o_x + (tt % 2) * DM, DM),
                                       op=ALU.add)
            b.op("vector", addx, reads=("xl", kx), writes=("fx1",))
            if DEBUG and fch == 0 and tt == 0:
                b.dma(dbg["d_x1"], fr(of_x1 + tt * DM, DM), reads=("fx1",))
        for tt in range(4):
            ln_stats(fr(of_x1 + tt * DM, DM), "fx1")
            def n1(e, tt=tt):
                return e.tensor_scalar(out=fr(of_xl2 + tt * DM, DM),
                                       in0=fr(of_x1 + tt * DM, DM),
                                       scalar1=AR[:, o_mu + 6:o_mu + 7],
                                       scalar2=AR[:, o_mu + 7:o_mu + 8],
                                       op0=ALU.mult, op1=ALU.add)
            b.op("vector", n1, reads=("fx1", "mu"), writes=("fxl",))
            ln_stats(fr(of_xl2 + tt * DM, DM), "fxl")
            def n2(e, tt=tt):
                return e.tensor_scalar(out=fr(o_sp, DM), in0=fr(of_xl2 + tt * DM, DM),
                                       scalar1=AR[:, o_mu + 6:o_mu + 7],
                                       scalar2=AR[:, o_mu + 7:o_mu + 8],
                                       op0=ALU.mult, op1=ALU.add)
            b.op("vector", n2, reads=("fxl", "mu"), writes=("sp0",))
            def gmul(e):
                return e.tensor_tensor(out=fr(o_sp + 1024, DM), in0=fr(o_sp, DM),
                                       in1=fr(of_gb, DM), op=ALU.mult)
            b.op("vector", gmul, reads=("sp0", "sz0"), writes=("sp1",))
            def badd(e):
                return e.tensor_tensor(out=fr(o_sp, DM), in0=fr(o_sp + 1024, DM),
                                       in1=fr(of_gb + DM, DM), op=ALU.add)
            b.op("vector", badd, reads=("sp1", "sz1"), writes=("sp0",))
            for k in range(8):
                bnk = k % 2
                def tr(e, k=k, bnk=bnk):
                    return e.transpose(ps(bnk, 128),
                                       fr(o_sp, DM)[:, k * 128:(k + 1) * 128],
                                       fr(o_idf, 128))
                b.op("tensor", tr, reads=("sp0", "idf"), writes=(pk(bnk),))
                def ev(e, tt=tt, k=k, bnk=bnk):
                    return e.activation(
                        out=fb(of_h2T, 8 * 512)[:, k * 512 + tt * 128:k * 512 + (tt + 1) * 128],
                        in_=ps(bnk, 128), func=AF.Copy)
                b.op("scalar", ev, reads=(pk(bnk),), writes=("h2T",))
        h2T_k = lambda k: fb(of_h2T, 8 * 512)[:, k * 512:(k + 1) * 512]
        for Fp in range(16):
            kw = f"w1b{Fp % 2}"
            wbuf = of_w1b + (Fp % 2) * 1024
            wkeys = (kw,) if Fp % 2 == 0 else (kw, "w2b0")
            b.dma(fb(wbuf, 8 * 256).rearrange("p (k m) -> p k m", k=8),
                  sc_w1[:, :, Fp * 256:(Fp + 1) * 256],
                  reads=("wsc", "fence") + wkeys, writes=wkeys)
            for sub in range(2):
                Ft = 2 * Fp + sub
                pb = 2 + (Ft % 2)
                for k in range(8):
                    def mm_(e, k=k, pb=pb, wbuf=wbuf, sub=sub):
                        wt = fb(wbuf, 8 * 256)[:, k * 256 + sub * 128:k * 256 + (sub + 1) * 128]
                        return e.matmul(ps(pb), wt, h2T_k(k), start=(k == 0), stop=(k == 7))
                    b.op("tensor", mm_, reads=wkeys + ("h2T",), writes=(pk(pb),))
                def sl(e, Ft=Ft, pb=pb):
                    return e.activation(out=fb(of_y1, 32 * 512)[:, Ft * 512:(Ft + 1) * 512],
                                        in_=ps(pb), func=AF.Silu,
                                        bias=AR[:, of_fb + Ft:of_fb + Ft + 1])
                b.op("scalar", sl, reads=("bctb", "fence", pk(pb)), writes=("y1",))
        for dm in range(8):
            kw = f"w2b{dm % 2}"
            b.dma(fb(of_w2b + (dm % 2) * 2048, 32 * 128).rearrange("p (k m) -> p k m", k=32),
                  sc_w2[:, :, dm * 128:(dm + 1) * 128],
                  reads=("wsc", "fence", kw), writes=(kw,))
            pb = 4 + (dm % 2)
            for k in range(32):
                def mm_(e, dm=dm, k=k, pb=pb):
                    wt = fb(of_w2b + (dm % 2) * 2048, 32 * 128)[:, k * 128:(k + 1) * 128]
                    y1k = fb(of_y1, 32 * 512)[:, k * 512:(k + 1) * 512]
                    return e.matmul(ps(pb), wt, y1k, start=(k == 0), stop=(k == 31))
                b.op("tensor", mm_, reads=(kw, "y1"), writes=(pk(pb),))
            def hv(e, dm=dm, pb=pb):
                return e.tensor_scalar(out=fr(z2sl(dm), 512), in0=ps(pb),
                                       scalar1=0.5,
                                       scalar2=AR[:, of_fb + 40 + dm:of_fb + 41 + dm],
                                       op0=ALU.mult, op1=ALU.add)
            b.op("vector", hv, reads=("bctb", "fence", pk(pb)), writes=("z2",))
        for tt in range(4):
            for dm in range(8):
                bnk = dm % 2
                def tr(e, dm=dm, tt=tt, bnk=bnk):
                    return e.transpose(ps(bnk, 128),
                                       fr(z2sl(dm), 512)[:, tt * 128:(tt + 1) * 128],
                                       fr(o_idf, 128))
                b.op("tensor", tr, reads=("z2", "idf"), writes=(pk(bnk),))
                def zev(e, dm=dm, bnk=bnk):
                    return e.activation(out=fr(o_sp + dm * 128, 128),
                                        in_=ps(bnk, 128), func=AF.Copy)
                b.op("scalar", zev, reads=(pk(bnk),), writes=("sp0",))
            def f1(e, tt=tt):
                return e.tensor_tensor(out=fr(o_sp + 1024, DM),
                                       in0=fr(of_x1 + tt * DM, DM),
                                       in1=fr(of_xl2 + tt * DM, DM), op=ALU.add)
            b.op("vector", f1, reads=("fx1", "fxl"), writes=("sp1",))
            def f2(e, tt=tt):
                return e.tensor_tensor(out=fr(o_xln, DM), in0=fr(o_sp + 1024, DM),
                                       in1=fr(o_sp, DM), op=ALU.add)
            b.op("vector", f2, reads=("sp0", "sp1"), writes=("xl",))
            trow = fch * 512 + tt * 128
            b.dma(out[trow:trow + 128, :], fr(o_xln, DM), reads=("xl",),
                  writes=("outw",))

    b.emit("sync", lambda e: e.nop(),
           waits=[("sync", b.cnt["sync"]), ("gpsimd", b.cnt["gpsimd"]),
                  ("coll", b.cnt.get("coll", 0))], sig=False)

    with nc.Block() as block:
        b.replay(block, sems)
    es.close()
    return nc


def kernel(**inputs):
    import ml_dtypes
    inputs = {k: np.asarray(v) for k, v in inputs.items()}
    nc = build_program()

    selb = np.zeros((32, 32 * 128), np.float32)
    for n in range(32):
        selb[n, n * 128:(n + 1) * 128] = 1.0
    selb = selb.astype(ml_dtypes.bfloat16)

    win = inputs["L0_attn_in_proj_w"]
    wop = inputs["L0_attn_out_proj_w"]
    xpf = inputs["L0_attn_x_proj_w"]
    xpr = inputs["L0_attn_x_proj_w_b"]
    dtf = inputs["L0_attn_dt_proj_w"]
    dtr = inputs["L0_attn_dt_proj_w_b"]

    def chsl(a, q, axis=0):
        sl = [slice(None)] * a.ndim
        sl[axis] = slice(512 * q, 512 * (q + 1))
        return np.ascontiguousarray(a[tuple(sl)])

    def pmaj(a, q):
        a = chsl(a, q)
        if a.ndim == 1:
            return np.ascontiguousarray(a.reshape(4, 128).T)
        k = a.shape[1]
        return np.ascontiguousarray(
            a.reshape(4, 128, k).transpose(1, 0, 2).reshape(128, 4 * k))

    shared = {
        "selb": selb,
        "id128f": np.eye(128, dtype=np.float32),
        "fc1_q": np.ascontiguousarray(inputs["L0_ffn_fc1_q"]),
        "fc1_sc": np.ascontiguousarray(inputs["L0_ffn_fc1_scales"]),
        "fc1_b": np.ascontiguousarray(inputs["L0_ffn_fc1_bias"].reshape(32, 128).T),
        "fc2_q": np.ascontiguousarray(inputs["L0_ffn_fc2_q"]),
        "fc2_sc": np.ascontiguousarray(inputs["L0_ffn_fc2_scales"]),
        "fc2_b": np.ascontiguousarray(inputs["L0_ffn_fc2_bias"].reshape(8, 128).T),
        "ln_g": np.ascontiguousarray(inputs["L0_ffn_ln_gamma"]),
        "ln_b": np.ascontiguousarray(inputs["L0_ffn_ln_beta"]),
    }
    in_maps = []
    for cc in range(8):
        bq, q = divmod(cc, 4)
        x = np.ascontiguousarray(inputs["L0_attn_hidden_states"][bq])
        m = dict(shared)
        m.update({
            "x_in": x,
            "xq_in": np.ascontiguousarray(x[OWN * q:OWN * (q + 1)]),
            "winx": np.ascontiguousarray(chsl(win[:DI], q).T),
            "winz": np.ascontiguousarray(chsl(win[DI:], q).T),
            "wout": np.ascontiguousarray(chsl(wop.T, q)),
            "xpw": np.stack([chsl(np.ascontiguousarray(xpf.T), q),
                             chsl(np.ascontiguousarray(xpr.T), q)], 0),
            "dtw": np.stack([np.ascontiguousarray(chsl(dtf, q).T),
                             np.ascontiguousarray(chsl(dtr, q).T)], 0),
            "convw": np.concatenate([pmaj(inputs["L0_attn_conv_w"][:, 0, :], q),
                                     pmaj(inputs["L0_attn_conv_w_b"][:, 0, :], q)], 1),
            "convb": np.concatenate([pmaj(inputs["L0_attn_conv_b"], q),
                                     pmaj(inputs["L0_attn_conv_b_b"], q)], 1),
            "dtb": np.concatenate([pmaj(inputs["L0_attn_dt_proj_b"], q),
                                   pmaj(inputs["L0_attn_dt_proj_b_b"], q)], 1),
            "alog": np.concatenate([pmaj(inputs["L0_attn_A_log"], q),
                                    pmaj(inputs["L0_attn_A_b_log"], q)], 1),
            "dvec": np.concatenate([pmaj(inputs["L0_attn_D"], q),
                                    pmaj(inputs["L0_attn_D_b"], q)], 1),
        })
        in_maps.append(m)

    res = run_bass_kernel_spmd(nc, in_maps, core_ids=list(range(8)))
    outp = np.empty((2, L, DM), np.float32)
    for cc in range(8):
        bq, q = divmod(cc, 4)
        outp[bq, OWN * q:OWN * (q + 1)] = np.asarray(res.results[cc]["out"])
    if DEBUG:
        kernel.last_results = res
    return outp


# version 17 (final)
# speedup vs baseline: 1.4130x; 1.0083x over previous
import os
import sys
import numpy as np

sys.path.insert(0, "/opt/trn_rl_repo")
import concourse.bass as bass
import concourse.mybir as mybir
from concourse.bass_utils import run_bass_kernel_spmd

F32 = mybir.dt.float32
BF16 = mybir.dt.bfloat16
I32 = mybir.dt.int32
AF = mybir.ActivationFunctionType
ALU = mybir.AluOpType

DM = 1024
DI = 2048
CHQ = 512
S = 4
NS = 16
RK = 64
L = 8192
NT = 1024
NSC = 8
OWN = 2048
EPS = 1e-5
DEBUG = bool(int(os.environ.get("BASSMK_DEBUG", "0")))

ENGINES = ("sync", "tensor", "scalar", "vector", "gpsimd")
GROUPS = [[0, 1, 2, 3], [4, 5, 6, 7]]


class Buf:
    def __init__(self):
        self.w = None
        self.readers = {}

    def wr(self):
        return [(e, v) for e, v in self.readers.items()] + \
            ([self.w] if self.w is not None else [])

    def setw(self, tok):
        self.w = tok
        self.readers = {}

    def rd(self):
        return [self.w] if self.w is not None else []

    def setr(self, tok):
        if tok is not None:
            e, v = tok
            if self.readers.get(e, 0) < v:
                self.readers[e] = v


class Bld:
    def __init__(self):
        self.ops = {e: [] for e in ENGINES}
        self.cnt = {e: 0 for e in ENGINES}
        self.bufs = {}

    def B(self, key):
        if key not in self.bufs:
            self.bufs[key] = Buf()
        return self.bufs[key]

    def emit(self, eng, fn, waits=(), sig=True, sig_as=None):
        ws = []
        for t in waits:
            if t is None:
                continue
            pe, v = t
            if pe == eng and sig_as is None:
                if pe == "tensor":
                    continue
                if pe != "sync" and self.cnt[eng] + 1 - v > 4:
                    continue
            ws.append((pe, v))
        if sig:
            sname = sig_as or eng
            self.cnt.setdefault(sname, 0)
            self.cnt[sname] += 16 if sname == "sync" else 1
            tok = (sname, self.cnt[sname])
        else:
            sname = None
            tok = None
        self.ops[eng].append((fn, ws, sname))
        return tok

    def op(self, eng, fn, reads=(), writes=(), sig_as=None):
        waits = []
        for k in reads:
            waits += self.B(k).rd()
        for k in writes:
            waits += self.B(k).wr()
        tok = self.emit(eng, fn, waits=waits, sig_as=sig_as)
        for k in reads:
            self.B(k).setr(tok)
        for k in writes:
            self.B(k).setw(tok)
        return tok

    def dma(self, out, in_, reads=(), writes=()):
        def fn(e):
            return e.dma_start(out=out, in_=in_)
        return self.op("sync", fn, reads=reads, writes=writes)

    def replay(self, block, sems):
        for eng_name in ENGINES:
            ops = self.ops[eng_name]
            if not ops:
                continue

            def make_body(ename, olist):
                def body(eng):
                    seen = {}
                    cum = 0
                    for fn, ws, sname in olist:
                        for pe, v in ws:
                            if seen.get(pe, 0) >= v:
                                continue
                            seen[pe] = v
                            eng.wait_ge(sems[pe], v)
                        inst = fn(eng)
                        if sname:
                            inst.then_inc(sems[sname], 16 if sname == "sync" else 1)
                        if sname == "sync":
                            cum += 16
                            eng.wait_ge(sems["sync"], cum)
                            seen["sync"] = cum
                return body
            getattr(block, eng_name)(make_body(eng_name, ops))


def build_program():
    nc = bass.Bass("TRN2", target_bir_lowering=False)

    def din(name, shape, dt=F32):
        return nc.dram_tensor(name, shape, dt, kind="ExternalInput").ap()

    x_in = din("x_in", [L, DM])
    xq_in = din("xq_in", [OWN, DM])
    winx = din("winx", [DM, CHQ])
    winz = din("winz", [DM, CHQ])
    wout = din("wout", [CHQ, DM])
    xpw = din("xpw", [2, CHQ, 96])
    dtw = din("dtw", [2, RK, CHQ])
    convw = din("convw", [128, 2 * S * 4])
    convb = din("convb", [128, 2 * S])
    dtb = din("dtb", [128, 2 * S])
    alog = din("alog", [128, 2 * S * NS])
    dvec = din("dvec", [128, 2 * S])
    selb = din("selb", [32, 32 * 128], BF16)
    id128f = din("id128f", [128, 128])
    fc1_q = din("fc1_q", [DM, 4 * DM], I32)
    fc1_sc = din("fc1_sc", [8, 4 * DM])
    fc1_b = din("fc1_b", [128, 32])
    fc2_q = din("fc2_q", [4 * DM, DM], I32)
    fc2_sc = din("fc2_sc", [32, DM])
    fc2_b = din("fc2_b", [128, 8])
    ln_g = din("ln_g", [DM])
    ln_b = din("ln_b", [DM])

    out = nc.dram_tensor("out", [OWN, DM], F32, kind="ExternalOutput").ap()

    def dscr(name, shape, dt=F32):
        return nc.dram_tensor(name, shape, dt, kind="Internal").ap()

    sc_xz = dscr("sc_xz", [NSC, S, 128, NT])
    sc_sz = dscr("sc_sz", [NSC, S, 128, NT])
    sc_yf = dscr("sc_yf", [NSC, S, 128, NT])
    xd_in = dscr("xd_in", [16, 96, NT])
    xd_out = dscr("xd_out", [16, 96, NT])
    op_in = dscr("op_in", [2, 4, 8, 128, NT])
    op_out = dscr("op_out", [2, 8, 128, NT])
    sc_w1 = dscr("sc_w1", [128, 8, 4 * DM], BF16)
    sc_w2 = dscr("sc_w2", [128, 32, DM], BF16)

    dbg = {}
    if DEBUG:
        for nm, shape in [("d_xln", [128, DM]), ("d_xzf", [128, NT]),
                          ("d_uf", [128, NT]), ("d_xdbl", [96, NT]),
                          ("d_delta", [128, NT]), ("d_E", [128, 512]),
                          ("d_bh", [128, 512]), ("d_h", [128, 512]),
                          ("d_yf", [128, NT]), ("d_ur", [128, NT]),
                          ("d_xdblr", [96, NT]),
                          ("d_yr", [128, NT]), ("d_op", [128, NT]),
                          ("d_x1", [128, DM]), ("d_sz", [128, NT]),
                          ("d_dtm", [128, NT]), ("d_Ball", [128, 512]),
                          ("d_dtrb", [128, NT])]:
            dbg[nm] = nc.dram_tensor(nm, shape, F32, kind="ExternalOutput").ap()

    ACOLS = 53180
    _off = [0]

    def A(n):
        o = _off[0]
        _off[0] += n
        assert _off[0] <= ACOLS, f"arena overflow at {_off[0]}"
        return o

    o_winx = A(2048)
    o_winz = A(2048)
    o_wout = A(2048)
    o_xpw = A(384)
    o_dtw = A(512)
    o_selb = A(2048)
    o_idf = A(128)
    o_idb = A(64)
    o_cw = A(32)
    o_cb = A(8)
    o_dtb = A(8)
    o_dv = A(8)
    o_ac = A(128)
    o_mu = A(16)
    o_st = A(320)
    o_x = A(2048)
    o_xln = A(1024)
    o_xlnT = A(4096)
    o_xzp = A(4112)
    o_ub = A(1024)
    o_sz = A(2048)
    o_u = A(8192)
    o_delta = A(4096)
    o_du = A(4096)
    o_xdb = A(1024)
    o_bctb = A(512)
    o_Ball = A(1024)
    o_Call = A(1024)
    o_E = A(1024)
    o_bh = A(1024)
    o_h = A(1024)
    o_q = A(512)
    o_sp = A(2048)
    o_gp = A(1024)
    o_dq = A(1536)
    o_dtrb = A(512)
    assert _off[0] <= ACOLS, _off[0]

    o_yf = o_xlnT
    o_szr = o_x
    o_ymix = o_x + 1024
    o_ops = o_sz
    of_x1 = o_u
    of_xl2 = o_u + 4096
    of_h2T = o_delta
    of_w1b = o_delta + 2048
    of_w2b = o_delta + 3072
    of_y1 = o_xlnT
    of_opsb = o_Ball
    of_gb = o_sz
    of_fb = o_bctb

    def z2sl(dm):
        offs = [o_gp, o_gp + 512, o_dq, o_dq + 512, o_dq + 1024,
                o_ub, o_ub + 512, o_xdb]
        return offs[dm]

    b = Bld()

    import contextlib
    es = contextlib.ExitStack()
    AR = es.enter_context(nc.sbuf_tensor("AR", [128, ACOLS], F32))
    PSF = es.enter_context(nc.psum_tensor("PSF", [128, 4096], F32))
    sems = {e: es.enter_context(nc.semaphore("sem_" + e))
            for e in ENGINES + ("coll",)}

    def fr(off, n):
        return AR[:, off:off + n]

    def fb(off, n_bf16):
        return AR[:, off:off + (n_bf16 + 1) // 2].bitcast(BF16)[:, 0:n_bf16]

    def ps(bank, n=512):
        return PSF[:, bank * 512:bank * 512 + n]

    pk = lambda bank: f"ps{bank}"

    b.op("vector", lambda e: e.memset(AR[:, o_mu + 8:o_mu + 9], EPS), writes=("eps",))
    b.op("vector", lambda e: e.memset(fr(o_st, 320), 0.0),
         writes=("stF0", "stF1", "stR0", "stR1", "stM"))

    b.dma(fb(o_selb, 4096)[0:32, :], selb, writes=("selb",))
    b.dma(fr(o_idf, 128), id128f, writes=("idf",))
    b.dma(fr(o_cw, 32), convw, writes=("cw",))
    b.dma(fr(o_cb, 8), convb, writes=("cb",))
    b.dma(fr(o_dtb, 8), dtb, writes=("dtbk",))
    b.dma(fr(o_dv, 8), dvec, writes=("dv",))
    b.dma(fr(o_sp, 128), alog, writes=("sp0",))
    def mkA(e):
        return e.activation(out=fr(o_ac, 128), in_=fr(o_sp, 128), func=AF.Exp)
    b.op("scalar", mkA, reads=("sp0",), writes=("ac",))
    def negA(e):
        return e.tensor_scalar(out=fr(o_ac, 128), in0=fr(o_ac, 128),
                               scalar1=-1.0, scalar2=None, op0=ALU.mult)
    b.op("vector", negA, reads=(), writes=("ac",))
    def mkidb(e):
        return e.activation(out=fb(o_idb, 128), in_=fr(o_idf, 128), func=AF.Copy)
    b.op("scalar", mkidb, reads=("idf",), writes=("idb",))

    def stage_resident(srcs, ncols, dst_off, key):
        half = ncols // 2
        for hh in range(2):
            b.dma(fr(o_u, half), srcs[hh], writes=("u0_0", "u0_1", "u0_2", "u0_3"))
            def cast(e, hh=hh, half=half, dst_off=dst_off, ncols=ncols):
                return e.activation(out=fb(dst_off, ncols)[:, hh * half:(hh + 1) * half],
                                    in_=fr(o_u, half), func=AF.Copy)
            b.op("scalar", cast, reads=("u0_0",), writes=(key,))

    winx_v = winx.rearrange("(k p) m -> p k m", p=128)
    winz_v = winz.rearrange("(k p) m -> p k m", p=128)
    wout_v = wout.rearrange("(s p) m -> p s m", p=128)
    stage_resident([winx_v[:, 0:4, :], winx_v[:, 4:8, :]], 4096, o_winx, "winx")
    stage_resident([winz_v[:, 0:4, :], winz_v[:, 4:8, :]], 4096, o_winz, "winz")
    stage_resident([wout_v[:, 0:2, :], wout_v[:, 2:4, :]], 4096, o_wout, "wout")
    b.dma(fr(o_u, 768), xpw.rearrange("d (s p) r -> p d s r", p=128),
          writes=("u0_0", "u0_1", "u0_2", "u0_3"))
    def cxp(e):
        return e.activation(out=fb(o_xpw, 768), in_=fr(o_u, 768), func=AF.Copy)
    b.op("scalar", cxp, reads=("u0_0",), writes=("xpw",))
    b.dma(fr(o_u, 1024)[0:64, :], dtw.rearrange("d r m -> r d m"),
          writes=("u0_0", "u0_1", "u0_2", "u0_3"))
    def cdt(e):
        return e.activation(out=fb(o_dtw, 1024)[0:64, :], in_=fr(o_u, 1024)[0:64, :],
                            func=AF.Copy)
    b.op("scalar", cdt, reads=("u0_0",), writes=("dtw",))

    winx_t = lambda k, m: fb(o_winx, 8 * CHQ)[:, k * CHQ + m * 128:k * CHQ + (m + 1) * 128]
    winz_t = lambda k, m: fb(o_winz, 8 * CHQ)[:, k * CHQ + m * 128:k * CHQ + (m + 1) * 128]
    wout_t = lambda s, d: fb(o_wout, S * DM)[:, s * DM + d * 128:s * DM + (d + 1) * 128]
    xpw_t = lambda dir_, m: fb(o_xpw, 768)[:, dir_ * 384 + m * 96:dir_ * 384 + (m + 1) * 96]
    dtw_t = lambda dir_, m: fb(o_dtw, 1024)[0:64, dir_ * 512 + m * 128:dir_ * 512 + (m + 1) * 128]
    selb_t = lambda j: fb(o_selb, 4096)[0:32, j * 128:(j + 1) * 128]
    id_b = lambda: fb(o_idb, 128)
    wcol = lambda dir_, m, j: AR[:, o_cw + dir_ * 16 + m * 4 + j:o_cw + dir_ * 16 + m * 4 + j + 1]
    cbcol = lambda dir_, m: AR[:, o_cb + dir_ * 4 + m:o_cb + dir_ * 4 + m + 1]
    dtbcol = lambda dir_, m: AR[:, o_dtb + dir_ * 4 + m:o_dtb + dir_ * 4 + m + 1]
    dvcol = lambda dir_, m: AR[:, o_dv + dir_ * 4 + m:o_dv + dir_ * 4 + m + 1]
    acol = lambda dir_, m, n: AR[:, o_ac + dir_ * 64 + m * 16 + n:o_ac + dir_ * 64 + m * 16 + n + 1]
    stcol = lambda base, lane: AR[:, o_st + base + lane:o_st + base + lane + 1]
    ST_F = (0, 64)
    ST_R = (128, 192)
    ST_M = 256
    xzp_sl = lambda m: AR[:, o_xzp + m * 1028:o_xzp + m * 1028 + 1028]
    u_sl = lambda par, m: fr(o_u + par * 4096 + m * NT, NT)
    xlnT_k = lambda k: fb(o_xlnT, 8 * NT)[:, k * NT:(k + 1) * NT]

    def ln_stats(src_ap, src_key):
        def s1(e):
            return e.activation(out=fr(o_sp, DM), in_=src_ap, func=AF.Copy,
                                accum_out=AR[:, o_mu:o_mu + 1])
        b.op("scalar", s1, reads=(src_key,), writes=("sp0", "mu"))
        def s2(e):
            return e.activation(out=fr(o_sp, DM), in_=src_ap, func=AF.Square,
                                accum_out=AR[:, o_mu + 1:o_mu + 2])
        b.op("scalar", s2, reads=(src_key,), writes=("sp0", "mu"))
        def mstat(e):
            return e.tensor_scalar(out=AR[:, o_mu + 2:o_mu + 4],
                                   in0=AR[:, o_mu:o_mu + 2], scalar1=1.0 / DM,
                                   scalar2=None, op0=ALU.mult)
        b.op("vector", mstat, reads=(), writes=("mu",))
        def musq(e):
            return e.activation(out=AR[:, o_mu + 9:o_mu + 10],
                                in_=AR[:, o_mu + 2:o_mu + 3], func=AF.Square)
        b.op("scalar", musq, reads=("mu",), writes=("mu",))
        def nvar(e):
            return e.tensor_tensor(out=AR[:, o_mu + 4:o_mu + 5],
                                   in0=AR[:, o_mu + 9:o_mu + 10],
                                   in1=AR[:, o_mu + 3:o_mu + 4], op=ALU.subtract)
        b.op("vector", nvar, reads=(), writes=("mu",))
        def lnv(e):
            return e.activation(out=AR[:, o_mu + 5:o_mu + 6],
                                in_=AR[:, o_mu + 4:o_mu + 5], func=AF.Ln,
                                scale=-1.0, bias=AR[:, o_mu + 8:o_mu + 9])
        b.op("scalar", lnv, reads=("mu", "eps"), writes=("mu",))
        def rstd(e):
            return e.activation(out=AR[:, o_mu + 6:o_mu + 7],
                                in_=AR[:, o_mu + 5:o_mu + 6], func=AF.Exp,
                                scale=-0.5)
        b.op("scalar", rstd, reads=(), writes=("mu",))
        def nb(e):
            return e.tensor_scalar(out=AR[:, o_mu + 7:o_mu + 8],
                                   in0=AR[:, o_mu + 2:o_mu + 3],
                                   scalar1=AR[:, o_mu + 6:o_mu + 7],
                                   scalar2=-1.0, op0=ALU.mult, op1=ALU.mult)
        b.op("vector", nb, reads=(), writes=("mu",))

    def emit_conv(m, dir_):
        taps = [0, 1, 2, 3] if dir_ == 0 else [3, 2, 1, 0]
        kxz = f"xz{m}"
        def c0(e, m=m, dir_=dir_, taps=taps):
            return e.tensor_scalar(out=fr(o_sp, NT), in0=xzp_sl(m)[:, 0:NT],
                                   scalar1=wcol(dir_, m, taps[0]),
                                   scalar2=cbcol(dir_, m),
                                   op0=ALU.mult, op1=ALU.add)
        b.op("vector", c0, reads=(kxz, "cw", "cb"), writes=("sp0",))
        srcdst = [("sp0", "sp1"), ("sp1", "sp0"), ("sp0", "sp1")]
        offd = [o_sp + 1024, o_sp, o_sp + 1024]
        offs_ = [o_sp, o_sp + 1024, o_sp]
        for j in range(1, 4):
            def cj(e, m=m, j=j, dir_=dir_, taps=taps):
                return e.scalar_tensor_tensor(out=fr(offd[j - 1], NT),
                                              in0=xzp_sl(m)[:, j:NT + j],
                                              scalar=wcol(dir_, m, taps[j]),
                                              in1=fr(offs_[j - 1], NT),
                                              op0=ALU.mult, op1=ALU.add)
            b.op("vector", cj, reads=(kxz, "cw", srcdst[j - 1][0]),
                 writes=(srcdst[j - 1][1],))
        return "sp1"

    def emit_u(m, par, kcv):
        ku = f"u{par}_{m}"
        def silu(e, m=m, par=par):
            return e.activation(out=u_sl(par, m), in_=fr(o_sp + 1024, NT), func=AF.Silu)
        b.op("scalar", silu, reads=(kcv,), writes=(ku,))
        kub = f"ub{m % 2}"
        def silub(e, m=m):
            return e.activation(out=fb(o_ub + (m % 2) * 512, NT),
                                in_=fr(o_sp + 1024, NT), func=AF.Silu)
        b.op("scalar", silub, reads=(kcv,), writes=(kub,))
        return ku, kub

    def emit_xp(m, dir_, kub):
        for half in range(2):
            def xp_(e, m=m, half=half, dir_=dir_):
                return e.matmul(ps(half)[0:96, :], xpw_t(dir_, m),
                                fb(o_ub + (m % 2) * 512, NT)[:, half * 512:(half + 1) * 512],
                                start=(m == 0), stop=(m == S - 1))
            b.op("tensor", xp_, reads=("xpw", kub), writes=(pk(half),))

    def emit_xp_finish(c, dir_):
        slot = dir_ * 8 + c
        def xpev(e):
            return e.activation(out=fr(o_ub, 1024)[0:96, :], in_=ps(0, 1024)[0:96, :],
                                func=AF.Copy)
        b.op("scalar", xpev, reads=(pk(0), pk(1)), writes=("ub0", "ub1", "xdsb"))
        b.dma(xd_in[slot], fr(o_ub, 1024)[0:96, :],
              reads=("xdsb", "ub0", "ub1"), writes=("xdw",))
        def coll(e, slot=slot):
            return e.collective_compute(
                "AllReduce", ALU.add, replica_groups=GROUPS,
                ins=[xd_in[slot]], outs=[xd_out[slot]])
        b.op("gpsimd", coll, reads=("xdw",), writes=(f"xdo{slot}",), sig_as="coll")

    def emit_dequant_tile(tile):
        if tile < 8:
            k, W, qd, scd, dst = tile, 4 * DM, fc1_q, fc1_sc, sc_w1
        else:
            k, W, qd, scd, dst = tile - 8, DM, fc2_q, fc2_sc, sc_w2
        for cc in range(W // 512):
            b.dma(AR[:, o_dq:o_dq + 512].bitcast(I32),
                  qd[k * 128:(k + 1) * 128, cc * 512:(cc + 1) * 512],
                  writes=("dq0",))
            b.dma(fr(o_dq + 512, 512),
                  scd[k, cc * 512:(cc + 1) * 512].partition_broadcast(128),
                  writes=("dq1",))
            def cvt(e):
                return e.activation(out=fr(o_dq + 1024, 512),
                                    in_=AR[:, o_dq:o_dq + 512].bitcast(I32),
                                    func=AF.Copy, bias=-128.0)
            b.op("scalar", cvt, reads=("dq0",), writes=("dq2",))
            def mul(e):
                return e.tensor_tensor(out=fb(o_dq, 512), in0=fr(o_dq + 1024, 512),
                                       in1=fr(o_dq + 512, 512), op=ALU.mult)
            b.op("gpsimd", mul, reads=("dq1", "dq2"), writes=("dq0",))
            b.dma(dst[:, k, cc * 512:(cc + 1) * 512], fb(o_dq, 512),
                  reads=("dq0",), writes=("wsc",))

    def emit_A_fwd(c):
        par = c % 2
        for tt in range(8):
            kx = f"x{tt % 2}"
            b.dma(fr(o_x + (tt % 2) * DM, DM),
                  x_in[c * NT + tt * 128:c * NT + (tt + 1) * 128, :], writes=(kx,))
            xa = fr(o_x + (tt % 2) * DM, DM)
            ln_stats(xa, kx)
            def norm(e, xa=xa):
                return e.tensor_scalar(out=fr(o_xln, DM), in0=xa,
                                       scalar1=AR[:, o_mu + 6:o_mu + 7],
                                       scalar2=AR[:, o_mu + 7:o_mu + 8],
                                       op0=ALU.mult, op1=ALU.add)
            b.op("vector", norm, reads=("mu", kx), writes=("xl",))
            if DEBUG and c == 0 and tt == 0:
                b.dma(dbg["d_xln"], fr(o_xln, DM), reads=("xl",))
            for k in range(8):
                bnk = k % 2
                def tr(e, k=k, bnk=bnk):
                    return e.transpose(ps(bnk, 128),
                                       fr(o_xln, DM)[:, k * 128:(k + 1) * 128],
                                       fr(o_idf, 128))
                b.op("tensor", tr, reads=("xl", "idf"), writes=(pk(bnk),))
                def evac(e, tt=tt, k=k, bnk=bnk):
                    return e.activation(
                        out=xlnT_k(k)[:, tt * 128:(tt + 1) * 128],
                        in_=ps(bnk, 128), func=AF.Copy)
                b.op("scalar", evac, reads=(pk(bnk),), writes=("xlnT",))
        for m in range(S):
            pb = 2 + 2 * (m % 2)
            for half in range(2):
                for k in range(8):
                    def mm_(e, k=k, m=m, half=half, pb=pb):
                        return e.matmul(ps(pb + half), winx_t(k, m),
                                        xlnT_k(k)[:, half * 512:(half + 1) * 512],
                                        start=(k == 0), stop=(k == 7))
                    b.op("tensor", mm_, reads=("winx", "xlnT"),
                         writes=(pk(pb + half),))
            kxz = f"xz{m}"
            if c == 0:
                b.op("vector", lambda e, m=m: e.memset(xzp_sl(m)[:, 0:3], 0.0),
                     writes=(kxz,))
            else:
                def tl(e, m=m):
                    return e.tensor_copy(out=xzp_sl(m)[:, 0:3],
                                         in_=xzp_sl(m)[:, NT:NT + 3])
                b.op("vector", tl, reads=(), writes=(kxz,))
            def xev(e, m=m, pb=pb):
                return e.activation(out=xzp_sl(m)[:, 3:NT + 3], in_=ps(pb, 1024),
                                    func=AF.Copy)
            b.op("scalar", xev, reads=(pk(pb), pk(pb + 1)), writes=(kxz,))
            b.dma(sc_xz[c, m], xzp_sl(m)[:, 3:NT + 3], reads=(kxz,), writes=("scxz",))
            if DEBUG and c == 0 and m == 0:
                b.dma(dbg["d_xzf"], xzp_sl(m)[:, 3:NT + 3], reads=(kxz,))
            kcv = emit_conv(m, 0)
            ku, kub = emit_u(m, par, kcv)
            if DEBUG and c == 0 and m == 0:
                b.dma(dbg["d_uf"], u_sl(par, m), reads=(ku,))
            emit_xp(m, 0, kub)
        emit_xp_finish(c, 0)
        for m in range(S):
            pb = 2 + 2 * (m % 2)
            for half in range(2):
                for k in range(8):
                    def mmz(e, k=k, m=m, half=half, pb=pb):
                        return e.matmul(ps(pb + half), winz_t(k, m),
                                        xlnT_k(k)[:, half * 512:(half + 1) * 512],
                                        start=(k == 0), stop=(k == 7))
                    b.op("tensor", mmz, reads=("winz", "xlnT"),
                         writes=(pk(pb + half),))
            ksz = f"sz{m % 2}"
            def zsil(e, m=m, pb=pb):
                return e.activation(out=fr(o_sz + (m % 2) * 1024, NT),
                                    in_=ps(pb, 1024), func=AF.Silu)
            b.op("scalar", zsil, reads=(pk(pb), pk(pb + 1)), writes=(ksz,))
            b.dma(sc_sz[c, m], fr(o_sz + (m % 2) * 1024, NT), reads=(ksz,),
                  writes=("scsz",))
            if DEBUG and c == 0 and m == 0:
                b.dma(dbg["d_sz"], fr(o_sz + (m % 2) * 1024, NT), reads=(ksz,))

    def emit_A_rev(c):
        par = c % 2
        for m in range(S):
            kxz = f"xz{m}"
            if c == NSC - 1:
                b.op("vector", lambda e, m=m: e.memset(xzp_sl(m)[:, NT:NT + 3], 0.0),
                     writes=(kxz,))
            else:
                def hd(e, m=m):
                    return e.tensor_copy(out=xzp_sl(m)[:, NT:NT + 3],
                                         in_=xzp_sl(m)[:, 0:3])
                b.op("vector", hd, reads=(), writes=(kxz,))
            b.dma(xzp_sl(m)[:, 0:NT], sc_xz[c, m], reads=("scxz", kxz), writes=(kxz,))
            kcv = emit_conv(m, 1)
            ku, kub = emit_u(m, par, kcv)
            if DEBUG and c == NSC - 1 and m == 0:
                b.dma(dbg["d_ur"], u_sl(par, m), reads=(ku,))
            emit_xp(m, 1, kub)
        emit_xp_finish(c, 1)

    stF = [0]
    stR = [0]

    def emit_B(c, dir_):
        par = c % 2
        rev = dir_ == 1
        slot = dir_ * 8 + c
        b.dma(fr(o_xdb, NT)[0:96, :], xd_out[slot], reads=(f"xdo{slot}",),
              writes=("xdb",))
        if DEBUG and ((not rev and c == 0) or (rev and c == NSC - 1)):
            b.dma(dbg["d_xdblr" if rev else "d_xdbl"], fr(o_xdb, NT)[0:96, :],
                  reads=("xdb",))
        def bct(e):
            return e.activation(out=fb(o_bctb, NT)[0:32, :],
                                in_=fr(o_xdb, NT)[64:96, :], func=AF.Copy)
        b.op("scalar", bct, reads=("xdb",), writes=("bctb",))
        def dtrc(e):
            return e.activation(out=fb(o_dtrb, NT)[0:64, :],
                                in_=fr(o_xdb, NT)[0:64, :], func=AF.Copy)
        b.op("scalar", dtrc, reads=("xdb",), writes=("dtrb",))
        if DEBUG and not rev and c == 0:
            def dtrd(e):
                return e.activation(out=fr(o_sp, NT)[0:64, :],
                                    in_=fb(o_dtrb, NT)[0:64, :], func=AF.Copy)
            b.op("scalar", dtrd, reads=("dtrb",), writes=("sp0",))
            b.dma(dbg["d_dtrb"][0:64, :], fr(o_sp, NT)[0:64, :], reads=("sp0",))
        if rev:
            for m in range(S):
                b.dma(fr(o_yf + m * NT, NT), sc_yf[c, m],
                      reads=("scyf", "revfence"), writes=(f"yf{m}",))
        for m in range(S):
            for half in range(2):
                def dt_(e, m=m, half=half, dir_=dir_):
                    return e.matmul(ps(2 + half), dtw_t(dir_, m),
                                    fb(o_dtrb, NT)[0:64, half * 512:(half + 1) * 512],
                                    start=True, stop=True)
                b.op("tensor", dt_, reads=("dtw", "dtrb"), writes=(pk(2 + half),))
            def vb(e, m=m, dir_=dir_):
                return e.tensor_scalar(out=fr(o_sp, NT), in0=ps(2, 1024),
                                       scalar1=dtbcol(dir_, m), scalar2=None,
                                       op0=ALU.add)
            b.op("vector", vb, reads=(pk(2), pk(3), "dtbk"), writes=("sp0",))
            if DEBUG and not rev and c == 0 and m == 0:
                b.dma(dbg["d_dtm"], fr(o_sp, NT), reads=("sp0",))
            def ab(e):
                return e.scalar_tensor_tensor(out=fr(o_sp + 1024, NT),
                                              in0=fr(o_sp, NT), scalar=-1.0,
                                              in1=fr(o_sp, NT),
                                              op0=ALU.mult, op1=ALU.max)
            b.op("vector", ab, reads=("sp0",), writes=("sp1",))
            def e1(e):
                return e.activation(out=fr(o_E, NT), in_=fr(o_sp + 1024, NT),
                                    func=AF.Exp, scale=-1.0)
            b.op("scalar", e1, reads=("sp1",), writes=("E0", "E1"))
            def l1(e):
                return e.activation(out=fr(o_sp + 1024, NT), in_=fr(o_E, NT),
                                    func=AF.Ln, bias=1.0)
            b.op("scalar", l1, reads=("E0", "E1"), writes=("sp1",))
            def rl(e):
                return e.tensor_scalar(out=fr(o_E, NT), in0=fr(o_sp, NT),
                                       scalar1=0.0, scalar2=None, op0=ALU.max)
            b.op("vector", rl, reads=("sp0",), writes=("E0", "E1"))
            def dl(e, m=m):
                return e.tensor_tensor(out=fr(o_delta + m * NT, NT),
                                       in0=fr(o_sp + 1024, NT),
                                       in1=fr(o_E, NT), op=ALU.add)
            b.op("vector", dl, reads=("sp1", "E0", "E1"), writes=(f"dl{m}",))
            if DEBUG and ((not rev and c == 0) or (rev and c == NSC - 1)) and m == 0:
                b.dma(dbg["d_delta"], fr(o_delta + m * NT, NT), reads=(f"dl{m}",))
            def du_(e, m=m, par=par):
                return e.tensor_tensor(out=fr(o_du + m * NT, NT),
                                       in0=fr(o_delta + m * NT, NT),
                                       in1=u_sl(par, m), op=ALU.mult)
            b.op("gpsimd", du_, reads=(f"dl{m}", f"u{par}_{m}"), writes=(f"du{m}",))
        st_bases, st_idx = (ST_R, stR) if rev else (ST_F, stF)
        fkeys = ("stR0", "stR1") if rev else ("stF0", "stF1")
        for half_i in range(2):
            half = (1 - half_i) if rev else half_i
            first = half_i == 0
            hs = slice(half * 512, (half + 1) * 512)
            for n in range(NS):
                rp = n % 2
                def brep(e, n=n, hs=hs):
                    return e.matmul(ps(6), selb_t(n),
                                    fb(o_bctb, NT)[0:32, hs], start=True, stop=True)
                b.op("tensor", brep, reads=("selb", "bctb"), writes=(pk(6),))
                def crep(e, n=n, hs=hs):
                    return e.matmul(ps(7), selb_t(16 + n),
                                    fb(o_bctb, NT)[0:32, hs], start=True, stop=True)
                b.op("tensor", crep, reads=("selb", "bctb"), writes=(pk(7),))
                def bev(e, rp=rp):
                    return e.activation(out=fr(o_Ball + rp * 512, 512), in_=ps(6),
                                        func=AF.Copy)
                b.op("scalar", bev, reads=(pk(6),), writes=(f"Ba{rp}",))
                def cev(e, rp=rp):
                    return e.activation(out=fr(o_Call + rp * 512, 512), in_=ps(7),
                                        func=AF.Copy)
                b.op("scalar", cev, reads=(pk(7),), writes=(f"Ca{rp}",))
                if DEBUG and not rev and c == 0 and n == 0 and half == 0:
                    b.dma(dbg["d_Ball"], fr(o_Ball + rp * 512, 512), reads=(f"Ba{rp}",))
                for m in range(S):
                    lane = m * 16 + n
                    ep = (n * S + m) % 2
                    def E_(e, m=m, n=n, hs=hs, ep=ep, dir_=dir_):
                        return e.activation(out=fr(o_E + ep * 512, 512),
                                            in_=fr(o_delta + m * NT, NT)[:, hs],
                                            func=AF.Exp, scale=acol(dir_, m, n))
                    b.op("scalar", E_, reads=(f"dl{m}", "ac"), writes=(f"E{ep}",))
                    def bh(e, m=m, hs=hs, ep=ep, rp=rp):
                        return e.tensor_tensor(out=fr(o_bh + ep * 512, 512),
                                               in0=fr(o_du + m * NT, NT)[:, hs],
                                               in1=fr(o_Ball + rp * 512, 512),
                                               op=ALU.mult)
                    b.op("gpsimd", bh, reads=(f"du{m}", f"Ba{rp}"), writes=(f"bh{ep}",))
                    if first:
                        init_ap = stcol(st_bases[st_idx[0]], lane)
                        ikey = fkeys[st_idx[0]]
                    else:
                        init_ap = stcol(ST_M, lane)
                        ikey = "stM"
                    def scan(e, ep=ep, init_ap=init_ap, rev=rev):
                        oh = fr(o_h + ep * 512, 512)
                        d0 = fr(o_E + ep * 512, 512)
                        d1 = fr(o_bh + ep * 512, 512)
                        if rev:
                            oh, d0, d1 = oh[:, ::-1], d0[:, ::-1], d1[:, ::-1]
                        return e.tensor_tensor_scan(out=oh, data0=d0, data1=d1,
                                                    initial=init_ap,
                                                    op0=ALU.mult, op1=ALU.add)
                    b.op("vector", scan, reads=(f"E{ep}", f"bh{ep}", ikey),
                         writes=(f"h{ep}",))
                    ecol = 0 if rev else 511
                    if m % 2 == 1:
                        lane0 = (m - 1) * 16 + n
                        base_ = ST_M if first else st_bases[1 - st_idx[0]]
                        dkey = "stM" if first else fkeys[1 - st_idx[0]]
                        dst = AR[:, o_st + base_ + lane0:o_st + base_ + lane0 + 17:16]
                        def stx(e, dst=dst, ecol=ecol):
                            return e.tensor_copy(
                                out=dst,
                                in_=AR[:, o_h + ecol:o_h + ecol + 513:512])
                        b.op("vector", stx, reads=("h0", "h1"), writes=(dkey,))
                    def q_(e, ep=ep, rp=rp):
                        return e.tensor_tensor(out=fb(o_q + ep * 256, 512),
                                               in0=fr(o_h + ep * 512, 512),
                                               in1=fr(o_Call + rp * 512, 512),
                                               op=ALU.mult)
                    b.op("vector", q_, reads=(f"h{ep}", f"Ca{rp}"), writes=(f"q{ep}",))
                    if DEBUG and ((not rev and c == 0) or (rev and c == NSC - 1)) \
                            and m == 0 and n == 0 and half == (1 if rev else 0):
                        b.dma(dbg["d_E"], fr(o_E + ep * 512, 512), reads=(f"E{ep}",))
                        b.dma(dbg["d_bh"], fr(o_bh + ep * 512, 512), reads=(f"bh{ep}",))
                        b.dma(dbg["d_h"], fr(o_h + ep * 512, 512), reads=(f"h{ep}",))
                    def nsum(e, m=m, n=n, ep=ep):
                        return e.matmul(ps(2 + m), id_b(),
                                        fb(o_q + ep * 256, 512),
                                        start=(n == 0), stop=(n == NS - 1))
                    b.op("tensor", nsum, reads=(f"q{ep}", "idb"), writes=(pk(2 + m),))
            for m in range(S):
                gp_p = m % 2
                def gpre(e, m=m, hs=hs, par=par, dir_=dir_, gp_p=gp_p):
                    return e.scalar_tensor_tensor(
                        out=fr(o_gp + gp_p * 512, 512),
                        in0=u_sl(par, m)[:, hs],
                        scalar=dvcol(dir_, m),
                        in1=ps(2 + m), op0=ALU.mult, op1=ALU.add)
                b.op("vector", gpre, reads=(f"u{par}_{m}", "dv", pk(2 + m)),
                     writes=(f"gp{gp_p}",))
                if not rev:
                    b.dma(sc_yf[c, m][:, hs], fr(o_gp + gp_p * 512, 512),
                          reads=(f"gp{gp_p}",), writes=("scyf",))
                    if DEBUG and c == 0 and m == 0:
                        b.dma(dbg["d_yf"][:, hs], fr(o_gp + gp_p * 512, 512),
                              reads=(f"gp{gp_p}",))
                else:
                    ksz = f"szr{gp_p}"
                    b.dma(fr(o_szr + gp_p * 512, 512), sc_sz[c, m][:, hs],
                          reads=("scsz", "revfence", ksz), writes=(ksz,))
                    if DEBUG and c == NSC - 1 and m == 0:
                        b.dma(dbg["d_yr"][:, hs], fr(o_gp + gp_p * 512, 512),
                              reads=(f"gp{gp_p}",))
                    def add1(e, m=m, hs=hs, gp_p=gp_p):
                        return e.tensor_tensor(out=fr(o_gp + gp_p * 512, 512),
                                               in0=fr(o_gp + gp_p * 512, 512),
                                               in1=fr(o_yf + m * NT, NT)[:, hs],
                                               op=ALU.add)
                    b.op("gpsimd", add1, reads=(f"yf{m}", f"gp{gp_p}"),
                         writes=(f"gp{gp_p}",))
                    def gmul(e, m=m, hs=hs, gp_p=gp_p):
                        return e.tensor_tensor(
                            out=fb(o_ymix + m * 512, NT)[:, hs],
                            in0=fr(o_gp + gp_p * 512, 512),
                            in1=fr(o_szr + gp_p * 512, 512), op=ALU.mult)
                    b.op("gpsimd", gmul, reads=(f"gp{gp_p}", ksz, "revfence"),
                         writes=(f"ym{m}",))
        st_idx[0] = 1 - st_idx[0]
        if rev:
            for dmt in range(8):
                pb = 2 + 2 * (dmt % 2)
                for half in range(2):
                    for s_ in range(S):
                        def op_(e, dmt=dmt, half=half, s_=s_, pb=pb):
                            return e.matmul(ps(pb + half), wout_t(s_, dmt),
                                            fb(o_ymix + s_ * 512, NT)[:, half * 512:(half + 1) * 512],
                                            start=(s_ == 0), stop=(s_ == S - 1))
                        b.op("tensor", op_, reads=("wout", f"ym{s_}"),
                             writes=(pk(pb + half),))
                kop = f"op{dmt % 2}"
                def opev(e, dmt=dmt, pb=pb):
                    return e.activation(out=fr(o_ops + (dmt % 2) * 1024, NT),
                                        in_=ps(pb, 1024), func=AF.Copy)
                b.op("scalar", opev, reads=(pk(pb), pk(pb + 1), "revfence"),
                     writes=(kop,))
                b.dma(op_in[c % 2, c // 2, dmt], fr(o_ops + (dmt % 2) * 1024, NT),
                      reads=(kop,), writes=("opw",))
                if DEBUG and c == NSC - 1 and dmt == 0:
                    b.dma(dbg["d_op"], fr(o_ops + (dmt % 2) * 1024, NT), reads=(kop,))

    emit_A_fwd(0)
    emit_dequant_tile(0)
    emit_dequant_tile(1)
    next_dq = 2
    for c in range(1, NSC):
        emit_A_fwd(c)
        for t in range(next_dq, min(next_dq + 5, 40)):
            emit_dequant_tile(t)
        next_dq = min(next_dq + 5, 40)
        emit_B(c - 1, 0)
    for t in range(next_dq, 40):
        emit_dequant_tile(t)
    emit_B(NSC - 1, 0)
    b.op("vector", lambda e: e.memset(fr(o_mu + 10, 2), 0.0),
         writes=("x0", "x1", "xl", "xlnT", "sz0", "sz1", "szr0", "szr1",
                 "revfence"))
    emit_A_rev(NSC - 1)
    for c in range(NSC - 2, -1, -1):
        emit_A_rev(c)
        emit_B(c + 1, 1)
    emit_B(0, 1)
    for j in range(2):
        def rs(e, j=j):
            return e.collective_compute(
                "ReduceScatter", ALU.add, replica_groups=GROUPS,
                ins=[op_in[j]], outs=[op_out[j]])
        b.op("gpsimd", rs, reads=("opw",), writes=(f"opo{j}",), sig_as="coll")

    fence_keys = tuple(f"u0_{m}" for m in range(S)) + tuple(f"u1_{m}" for m in range(S)) + \
        tuple(f"dl{m}" for m in range(S)) + tuple(f"du{m}" for m in range(S)) + \
        tuple(f"yf{m}" for m in range(S)) + tuple(f"ym{m}" for m in range(S)) + \
        tuple(f"xz{m}" for m in range(S)) + \
        ("xlnT", "xl", "x0", "x1", "sp0", "sp1", "xdb", "bctb",
         "fence", "Ba0", "Ba1", "Ca0", "Ca1", "E0", "E1", "bh0", "bh1",
         "h0", "h1", "q0", "q1", "gp0", "gp1", "sz0", "sz1", "op0", "op1",
         "szr0", "szr1", "xdsb", "dq0", "dq1", "dq2", "ub0", "ub1", "dtrb",
         "stF0", "stF1", "stR0", "stR1", "stM")
    b.op("vector", lambda e: e.memset(fr(o_mu + 12, 2), 0.0), writes=fence_keys)
    b.dma(fr(of_gb, DM), ln_g.partition_broadcast(128), reads=("fence",),
          writes=("sz0",))
    b.dma(fr(of_gb + DM, DM), ln_b.partition_broadcast(128), reads=("fence",),
          writes=("sz1",))
    b.dma(fr(of_fb, 32), fc1_b, reads=("fence",), writes=("bctb",))
    b.dma(fr(of_fb + 32, 8), fc2_b, reads=("fence",), writes=("bctb",))
    def hb2(e):
        return e.tensor_scalar(out=fr(of_fb + 40, 8), in0=fr(of_fb + 32, 8),
                               scalar1=0.5, scalar2=None, op0=ALU.mult)
    b.op("vector", hb2, reads=("bctb",), writes=("bctb",))

    for fch in range(4):
        j = fch // 2
        off = (fch % 2) * 512
        for dmt in range(8):
            b.dma(fr(of_opsb + dmt * 512, 512), op_out[j, dmt][:, off:off + 512],
                  reads=(f"opo{j}", "fence", "opsb"), writes=("opsb",))
        for tt in range(4):
            kx = f"x{tt % 2}"
            b.dma(fr(o_x + (tt % 2) * DM, DM),
                  xq_in[fch * 512 + tt * 128:fch * 512 + (tt + 1) * 128, :],
                  reads=("fence",), writes=(kx,))
            for dmt in range(8):
                bnk = dmt % 2
                def tr(e, dmt=dmt, tt=tt, bnk=bnk):
                    return e.transpose(ps(bnk, 128),
                                       fr(of_opsb + dmt * 512, 512)[:, tt * 128:(tt + 1) * 128],
                                       fr(o_idf, 128))
                b.op("tensor", tr, reads=("opsb", "idf"), writes=(pk(bnk),))
                def ev(e, dmt=dmt, bnk=bnk):
                    return e.activation(out=fr(o_xln, DM)[:, dmt * 128:(dmt + 1) * 128],
                                        in_=ps(bnk, 128), func=AF.Copy)
                b.op("scalar", ev, reads=(pk(bnk),), writes=("xl",))
            def addx(e, tt=tt):
                return e.tensor_tensor(out=fr(of_x1 + tt * DM, DM),
                                       in0=fr(o_xln, DM),
                                       in1=fr(o_x + (tt % 2) * DM, DM),
                                       op=ALU.add)
            b.op("vector", addx, reads=("xl", kx), writes=("fx1",))
            if DEBUG and fch == 0 and tt == 0:
                b.dma(dbg["d_x1"], fr(of_x1 + tt * DM, DM), reads=("fx1",))
        for tt in range(4):
            ln_stats(fr(of_x1 + tt * DM, DM), "fx1")
            def n1(e, tt=tt):
                return e.tensor_scalar(out=fr(of_xl2 + tt * DM, DM),
                                       in0=fr(of_x1 + tt * DM, DM),
                                       scalar1=AR[:, o_mu + 6:o_mu + 7],
                                       scalar2=AR[:, o_mu + 7:o_mu + 8],
                                       op0=ALU.mult, op1=ALU.add)
            b.op("vector", n1, reads=("fx1", "mu"), writes=("fxl",))
            ln_stats(fr(of_xl2 + tt * DM, DM), "fxl")
            def n2(e, tt=tt):
                return e.tensor_scalar(out=fr(o_sp, DM), in0=fr(of_xl2 + tt * DM, DM),
                                       scalar1=AR[:, o_mu + 6:o_mu + 7],
                                       scalar2=AR[:, o_mu + 7:o_mu + 8],
                                       op0=ALU.mult, op1=ALU.add)
            b.op("vector", n2, reads=("fxl", "mu"), writes=("sp0",))
            def gmul(e):
                return e.tensor_tensor(out=fr(o_sp + 1024, DM), in0=fr(o_sp, DM),
                                       in1=fr(of_gb, DM), op=ALU.mult)
            b.op("vector", gmul, reads=("sp0", "sz0"), writes=("sp1",))
            def badd(e):
                return e.tensor_tensor(out=fr(o_sp, DM), in0=fr(o_sp + 1024, DM),
                                       in1=fr(of_gb + DM, DM), op=ALU.add)
            b.op("vector", badd, reads=("sp1", "sz1"), writes=("sp0",))
            for k in range(8):
                bnk = k % 2
                def tr(e, k=k, bnk=bnk):
                    return e.transpose(ps(bnk, 128),
                                       fr(o_sp, DM)[:, k * 128:(k + 1) * 128],
                                       fr(o_idf, 128))
                b.op("tensor", tr, reads=("sp0", "idf"), writes=(pk(bnk),))
                def ev(e, tt=tt, k=k, bnk=bnk):
                    return e.activation(
                        out=fb(of_h2T, 8 * 512)[:, k * 512 + tt * 128:k * 512 + (tt + 1) * 128],
                        in_=ps(bnk, 128), func=AF.Copy)
                b.op("scalar", ev, reads=(pk(bnk),), writes=("h2T",))
        h2T_k = lambda k: fb(of_h2T, 8 * 512)[:, k * 512:(k + 1) * 512]
        for Fp in range(16):
            kw = f"w1b{Fp % 2}"
            wbuf = of_w1b + (Fp % 2) * 1024
            wkeys = (kw,) if Fp % 2 == 0 else (kw, "w2b0")
            b.dma(fb(wbuf, 8 * 256).rearrange("p (k m) -> p k m", k=8),
                  sc_w1[:, :, Fp * 256:(Fp + 1) * 256],
                  reads=("wsc", "fence") + wkeys, writes=wkeys)
            for sub in range(2):
                Ft = 2 * Fp + sub
                pb = 2 + (Ft % 2)
                for k in range(8):
                    def mm_(e, k=k, pb=pb, wbuf=wbuf, sub=sub):
                        wt = fb(wbuf, 8 * 256)[:, k * 256 + sub * 128:k * 256 + (sub + 1) * 128]
                        return e.matmul(ps(pb), wt, h2T_k(k), start=(k == 0), stop=(k == 7))
                    b.op("tensor", mm_, reads=wkeys + ("h2T",), writes=(pk(pb),))
                def sl(e, Ft=Ft, pb=pb):
                    return e.activation(out=fb(of_y1, 32 * 512)[:, Ft * 512:(Ft + 1) * 512],
                                        in_=ps(pb), func=AF.Silu,
                                        bias=AR[:, of_fb + Ft:of_fb + Ft + 1])
                b.op("scalar", sl, reads=("bctb", "fence", pk(pb)), writes=("y1",))
        for dm in range(8):
            kw = f"w2b{dm % 2}"
            b.dma(fb(of_w2b + (dm % 2) * 2048, 32 * 128).rearrange("p (k m) -> p k m", k=32),
                  sc_w2[:, :, dm * 128:(dm + 1) * 128],
                  reads=("wsc", "fence", kw), writes=(kw,))
            pb = 4 + (dm % 2)
            for k in range(32):
                def mm_(e, dm=dm, k=k, pb=pb):
                    wt = fb(of_w2b + (dm % 2) * 2048, 32 * 128)[:, k * 128:(k + 1) * 128]
                    y1k = fb(of_y1, 32 * 512)[:, k * 512:(k + 1) * 512]
                    return e.matmul(ps(pb), wt, y1k, start=(k == 0), stop=(k == 31))
                b.op("tensor", mm_, reads=(kw, "y1"), writes=(pk(pb),))
            def hv(e, dm=dm, pb=pb):
                return e.tensor_scalar(out=fr(z2sl(dm), 512), in0=ps(pb),
                                       scalar1=0.5,
                                       scalar2=AR[:, of_fb + 40 + dm:of_fb + 41 + dm],
                                       op0=ALU.mult, op1=ALU.add)
            b.op("vector", hv, reads=("bctb", "fence", pk(pb)), writes=("z2",))
        for tt in range(4):
            for dm in range(8):
                bnk = dm % 2
                def tr(e, dm=dm, tt=tt, bnk=bnk):
                    return e.transpose(ps(bnk, 128),
                                       fr(z2sl(dm), 512)[:, tt * 128:(tt + 1) * 128],
                                       fr(o_idf, 128))
                b.op("tensor", tr, reads=("z2", "idf"), writes=(pk(bnk),))
                def zev(e, dm=dm, bnk=bnk):
                    return e.activation(out=fr(o_sp + dm * 128, 128),
                                        in_=ps(bnk, 128), func=AF.Copy)
                b.op("scalar", zev, reads=(pk(bnk),), writes=("sp0",))
            def f1(e, tt=tt):
                return e.tensor_tensor(out=fr(o_sp + 1024, DM),
                                       in0=fr(of_x1 + tt * DM, DM),
                                       in1=fr(of_xl2 + tt * DM, DM), op=ALU.add)
            b.op("vector", f1, reads=("fx1", "fxl"), writes=("sp1",))
            def f2(e, tt=tt):
                return e.tensor_tensor(out=fr(o_xln, DM), in0=fr(o_sp + 1024, DM),
                                       in1=fr(o_sp, DM), op=ALU.add)
            b.op("vector", f2, reads=("sp0", "sp1"), writes=("xl",))
            trow = fch * 512 + tt * 128
            b.dma(out[trow:trow + 128, :], fr(o_xln, DM), reads=("xl",),
                  writes=("outw",))

    b.emit("sync", lambda e: e.nop(),
           waits=[("sync", b.cnt["sync"]), ("gpsimd", b.cnt["gpsimd"]),
                  ("coll", b.cnt.get("coll", 0))], sig=False)

    with nc.Block() as block:
        b.replay(block, sems)
    es.close()
    return nc


def kernel(**inputs):
    import ml_dtypes
    inputs = {k: np.asarray(v) for k, v in inputs.items()}
    nc = build_program()

    selb = np.zeros((32, 32 * 128), np.float32)
    for n in range(32):
        selb[n, n * 128:(n + 1) * 128] = 1.0
    selb = selb.astype(ml_dtypes.bfloat16)

    win = inputs["L0_attn_in_proj_w"]
    wop = inputs["L0_attn_out_proj_w"]
    xpf = inputs["L0_attn_x_proj_w"]
    xpr = inputs["L0_attn_x_proj_w_b"]
    dtf = inputs["L0_attn_dt_proj_w"]
    dtr = inputs["L0_attn_dt_proj_w_b"]

    def chsl(a, q, axis=0):
        sl = [slice(None)] * a.ndim
        sl[axis] = slice(512 * q, 512 * (q + 1))
        return np.ascontiguousarray(a[tuple(sl)])

    def pmaj(a, q):
        a = chsl(a, q)
        if a.ndim == 1:
            return np.ascontiguousarray(a.reshape(4, 128).T)
        k = a.shape[1]
        return np.ascontiguousarray(
            a.reshape(4, 128, k).transpose(1, 0, 2).reshape(128, 4 * k))

    shared = {
        "selb": selb,
        "id128f": np.eye(128, dtype=np.float32),
        "fc1_q": np.ascontiguousarray(inputs["L0_ffn_fc1_q"]),
        "fc1_sc": np.ascontiguousarray(inputs["L0_ffn_fc1_scales"]),
        "fc1_b": np.ascontiguousarray(inputs["L0_ffn_fc1_bias"].reshape(32, 128).T),
        "fc2_q": np.ascontiguousarray(inputs["L0_ffn_fc2_q"]),
        "fc2_sc": np.ascontiguousarray(inputs["L0_ffn_fc2_scales"]),
        "fc2_b": np.ascontiguousarray(inputs["L0_ffn_fc2_bias"].reshape(8, 128).T),
        "ln_g": np.ascontiguousarray(inputs["L0_ffn_ln_gamma"]),
        "ln_b": np.ascontiguousarray(inputs["L0_ffn_ln_beta"]),
    }
    in_maps = []
    for cc in range(8):
        bq, q = divmod(cc, 4)
        x = np.ascontiguousarray(inputs["L0_attn_hidden_states"][bq])
        m = dict(shared)
        m.update({
            "x_in": x,
            "xq_in": np.ascontiguousarray(x[OWN * q:OWN * (q + 1)]),
            "winx": np.ascontiguousarray(chsl(win[:DI], q).T),
            "winz": np.ascontiguousarray(chsl(win[DI:], q).T),
            "wout": np.ascontiguousarray(chsl(wop.T, q)),
            "xpw": np.stack([chsl(np.ascontiguousarray(xpf.T), q),
                             chsl(np.ascontiguousarray(xpr.T), q)], 0),
            "dtw": np.stack([np.ascontiguousarray(chsl(dtf, q).T),
                             np.ascontiguousarray(chsl(dtr, q).T)], 0),
            "convw": np.concatenate([pmaj(inputs["L0_attn_conv_w"][:, 0, :], q),
                                     pmaj(inputs["L0_attn_conv_w_b"][:, 0, :], q)], 1),
            "convb": np.concatenate([pmaj(inputs["L0_attn_conv_b"], q),
                                     pmaj(inputs["L0_attn_conv_b_b"], q)], 1),
            "dtb": np.concatenate([pmaj(inputs["L0_attn_dt_proj_b"], q),
                                   pmaj(inputs["L0_attn_dt_proj_b_b"], q)], 1),
            "alog": np.concatenate([pmaj(inputs["L0_attn_A_log"], q),
                                    pmaj(inputs["L0_attn_A_b_log"], q)], 1),
            "dvec": np.concatenate([pmaj(inputs["L0_attn_D"], q),
                                    pmaj(inputs["L0_attn_D_b"], q)], 1),
        })
        in_maps.append(m)

    res = run_bass_kernel_spmd(nc, in_maps, core_ids=list(range(8)))
    outp = np.empty((2, L, DM), np.float32)
    for cc in range(8):
        bq, q = divmod(cc, 4)
        outp[bq, OWN * q:OWN * (q + 1)] = np.asarray(res.results[cc]["out"])
    if DEBUG:
        kernel.last_results = res
    return outp
